# version 05
# baseline (speedup 1.0000x reference)
import math
import contextlib
import numpy as np
import ml_dtypes

import concourse.bass as bass
import concourse.mybir as mybir
from concourse.bass_utils import run_bass_kernel_spmd

F32 = mybir.dt.float32
BF16 = mybir.dt.bfloat16
I8 = mybir.dt.int8
AF = mybir.ActivationFunctionType
ALU = mybir.AluOpType
AX = mybir.AxisListType

NCORES = 8
NEG = -1e30
RS128 = 1.0 / math.sqrt(128.0)


class Cfg:
    def __init__(self, D=2048, S=4096, H=16, INTER=8192, Bsz=2):
        self.D, self.S, self.H, self.INTER, self.B = D, S, H, INTER, Bsz
        self.ND = D // 128
        self.NH = H
        self.NE3 = 3 * D // 128
        self.NO = INTER // 128
        self.TOKC = Bsz * S // NCORES
        self.NQT = self.TOKC // 128
        assert self.NQT == S // 512
        self.TT = min(512, self.TOKC)
        self.NTILES = self.TOKC // self.TT
        assert H == self.ND, "head_dim must be 128"


ENGINES = ("tensor", "vector", "scalar", "gpsimd", "sync")
SEMNAMES = ("t", "v", "s", "g", "cc", "gwx", "gwa", "gwb", "gwd",
            "swc", "swkv0", "swkv1", "swk0", "swk1", "swv",
            "swx0", "swx1", "swy0", "swy1")


class Prog:

    def __init__(self):
        self.cnt = {n: 0 for n in SEMNAMES}
        self.ops = {e: [] for e in ENGINES}

    def emit(self, eng, fn=None, wait=(), inc=()):
        w = [(sn, val) for (sn, val) in wait if val > 0]
        inc = list(inc)
        for sn, k in inc:
            self.cnt[sn] += k
        self.ops[eng].append((fn, w, inc))

    def now(self, sn):
        return self.cnt[sn]

    def dma(self, eng, fn, sem, wait=()):
        self.emit(eng, fn, wait=list(wait) + [(sem, self.now(sem))],
                  inc=[(sem, 16)])
        return self.now(sem)

    def finalize(self, block, semh):
        for ename in ENGINES:
            ops = self.ops[ename]

            def body(eng, ops=ops):
                lastw = {}
                for fn, waits, incs in ops:
                    for sn, val in waits:
                        if lastw.get(sn, 0) < val:
                            eng.wait_ge(semh[sn], val)
                            lastw[sn] = val
                    ins = fn(eng) if fn is not None else None
                    for sn, k in incs:
                        ins.then_inc(semh[sn], k)
            getattr(block, ename)(body)


def build_nc(cfg: Cfg, debug=False):
    c = cfg
    nc = bass.Bass("TRN2", target_bir_lowering=False)

    xT = nc.dram_tensor("xT", [c.D, c.TOKC], F32, kind="ExternalInput").ap()
    wqkvT = nc.dram_tensor("wqkvT", [c.NE3, 128, c.ND, 128], F32, kind="ExternalInput").ap()
    woutT = nc.dram_tensor("woutT", [c.ND, 128, c.ND, 128], F32, kind="ExternalInput").ap()
    wgu = nc.dram_tensor("wgu", [c.NO, 128, 2 * c.ND * 128], I8, kind="ExternalInput").ap()
    wdt = nc.dram_tensor("wdt", [c.ND, 128, c.NO * 128], I8, kind="ExternalInput").ap()
    bqkv = nc.dram_tensor("bqkv", [128, c.NE3], F32, kind="ExternalInput").ap()
    bout = nc.dram_tensor("bout", [128, c.ND], F32, kind="ExternalInput").ap()
    sg = nc.dram_tensor("sg", [128, c.NO], F32, kind="ExternalInput").ap()
    su = nc.dram_tensor("su", [128, c.NO], F32, kind="ExternalInput").ap()
    sd = nc.dram_tensor("sd", [128, c.ND], F32, kind="ExternalInput").ap()
    maskc = nc.dram_tensor("maskc", [128, 512], BF16, kind="ExternalInput").ap()
    identc = nc.dram_tensor("identc", [128, 128], BF16, kind="ExternalInput").ap()
    identfc = nc.dram_tensor("identfc", [128, 128], F32, kind="ExternalInput").ap()
    yT = nc.dram_tensor("yT", [c.D, c.TOKC], F32, kind="ExternalOutput").ap()

    kv_own = nc.dram_tensor("kv_own", [2 * c.D, c.TOKC], BF16)
    kv_gather = nc.dram_tensor("kv_gather", [4 * 2 * c.D, c.TOKC], BF16)

    dbg = {}
    if debug:
        for nm, shp in (("dbg_qt", [128, c.NH * c.TOKC]),
                        ("dbg_kvg", [4 * 2 * c.D, c.TOKC]),
                        ("dbg_attn", [128, c.NQT * c.D]),
                        ("dbg_attnT", [128, c.ND * c.TOKC]),
                        ("dbg_preln", [128, c.ND * c.TT]),
                        ("dbg_x1b", [128, c.ND * c.TOKC])):
            dbg[nm] = nc.dram_tensor(nm, shp, F32, kind="ExternalOutput").ap()

    P = Prog()

    with contextlib.ExitStack() as ctx:
        E = ctx.enter_context

        QTx1b = E(nc.sbuf_tensor("QTx1b", [128, c.NH * c.TOKC], BF16))
        bigA = E(nc.sbuf_tensor("bigA", [128, c.NH * c.TOKC], BF16))
        attnTf = E(nc.sbuf_tensor("attnTf", [128, c.ND * c.TOKC], BF16))

        VW = (c.S // 128) * 132
        KVC = max(3 * c.S + 2 * VW, 2 * c.ND * c.TT + 2 * c.ND * 128 + 8 * c.TT)
        kvar = E(nc.sbuf_tensor("kvar", [128, KVC], BF16))
        kh = [kvar[:, i * c.S:(i + 1) * c.S] for i in range(2)]
        vtT = kvar[:, 2 * c.S:3 * c.S]
        vtok = [kvar[:, 3 * c.S + i * VW: 3 * c.S + (i + 1) * VW] for i in range(2)]
        preln = kvar[:, 0:2 * c.ND * c.TT].bitcast(F32)
        wo0 = 2 * c.ND * c.TT
        woutb = [kvar[:, wo0 + i * c.ND * 128: wo0 + (i + 1) * c.ND * 128]
                 for i in range(2)]
        xr0 = wo0 + 2 * c.ND * 128
        xres = [kvar[:, xr0 + i * 2 * c.TT: xr0 + (i + 1) * 2 * c.TT].bitcast(F32)
                for i in range(2)]
        gact = [kvar[:, wo0 + i * 2 * c.TT: wo0 + (i + 1) * 2 * c.TT].bitcast(F32)
                for i in range(2)]
        ys0 = wo0 + 4 * c.TT
        ystage = [kvar[:, ys0 + i * 2 * c.TT: ys0 + (i + 1) * 2 * c.TT].bitcast(F32)
                  for i in range(2)]
        assert ys0 + 4 * c.TT <= KVC

        SPC = max(5 * c.S, 2 * c.NO * 128 + 4 * c.ND * 128,
                  2 * c.ND * 128 + 2 * c.TOKC + 4 * 512 + 3 * 1024)
        spar = E(nc.sbuf_tensor("spar", [128, SPC], BF16))
        Sbf = [spar[:, 0:c.S], spar[:, 4 * c.S:5 * c.S]]
        PTb = [spar[:, (2 + i) * c.S:(3 + i) * c.S] for i in range(2)]
        wqkvb = [spar[:, i * c.ND * 128:(i + 1) * c.ND * 128] for i in range(2)]
        kvs0 = 2 * c.ND * 128
        kvstage = [spar[:, kvs0 + i * c.TOKC: kvs0 + (i + 1) * c.TOKC] for i in range(2)]
        assert kvs0 + 2 * c.TOKC <= 2 * c.S
        wdb = [spar[:, i * c.NO * 128:(i + 1) * c.NO * 128] for i in range(2)]
        wg0 = 2 * c.NO * 128
        wgub = [spar[:, wg0 + i * 2 * c.ND * 128: wg0 + (i + 1) * 2 * c.ND * 128]
                for i in range(2)]
        assert wg0 + 4 * c.ND * 128 <= SPC

        bqkv_s = E(nc.sbuf_tensor("bqkv_s", [128, c.NE3], F32))
        sbq_s = E(nc.sbuf_tensor("sbq_s", [128, c.ND], F32))
        bout_s = E(nc.sbuf_tensor("bout_s", [128, c.ND], F32))
        sg_s = E(nc.sbuf_tensor("sg_s", [128, c.NO], F32))
        su_s = E(nc.sbuf_tensor("su_s", [128, c.NO], F32))
        sd_s = E(nc.sbuf_tensor("sd_s", [128, c.ND], F32))
        maskb = E(nc.sbuf_tensor("maskb", [128, 512], BF16))
        identb = E(nc.sbuf_tensor("identb", [128, 128], BF16))
        identf = E(nc.sbuf_tensor("identf", [128, 128], F32))
        onesb = E(nc.sbuf_tensor("onesb", [128, 128], BF16))
        onesf = E(nc.sbuf_tensor("onesf", [1, 128], F32))
        negmax = E(nc.sbuf_tensor("negmax", [128, 2], F32))
        mcols = E(nc.sbuf_tensor("mcols", [128, 16], F32))
        kmax = E(nc.sbuf_tensor("kmax", [128, 2], F32))
        tmn = E(nc.sbuf_tensor("tmn", [1, 2], F32))
        sumv = E(nc.sbuf_tensor("sumv", [128, 2], F32))
        rr = E(nc.sbuf_tensor("rr", [128, 2], F32))
        ln0 = kvs0 + 2 * c.TOKC
        zb = [spar[:, ln0 + i * 512: ln0 + (i + 1) * 512] for i in range(2)]
        sqb = [spar[:, ln0 + (2 + i) * 512: ln0 + (3 + i) * 512] for i in range(2)]
        mrt0 = ln0 + 4 * 512
        mb_s = spar[:, mrt0: mrt0 + 1024].bitcast(F32)
        rb_s = spar[:, mrt0 + 1024: mrt0 + 2048].bitcast(F32)
        tmp_n = spar[:, mrt0 + 2048: mrt0 + 3072].bitcast(F32)
        assert mrt0 + 3072 <= SPC
        stat_a = E(nc.sbuf_tensor("stat_a", [1, 512], F32))
        stat_b = E(nc.sbuf_tensor("stat_b", [1, 512], F32))
        stat_c = E(nc.sbuf_tensor("stat_c", [1, 512], F32))

        pb = [E(nc.psum_tensor(f"pb{i}", [128, 512], F32)) for i in range(7)]
        tb = [E(nc.psum_tensor("tbb0", [128, 1024], BF16))]

        sems = {n: E(nc.semaphore(n)) for n in SEMNAMES}
        blk = E(nc.Block())

        dwar = {"x1b": [], "fuse": [], "preln": []}
        lnscratch = {"w": []}
        freed = {}
        for nm in ("pb0", "pb1", "pb2", "pb3", "pb4", "pb5", "pb6", "tb0",
                   "wqkvb0", "wqkvb1", "kvstage0", "kvstage1",
                   "kh0", "kh1", "vtT", "vtok0", "vtok1",
                   "Ssb0", "Ssb1", "negmax0", "negmax1", "PTb0", "PTb1", "kmax0", "kmax1",
                   "woutb0", "woutb1", "xres0", "xres1",
                   "zb0", "zb1", "sqb0", "sqb1", "mbrb",
                   "wgub0", "wgub1", "wdb0", "wdb1", "gact0", "gact1",
                   "ystage0", "ystage1"):
            freed[nm] = []

        for dst, src in ((bqkv_s, bqkv), (bout_s, bout), (sg_s, sg), (su_s, su),
                         (sd_s, sd), (maskb, maskc), (identb, identc),
                         (identf, identfc)):
            P.emit("sync", lambda e, d=dst, s=src: e.dma_start(out=d[:, :], in_=s[:, :]),
                   inc=[("swc", 16)])
        sw_consts = P.now("swc")
        P.emit("vector", lambda e: e.memset(onesb[:, :], 1.0), inc=[("v", 1)])
        P.emit("vector", lambda e: e.memset(onesf[:, :], 1.0), inc=[("v", 1)])
        for i in range(2):
            P.emit("vector",
                   lambda e, i=i: e.memset(
                       vtok[i].rearrange("p (kc w) -> p kc w", w=132)[:, :, 128:132],
                       1.0),
                   inc=[("v", 1)])
        P.emit("vector", lambda e: e.tensor_scalar_mul(sbq_s[:, :], bqkv_s[:, 0:c.ND],
                                                       RS128),
               wait=[("swc", sw_consts)], inc=[("v", 1)])
        v_setup = P.now("v")

        xbf = bigA
        for dc in range(c.ND):
            P.emit("gpsimd",
                   lambda e, dc=dc: e.dma_start(
                       out=xbf[:, dc * c.TOKC:(dc + 1) * c.TOKC],
                       in_=xT[dc * 128:(dc + 1) * 128, :]),
                   inc=[("gwx", 16)])
        gw_xbf = P.now("gwx")

        nhalves = max(1, c.TOKC // 512)
        width = min(512, c.TOKC)
        ec_order = list(range(c.ND, c.NE3)) + list(range(c.ND))
        for eci, ec in enumerate(ec_order):
            if ec == 0:
                P.emit("gpsimd",
                       lambda e: e.collective_compute(
                           "AllGather", ALU.bypass,
                           replica_groups=[[0, 1, 2, 3], [4, 5, 6, 7]],
                           ins=[kv_own.ap().opt()], outs=[kv_gather.ap().opt()]),
                       wait=[("swkv0", P.now("swkv0")), ("swkv1", P.now("swkv1"))],
                       inc=[("cc", 1)])
            bi = eci % 2
            sn = f"gw{'ab'[bi]}"
            P.dma("gpsimd",
                  lambda e, bi=bi, ec=ec: e.dma_start(
                      out=wqkvb[bi][:, :],
                      in_=wqkvT[ec].rearrange("p d e -> p (d e)")),
                  sn, wait=freed[f"wqkvb{bi}"])
            gw_w = (sn, P.now(sn))

            is_q = ec < c.ND
            is_k = c.ND <= ec < 2 * c.ND
            mm_done = []
            for n in range(nhalves):
                bank = 2 * bi + (n % 2)
                for dc in range(c.ND):
                    P.emit("tensor",
                           lambda e, bi=bi, dc=dc, n=n, bank=bank: e.matmul(
                               pb[bank][:, 0:width],
                               wqkvb[bi][:, dc * 128:(dc + 1) * 128],
                               xbf[:, dc * c.TOKC + n * 512: dc * c.TOKC + n * 512 + width],
                               start=(dc == 0), stop=(dc == c.ND - 1)),
                           wait=([gw_w, ("gwx", gw_xbf)] + freed[f"pb{bank}"])
                           if dc == 0 else [],
                           inc=[("t", 1)] if dc == c.ND - 1 else [])
                mm_done.append((bank, n, P.now("t")))
            freed[f"wqkvb{bi}"] = [("t", P.now("t"))]

            if is_q:
                for bank, n, t_mm in mm_done:
                    P.emit("scalar",
                           lambda e, bank=bank, n=n, ec=ec: e.activation(
                               QTx1b[:, ec * c.TOKC + n * 512: ec * c.TOKC + n * 512 + width],
                               pb[bank][:, 0:width], AF.Identity,
                               bias=sbq_s[:, ec:ec + 1], scale=RS128),
                           wait=[("t", t_mm), ("v", v_setup)], inc=[("s", 1)])
                    freed[f"pb{bank}"] = [("s", P.now("s"))]
            else:
                kvrow = (ec - c.ND) * 128 if is_k else c.D + (ec - 2 * c.ND) * 128
                for bank, n, t_mm in mm_done:
                    P.emit("scalar",
                           lambda e, bank=bank, n=n, bi=bi, ec=ec: e.activation(
                               kvstage[bi][:, n * 512: n * 512 + width],
                               pb[bank][:, 0:width], AF.Identity,
                               bias=bqkv_s[:, ec:ec + 1]),
                           wait=[("t", t_mm), ("swc", sw_consts)] + freed[f"kvstage{bi}"],
                           inc=[("s", 1)])
                    freed[f"pb{bank}"] = [("s", P.now("s"))]
                s_evac = P.now("s")
                P.dma("sync",
                      lambda e, bi=bi, kvrow=kvrow: e.dma_start(
                          out=kv_own[kvrow:kvrow + 128, :], in_=kvstage[bi][:, :]),
                      f"swkv{bi}", wait=[("s", s_evac)])
                freed[f"kvstage{bi}"] = [(f"swkv{bi}", P.now(f"swkv{bi}"))]
        s_qkv_done = P.now("s")

        if debug:
            P.emit("gpsimd",
                   lambda e: e.dma_start(out=dbg["dbg_kvg"][:, :], in_=kv_gather.ap()),
                   wait=[("cc", 1), ("gwd", P.now("gwd"))], inc=[("gwd", 16)])
            P.emit("gpsimd",
                   lambda e: e.dma_start(out=dbg["dbg_qt"][:, :], in_=QTx1b[:, :]),
                   wait=[("s", s_qkv_done), ("gwd", P.now("gwd"))], inc=[("gwd", 16)])
            dwar["x1b"] = [("gwd", P.now("gwd"))]

        attn_out = bigA
        kvg4 = kv_gather.ap().rearrange("(r f) (sb tt) -> f sb r tt", r=4, tt=128)
        nkc_all = c.S // 128
        SCB = (0, 1, 2, 3, 5)
        vfm = [0, 0]

        for h in range(c.NH):
            hb = h % 2
            P.dma("sync",
                  lambda e, h=h, hb=hb: e.dma_start(
                      out=kh[hb][:, 0:c.S].rearrange("p (sb r tt) -> p sb r tt",
                                                     r=4, tt=128),
                      in_=kvg4[h * 128:(h + 1) * 128]),
                  f"swk{hb}", wait=[("cc", 1)] + freed[f"kh{hb}"])
            sw_kh = (f"swk{hb}", P.now(f"swk{hb}"))
            P.dma("sync",
                  lambda e, h=h: e.dma_start(
                      out=vtT[:, 0:c.S].rearrange("p (sb r tt) -> p sb r tt",
                                                  r=4, tt=128),
                      in_=kvg4[c.D + h * 128: c.D + (h + 1) * 128]),
                  "swv", wait=[("cc", 1)] + freed["vtT"])
            sw_vt = P.now("swv")

            vb = f"vtok{hb}"
            for kc8 in range((nkc_all + 7) // 8):
                n_in = min(8, nkc_all - kc8 * 8)
                for i in range(n_in):
                    kc = kc8 * 8 + i
                    P.emit("tensor",
                           lambda e, kc=kc, i=i: e.transpose(
                               tb[0][:, i * 128:(i + 1) * 128],
                               vtT[:, kc * 128:(kc + 1) * 128], identb[:, :]),
                           wait=([("swv", sw_vt), ("swc", sw_consts)] + freed["tb0"]
                                 + (freed[vb] if kc8 == 0 else []))
                           if i == 0 else [],
                           inc=[("t", 1)] if i == n_in - 1 else [])
                t_tr = P.now("t")
                P.emit("vector",
                       lambda e, kc8=kc8, n_in=n_in, hb=hb: e.tensor_copy(
                           vtok[hb][:, kc8 * 8 * 132:]
                           .rearrange("p (kc w) -> p kc w", w=132)[:, 0:n_in, 0:128],
                           tb[0][:, 0:n_in * 128]
                           .rearrange("p (kc w) -> p kc w", w=128)),
                       wait=[("t", t_tr)], inc=[("v", 1)])
                freed["tb0"] = [("v", P.now("v"))]
            freed["vtT"] = [("t", P.now("t"))]
            v_vtok = P.now("v")

            pend_s = {}
            pend_e = {}

            def emit_scores(qt, h=h, hb=hb, sw_kh=sw_kh):
                qtp = qt % 2
                ngr = qt + 1
                for g in range(ngr):
                    bank = SCB[g % len(SCB)]
                    for j4 in range(4):
                        kc = g * 4 + j4
                        diag = (g == qt)
                        P.emit("tensor",
                               lambda e, kc=kc, j4=j4, bank=bank, diag=diag, qt=qt: e.matmul(
                                   pb[bank][:, j4 * 128:(j4 + 1) * 128],
                                   kh[hb][:, kc * 128:(kc + 1) * 128],
                                   QTx1b[:, h * c.TOKC + qt * 128:
                                         h * c.TOKC + (qt + 1) * 128],
                                   start=True, stop=not diag),
                               wait=([sw_kh, ("s", s_qkv_done)] + freed[f"pb{bank}"])
                               if j4 == 0 else [],
                               inc=[("t", 1)] if (not diag and j4 == 3) else [])
                        if diag:
                            P.emit("tensor",
                                   lambda e, j4=j4, bank=bank: e.matmul(
                                       pb[bank][:, j4 * 128:(j4 + 1) * 128],
                                       maskb[:, j4 * 128:(j4 + 1) * 128],
                                       identb[:, :], start=False, stop=True),
                                   inc=[("t", 1)] if j4 == 3 else [])
                    t_g = P.now("t")
                    P.emit("vector",
                           lambda e, g=g, bank=bank, qtp=qtp: e.tensor_reduce(
                               mcols[:, qtp * 8 + g: qtp * 8 + g + 1],
                               pb[bank][:, 0:512], AX.X, ALU.max),
                           wait=[("t", t_g), ("v", vfm[qtp])], inc=[("v", 1)])
                    P.emit("vector",
                           lambda e, g=g, bank=bank, qtp=qtp: e.tensor_copy(
                               Sbf[qtp][:, g * 512:(g + 1) * 512], pb[bank][:, 0:512]),
                           wait=[("v", P.now("v"))] +
                                (freed[f"Ssb{qtp}"] if g == 0 else []),
                           inc=[("v", 1)])
                    freed[f"pb{bank}"] = [("v", P.now("v"))]
                pend_s[qt] = P.now("v")

            def emit_soft(qt, h=h):
                qtp = qt % 2
                ngr = qt + 1
                v_bm = pend_s.pop(qt)
                P.emit("vector",
                       lambda e, qtp=qtp, ngr=ngr: e.tensor_reduce(
                           kmax[:, qtp:qtp + 1], mcols[:, qtp * 8: qtp * 8 + ngr],
                           AX.X, ALU.max),
                       wait=[("v", v_bm)] + freed[f"kmax{qtp}"], inc=[("v", 1)])
                v_km = P.now("v")
                vfm[qtp] = v_km
                P.emit("tensor",
                       lambda e, qtp=qtp: e.transpose(
                           pb[6][0:1, 0:128], kmax[:, qtp:qtp + 1], identf[:, :]),
                       wait=[("v", v_km)] + freed["pb6"], inc=[("t", 1)])
                t_kt = P.now("t")
                P.emit("vector",
                       lambda e, qtp=qtp: e.tensor_reduce(
                           tmn[:, qtp:qtp + 1], pb[6][0:1, 0:128], AX.X, ALU.max,
                           negate=True),
                       wait=[("t", t_kt)], inc=[("v", 1)])
                v_tm = P.now("v")
                P.emit("tensor",
                       lambda e, qtp=qtp: e.matmul(
                           pb[6][:, 128:129], onesf[:, :], tmn[:, qtp:qtp + 1],
                           start=True, stop=True),
                       wait=[("v", v_tm)], inc=[("t", 1)])
                t_bc = P.now("t")
                P.emit("vector",
                       lambda e, qtp=qtp: e.tensor_copy(negmax[:, qtp:qtp + 1],
                                                        pb[6][:, 128:129]),
                       wait=[("t", t_bc)] + freed[f"negmax{qtp}"], inc=[("v", 1)])
                v_nm = P.now("v")
                freed["pb6"] = [("v", v_nm)]
                freed[f"kmax{qtp}"] = [("v", v_nm)]
                exp_marks = {}
                for g in range(ngr):
                    P.emit("scalar",
                           lambda e, g=g, qtp=qtp: e.activation(
                               PTb[qtp][:, g * 512:(g + 1) * 512],
                               Sbf[qtp][:, g * 512:(g + 1) * 512], AF.Exp,
                               bias=negmax[:, qtp:qtp + 1]),
                           wait=[("v", v_nm)] + (freed[f"PTb{qtp}"] if g == 0 else []),
                           inc=[("s", 1)])
                    exp_marks[g] = P.now("s")
                freed[f"Ssb{qtp}"] = [("s", exp_marks[ngr - 1])]
                freed[f"negmax{qtp}"] = [("s", exp_marks[ngr - 1])]
                pend_e[qt] = exp_marks

            def emit_pv(qt, h=h, hb=hb, v_vtok=v_vtok):
                qtp = qt % 2
                exp_marks = pend_e.pop(qt)
                nkc = 4 * (qt + 1)
                for kc in range(nkc):
                    w0 = [("s", exp_marks[kc // 4]), ("v", v_vtok)]
                    if kc == 0:
                        w0 = w0 + freed["pb4"]
                    P.emit("tensor",
                           lambda e, kc=kc, nkc=nkc, hb=hb, qtp=qtp: e.matmul(
                               pb[4][:, 0:129],
                               PTb[qtp][:, kc * 128:(kc + 1) * 128],
                               vtok[hb][:, kc * 132: kc * 132 + 129],
                               start=(kc == 0), stop=(kc == nkc - 1)),
                           wait=w0 if kc % 4 == 0 else [],
                           inc=[("t", 1)] if kc == nkc - 1 else [])
                t_pv = P.now("t")
                freed[f"PTb{qtp}"] = [("t", t_pv)]
                P.emit("vector",
                       lambda e, qtp=qtp: e.reciprocal(rr[:, qtp:qtp + 1],
                                                       pb[4][:, 128:129]),
                       wait=[("t", t_pv)], inc=[("v", 1)])
                v_rr = P.now("v")
                P.emit("scalar",
                       lambda e, h=h, qt=qt, qtp=qtp: e.mul(
                           attn_out[:, qt * c.D + h * 128: qt * c.D + (h + 1) * 128],
                           pb[4][:, 0:128], rr[:, qtp:qtp + 1]),
                       wait=[("v", v_rr)], inc=[("s", 1)])
                freed["pb4"] = [("s", P.now("s"))]

            for qt in range(c.NQT + 2):
                if qt < c.NQT:
                    emit_scores(qt)
                if 1 <= qt <= c.NQT:
                    emit_soft(qt - 1)
                if qt >= 2:
                    emit_pv(qt - 2)
            freed[f"kh{hb}"] = [("t", P.now("t"))]
            freed[f"vtok{hb}"] = [("t", P.now("t"))]
        v_attn_done = max(P.now("v"), 1)
        s_attn_done = P.now("s")

        if debug:
            P.emit("gpsimd",
                   lambda e: e.dma_start(out=dbg["dbg_attn"][:, :],
                                         in_=attn_out[:, 0:c.NQT * c.D]),
                   wait=[("s", s_attn_done), ("gwd", P.now("gwd"))], inc=[("gwd", 16)])
            dwar["fuse"] = [("gwd", P.now("gwd"))]

        for h in range(c.NH):
            for qt in range(c.NQT):
                P.emit("tensor",
                       lambda e, h=h, qt=qt: e.transpose(
                           tb[0][:, qt * 128:(qt + 1) * 128],
                           attn_out[:, qt * c.D + h * 128: qt * c.D + (h + 1) * 128],
                           identb[:, :]),
                       wait=([("s", s_attn_done)] + freed["tb0"]) if qt == 0 else [],
                       inc=[("t", 1)] if qt == c.NQT - 1 else [])
            t_tr = P.now("t")
            P.emit("vector",
                   lambda e, h=h: e.tensor_copy(
                       attnTf[:, h * c.TOKC:(h + 1) * c.TOKC],
                       tb[0][:, 0:c.NQT * 128]),
                   wait=[("t", t_tr)], inc=[("v", 1)])
            freed["tb0"] = [("v", P.now("v"))]
        v_attnT = P.now("v")
        attn_end = [("t", P.now("t")), ("v", P.now("v")), ("s", P.now("s"))]
        for nm in ("woutb0", "woutb1", "xres0", "xres1",
                   "wgub0", "wgub1", "wdb0", "wdb1",
                   "gact0", "gact1", "ystage0", "ystage1"):
            freed[nm] = list(attn_end)
        lnscratch["w"] = list(attn_end)
        freed["pv"] = list(attn_end)
        freed["pb5"] = list(attn_end)

        if debug:
            P.emit("gpsimd",
                   lambda e: e.dma_start(out=dbg["dbg_attnT"][:, :], in_=attnTf[:, :]),
                   wait=[("v", v_attnT), ("gwd", P.now("gwd"))], inc=[("gwd", 16)])
            dwar["fuse"] = dwar["fuse"] + [("gwd", P.now("gwd"))]

        freed["pv"] = freed["pv"]

        def emit_ln(preln_marks, dst_write):
            war = lnscratch["w"] + [("t", P.now("t")),
                                    ("gwa", P.now("gwa")), ("gwb", P.now("gwb"))]
            for ec in range(c.ND):
                zi = ec % 2
                P.emit("gpsimd",
                       lambda e, ec=ec, zi=zi: e.tensor_copy(
                           zb[zi][:, 0:c.TT], preln[:, ec * c.TT:(ec + 1) * c.TT]),
                       wait=[("v", preln_marks[ec])] + freed[f"zb{zi}"] + war,
                       inc=[("g", 1)])
                g_z = P.now("g")
                P.emit("scalar",
                       lambda e, ec=ec, zi=zi: e.activation(
                           sqb[zi][:, 0:c.TT], preln[:, ec * c.TT:(ec + 1) * c.TT],
                           AF.Square),
                       wait=[("v", preln_marks[ec])] + freed[f"sqb{zi}"] + war,
                       inc=[("s", 1)])
                s_q = P.now("s")
                P.emit("tensor",
                       lambda e, ec=ec, zi=zi: e.matmul(
                           pb[4][0:1, 0:c.TT], onesb[:, 0:1], zb[zi][:, 0:c.TT],
                           start=(ec == 0), stop=(ec == c.ND - 1)),
                       wait=[("g", g_z), ("v", v_setup)] + (freed["pv"] if ec == 0 else []),
                       inc=[("t", 1)])
                freed[f"zb{zi}"] = [("t", P.now("t"))]
                P.emit("tensor",
                       lambda e, ec=ec, zi=zi: e.matmul(
                           pb[5][0:1, 0:c.TT], onesb[:, 0:1], sqb[zi][:, 0:c.TT],
                           start=(ec == 0), stop=(ec == c.ND - 1)),
                       wait=[("s", s_q)], inc=[("t", 1)])
                freed[f"sqb{zi}"] = [("t", P.now("t"))]
            t_stats = P.now("t")
            P.emit("vector",
                   lambda e: e.tensor_scalar_mul(stat_a[:, 0:c.TT], pb[4][0:1, 0:c.TT],
                                                 1.0 / c.D),
                   wait=[("t", t_stats)], inc=[("v", 1)])
            P.emit("vector",
                   lambda e: e.tensor_mul(stat_c[:, 0:c.TT], stat_a[:, 0:c.TT],
                                          stat_a[:, 0:c.TT]),
                   wait=[("v", P.now("v"))], inc=[("v", 1)])
            P.emit("vector",
                   lambda e: e.scalar_tensor_tensor(
                       stat_b[:, 0:c.TT], pb[5][0:1, 0:c.TT], 1.0 / c.D,
                       stat_c[:, 0:c.TT], ALU.mult, ALU.subtract),
                   wait=[("v", P.now("v"))], inc=[("v", 1)])
            P.emit("vector",
                   lambda e: e.tensor_scalar_add(stat_b[:, 0:c.TT], stat_b[:, 0:c.TT],
                                                 1e-5),
                   wait=[("v", P.now("v"))], inc=[("v", 1)])
            v_var = P.now("v")
            P.emit("scalar",
                   lambda e: e.activation(stat_c[:, 0:c.TT], stat_b[:, 0:c.TT],
                                          AF.Sqrt),
                   wait=[("v", v_var)], inc=[("s", 1)])
            s_sd = P.now("s")
            P.emit("vector",
                   lambda e: e.reciprocal(stat_b[:, 0:c.TT], stat_c[:, 0:c.TT]),
                   wait=[("s", s_sd)], inc=[("v", 1)])
            v_rstd = P.now("v")
            P.emit("tensor",
                   lambda e: e.matmul(pb[4][:, 0:c.TT], onesf[:, :], stat_a[:, 0:c.TT],
                                      start=True, stop=True),
                   wait=[("v", v_rstd)], inc=[("t", 1)])
            P.emit("tensor",
                   lambda e: e.matmul(pb[5][:, 0:c.TT], onesf[:, :], stat_b[:, 0:c.TT],
                                      start=True, stop=True),
                   inc=[("t", 1)])
            t_bc = P.now("t")
            P.emit("vector", lambda e: e.tensor_copy(mb_s[:, 0:c.TT], pb[4][:, 0:c.TT]),
                   wait=[("t", t_bc)] + freed["mbrb"] + war, inc=[("v", 1)])
            P.emit("vector", lambda e: e.tensor_copy(rb_s[:, 0:c.TT], pb[5][:, 0:c.TT]),
                   inc=[("v", 1)])
            v_mbrb = P.now("v")
            freed["pv"] = [("v", v_mbrb)]
            for ec in range(c.ND):
                P.emit("vector",
                       lambda e, ec=ec: e.tensor_sub(
                           tmp_n[:, 0:c.TT], preln[:, ec * c.TT:(ec + 1) * c.TT],
                           mb_s[:, 0:c.TT]),
                       wait=[("v", max(v_mbrb, P.now("v")))], inc=[("v", 1)])
                dst_write(ec)
            freed["mbrb"] = [("v", P.now("v"))]
            lnscratch["w"] = [("v", P.now("v")), ("t", P.now("t")),
                              ("s", P.now("s")), ("g", P.now("g"))]

        x1b = QTx1b
        for T in range(c.NTILES):
            preln_marks = {}
            for ec in range(c.ND):
                bi = ec % 2
                sn = f"gw{'ab'[bi]}"
                P.dma("gpsimd",
                      lambda e, ec=ec, bi=bi: e.dma_start(
                          out=woutb[bi][:, :],
                          in_=woutT[ec].rearrange("p d e -> p (d e)")),
                      sn, wait=freed[f"woutb{bi}"])
                gw_w = (sn, P.now(sn))
                P.dma("sync",
                      lambda e, ec=ec, bi=bi, T=T: e.dma_start(
                          out=xres[bi][:, 0:c.TT],
                          in_=xT[ec * 128:(ec + 1) * 128, T * c.TT:(T + 1) * c.TT]),
                      f"swx{bi}", wait=freed[f"xres{bi}"])
                sw_x = (f"swx{bi}", P.now(f"swx{bi}"))
                bank = ec % 4
                for dc in range(c.ND):
                    P.emit("tensor",
                           lambda e, dc=dc, bi=bi, bank=bank, T=T: e.matmul(
                               pb[bank][:, 0:c.TT],
                               woutb[bi][:, dc * 128:(dc + 1) * 128],
                               attnTf[:, dc * c.TOKC + T * c.TT:
                                      dc * c.TOKC + (T + 1) * c.TT],
                               start=(dc == 0), stop=(dc == c.ND - 1)),
                           wait=([gw_w, ("v", v_attnT)] + freed[f"pb{bank}"])
                           if dc == 0 else [],
                           inc=[("t", 1)] if dc == c.ND - 1 else [])
                t_mm = P.now("t")
                freed[f"woutb{bi}"] = [("t", t_mm)]
                P.emit("vector",
                       lambda e, ec=ec, bank=bank, bi=bi: e.scalar_tensor_tensor(
                           preln[:, ec * c.TT:(ec + 1) * c.TT], pb[bank][:, 0:c.TT],
                           bout_s[:, ec:ec + 1], xres[bi][:, 0:c.TT],
                           ALU.add, ALU.add),
                       wait=[("t", t_mm), sw_x, ("swc", sw_consts)] + dwar["preln"],
                       inc=[("v", 1)])
                freed[f"pb{bank}"] = [("v", P.now("v"))]
                freed[f"xres{bi}"] = [("v", P.now("v"))]
                preln_marks[ec] = P.now("v")

            if debug and T == 0:
                P.emit("gpsimd",
                       lambda e: e.dma_start(out=dbg["dbg_preln"][:, :],
                                             in_=preln[:, 0:c.ND * c.TT]),
                       wait=[("v", max(preln_marks.values())), ("gwd", P.now("gwd"))], inc=[("gwd", 16)])
                dwar["preln"] = [("gwd", P.now("gwd"))]

            def dst_x1(ec, T=T):
                P.emit("vector",
                       lambda e, ec=ec, T=T: e.tensor_mul(
                           x1b[:, ec * c.TOKC + T * c.TT: ec * c.TOKC + (T + 1) * c.TT],
                           tmp_n[:, 0:c.TT], rb_s[:, 0:c.TT]),
                       wait=[("v", P.now("v"))] + dwar["x1b"], inc=[("v", 1)])
            emit_ln(preln_marks, dst_x1)
        v_x1_done = P.now("v")

        if debug:
            P.emit("gpsimd",
                   lambda e: e.dma_start(out=dbg["dbg_x1b"][:, :], in_=x1b[:, :]),
                   wait=[("v", v_x1_done), ("gwd", P.now("gwd"))], inc=[("gwd", 16)])

        nslotA = (c.NH * c.TOKC) // c.TT

        def fuse_slice(oc):
            if oc < nslotA:
                return bigA[:, oc * c.TT:(oc + 1) * c.TT]
            oc2 = oc - nslotA
            return attnTf[:, oc2 * c.TT:(oc2 + 1) * c.TT]

        for T in range(c.NTILES):
            fuse_last = 0
            for oc in range(c.NO):
                bi = oc % 2
                sn = f"gw{'ab'[bi]}"
                P.dma("gpsimd",
                      lambda e, oc=oc, bi=bi: e.dma_start(
                          out=wgub[bi][:, :], in_=wgu[oc]),
                      sn, wait=freed[f"wgub{bi}"] + lnscratch["w"])
                gw_w = (sn, P.now(sn))
                bk_g = (oc % 2) * 2
                bk_u = bk_g + 1
                for which, bank in ((0, bk_g), (1, bk_u)):
                    for dc in range(c.ND):
                        P.emit("tensor",
                               lambda e, bi=bi, which=which, dc=dc, bank=bank, T=T: e.matmul(
                                   pb[bank][:, 0:c.TT],
                                   wgub[bi][:, which * c.ND * 128 + dc * 128:
                                             which * c.ND * 128 + (dc + 1) * 128],
                                   x1b[:, dc * c.TOKC + T * c.TT:
                                       dc * c.TOKC + (T + 1) * c.TT],
                                   start=(dc == 0), stop=(dc == c.ND - 1)),
                               wait=([gw_w, ("v", v_x1_done)] + freed[f"pb{bank}"])
                               if dc == 0 else [],
                               inc=[("t", 1)] if dc == c.ND - 1 else [])
                t_gu = P.now("t")
                freed[f"wgub{bi}"] = [("t", t_gu)]
                gi = oc % 2
                P.emit("scalar",
                       lambda e, oc=oc, bank=bk_g, gi=gi: e.activation(
                           gact[gi][:, 0:c.TT], pb[bank][:, 0:c.TT], AF.Silu,
                           scale=sg_s[:, oc:oc + 1]),
                       wait=[("t", t_gu), ("swc", sw_consts)] + freed[f"gact{gi}"],
                       inc=[("s", 1)])
                s_gact = P.now("s")
                freed[f"pb{bk_g}"] = [("s", s_gact)]
                P.emit("vector",
                       lambda e, oc=oc, bank=bk_u, gi=gi: e.scalar_tensor_tensor(
                           fuse_slice(oc), pb[bank][:, 0:c.TT], su_s[:, oc:oc + 1],
                           gact[gi][:, 0:c.TT], ALU.mult, ALU.mult),
                       wait=[("s", s_gact)] + dwar["fuse"], inc=[("v", 1)])
                v_f = P.now("v")
                freed[f"pb{bk_u}"] = [("v", v_f)]
                freed[f"gact{gi}"] = [("v", v_f)]
                fuse_last = v_f
            v_fuse_all = fuse_last

            preln_marks = {}
            for ec in range(c.ND):
                bi = ec % 2
                sn = f"gw{'ab'[bi]}"
                P.dma("gpsimd",
                      lambda e, ec=ec, bi=bi: e.dma_start(
                          out=wdb[bi][:, :], in_=wdt[ec]),
                      sn, wait=freed[f"wdb{bi}"] + lnscratch["w"])
                gw_w = (sn, P.now(sn))
                bank = ec % 4
                for oc in range(c.NO):
                    P.emit("tensor",
                           lambda e, oc=oc, bi=bi, bank=bank: e.matmul(
                               pb[bank][:, 0:c.TT],
                               wdb[bi][:, oc * 128:(oc + 1) * 128],
                               fuse_slice(oc),
                               start=(oc == 0), stop=(oc == c.NO - 1)),
                           wait=([gw_w, ("v", v_fuse_all)] + freed[f"pb{bank}"])
                           if oc == 0 else [],
                           inc=[("t", 1)] if oc == c.NO - 1 else [])
                t_mm = P.now("t")
                freed[f"wdb{bi}"] = [("t", t_mm)]
                P.emit("vector",
                       lambda e, ec=ec, bank=bank, T=T: e.scalar_tensor_tensor(
                           preln[:, ec * c.TT:(ec + 1) * c.TT], pb[bank][:, 0:c.TT],
                           sd_s[:, ec:ec + 1],
                           x1b[:, ec * c.TOKC + T * c.TT: ec * c.TOKC + (T + 1) * c.TT],
                           ALU.mult, ALU.add),
                       wait=[("t", t_mm)] + dwar["preln"], inc=[("v", 1)])
                freed[f"pb{bank}"] = [("v", P.now("v"))]
                preln_marks[ec] = P.now("v")

            def dst_y(ec, T=T):
                yi = ec % 2
                P.emit("vector",
                       lambda e, yi=yi: e.tensor_mul(
                           ystage[yi][:, 0:c.TT], tmp_n[:, 0:c.TT], rb_s[:, 0:c.TT]),
                       wait=[("v", P.now("v"))] + freed[f"ystage{yi}"], inc=[("v", 1)])
                v_y = P.now("v")
                P.dma("sync",
                      lambda e, ec=ec, yi=yi, T=T: e.dma_start(
                          out=yT[ec * 128:(ec + 1) * 128, T * c.TT:(T + 1) * c.TT],
                          in_=ystage[yi][:, 0:c.TT]),
                      f"swy{yi}", wait=[("v", v_y)])
                freed[f"ystage{yi}"] = [(f"swy{yi}", P.now(f"swy{yi}"))]
            emit_ln(preln_marks, dst_y)

        P.emit("sync", None, wait=[("swy0", P.now("swy0")), ("swy1", P.now("swy1"))])
        P.emit("gpsimd", None, wait=[("gwd", P.now("gwd")), ("gwa", P.now("gwa")),
                                     ("gwb", P.now("gwb"))])
        P.finalize(blk, sems)
    return nc



def prep_shared(cfg, wqkv, b_qkv, w_out, b_out, w_g, w_u, w_d, s_g, s_u, s_d):
    c = cfg
    f32 = np.float32
    sh = {}
    sh["wqkvT"] = np.ascontiguousarray(
        wqkv.reshape(c.NE3, 128, c.ND, 128).transpose(0, 3, 2, 1)).astype(f32)
    sh["woutT"] = np.ascontiguousarray(
        w_out.reshape(c.ND, 128, c.ND, 128).transpose(0, 3, 2, 1)).astype(f32)
    wgt = w_g.reshape(c.ND, 128, c.NO, 128).transpose(2, 1, 0, 3)
    wut = w_u.reshape(c.ND, 128, c.NO, 128).transpose(2, 1, 0, 3)
    sh["wgu"] = np.ascontiguousarray(
        np.concatenate([wgt.reshape(c.NO, 128, 1, c.ND * 128),
                        wut.reshape(c.NO, 128, 1, c.ND * 128)], axis=2)
        .reshape(c.NO, 128, 2 * c.ND * 128)).astype(np.int8)
    sh["wdt"] = np.ascontiguousarray(
        w_d.reshape(c.NO, 128, c.ND, 128).transpose(2, 1, 0, 3)
        .reshape(c.ND, 128, c.NO * 128)).astype(np.int8)
    sh["bqkv"] = np.ascontiguousarray(b_qkv.reshape(c.NE3, 128).T).astype(f32)
    sh["bout"] = np.ascontiguousarray(b_out.reshape(c.ND, 128).T).astype(f32)
    sh["sg"] = np.ascontiguousarray(s_g.reshape(c.NO, 128).T).astype(f32)
    sh["su"] = np.ascontiguousarray(s_u.reshape(c.NO, 128).T).astype(f32)
    sh["sd"] = np.ascontiguousarray(s_d.reshape(c.ND, 128).T).astype(f32)
    sh["identc"] = np.eye(128).astype(ml_dtypes.bfloat16)
    sh["identfc"] = np.eye(128, dtype=np.float32)
    return sh


def core_token_index(cfg, j):
    c = cfg
    idx = []
    for qt in range(c.NQT):
        base = 512 * qt + 128 * j
        idx.extend(range(base, base + 128))
    return np.asarray(idx)


def core_mask(j):
    q = np.arange(128)[:, None]
    k = np.arange(512)[None, :]
    m = np.where(k <= 128 * j + q, 0.0, NEG).astype(np.float32)
    return np.ascontiguousarray(m.astype(ml_dtypes.bfloat16))


def make_in_maps(cfg, x, sh):
    in_maps, idxs = [], []
    for core in range(NCORES):
        b, j = core // 4, core % 4
        idx = core_token_index(cfg, j)
        idxs.append(idx)
        m = dict(sh)
        m["xT"] = np.ascontiguousarray(x[b].T[:, idx])
        m["maskc"] = core_mask(j)
        in_maps.append(m)
    return in_maps, idxs


def kernel(L0_attn_x, L0_attn_w_qkv, L0_attn_b_qkv, L0_attn_w_out, L0_attn_b_out,
           L0_ffn_w_gate, L0_ffn_w_up, L0_ffn_w_down, L0_ffn_s_gate, L0_ffn_s_up,
           L0_ffn_s_down):
    cfg = Cfg()
    x = np.asarray(L0_attn_x, np.float32)
    sh = prep_shared(cfg, np.asarray(L0_attn_w_qkv, np.float32),
                     np.asarray(L0_attn_b_qkv, np.float32),
                     np.asarray(L0_attn_w_out, np.float32),
                     np.asarray(L0_attn_b_out, np.float32),
                     np.asarray(L0_ffn_w_gate), np.asarray(L0_ffn_w_up),
                     np.asarray(L0_ffn_w_down),
                     np.asarray(L0_ffn_s_gate, np.float32),
                     np.asarray(L0_ffn_s_up, np.float32),
                     np.asarray(L0_ffn_s_down, np.float32))
    in_maps, idxs = make_in_maps(cfg, x, sh)
    nc = build_nc(cfg, debug=False)
    res = run_bass_kernel_spmd(nc, in_maps, core_ids=list(range(NCORES)))
    out = np.empty((cfg.B, cfg.S, cfg.D), np.float32)
    for core in range(NCORES):
        b = core // 4
        out[b, idxs[core], :] = np.asarray(res.results[core]["yT"]).T
    return out


# version 06
# speedup vs baseline: 1.0498x; 1.0498x over previous
import math
import contextlib
import numpy as np
import ml_dtypes

import concourse.bass as bass
import concourse.mybir as mybir
from concourse.bass_utils import run_bass_kernel_spmd

F32 = mybir.dt.float32
BF16 = mybir.dt.bfloat16
I8 = mybir.dt.int8
AF = mybir.ActivationFunctionType
ALU = mybir.AluOpType
AX = mybir.AxisListType

NCORES = 8
NEG = -1e30
RS128 = 1.0 / math.sqrt(128.0)


class Cfg:
    def __init__(self, D=2048, S=4096, H=16, INTER=8192, Bsz=2):
        self.D, self.S, self.H, self.INTER, self.B = D, S, H, INTER, Bsz
        self.ND = D // 128
        self.NH = H
        self.NE3 = 3 * D // 128
        self.NO = INTER // 128
        self.TOKC = Bsz * S // NCORES
        self.NQT = self.TOKC // 128
        assert self.NQT == S // 512
        self.TT = min(512, self.TOKC)
        self.NTILES = self.TOKC // self.TT
        assert H == self.ND, "head_dim must be 128"


ENGINES = ("tensor", "vector", "scalar", "gpsimd", "sync")
SEMNAMES = ("t", "v", "s", "g", "cc", "gwx", "gwa", "gwb", "gwd",
            "swc", "swkv0", "swkv1", "swk0", "swk1", "swv",
            "swx0", "swx1", "swy0", "swy1")


class Prog:

    def __init__(self):
        self.cnt = {n: 0 for n in SEMNAMES}
        self.ops = {e: [] for e in ENGINES}

    def emit(self, eng, fn=None, wait=(), inc=()):
        w = [(sn, val) for (sn, val) in wait if val > 0]
        inc = list(inc)
        for sn, k in inc:
            self.cnt[sn] += k
        self.ops[eng].append((fn, w, inc))

    def now(self, sn):
        return self.cnt[sn]

    def dma(self, eng, fn, sem, wait=()):
        self.emit(eng, fn, wait=list(wait) + [(sem, self.now(sem))],
                  inc=[(sem, 16)])
        return self.now(sem)

    def finalize(self, block, semh):
        for ename in ENGINES:
            ops = self.ops[ename]

            def body(eng, ops=ops):
                lastw = {}
                for fn, waits, incs in ops:
                    for sn, val in waits:
                        if lastw.get(sn, 0) < val:
                            eng.wait_ge(semh[sn], val)
                            lastw[sn] = val
                    ins = fn(eng) if fn is not None else None
                    for sn, k in incs:
                        ins.then_inc(semh[sn], k)
            getattr(block, ename)(body)


def build_nc(cfg: Cfg, debug=False):
    c = cfg
    nc = bass.Bass("TRN2", target_bir_lowering=False)

    xT = nc.dram_tensor("xT", [c.D, c.TOKC], F32, kind="ExternalInput").ap()
    wqkvT = nc.dram_tensor("wqkvT", [c.NE3, 128, c.ND, 128], F32, kind="ExternalInput").ap()
    woutT = nc.dram_tensor("woutT", [c.ND, 128, c.ND, 128], F32, kind="ExternalInput").ap()
    wgu = nc.dram_tensor("wgu", [c.NO, 128, 2 * c.ND * 128], I8, kind="ExternalInput").ap()
    wdt = nc.dram_tensor("wdt", [c.ND, 128, c.NO * 128], I8, kind="ExternalInput").ap()
    bqkv = nc.dram_tensor("bqkv", [128, c.NE3], F32, kind="ExternalInput").ap()
    bout = nc.dram_tensor("bout", [128, c.ND], F32, kind="ExternalInput").ap()
    sg = nc.dram_tensor("sg", [128, c.NO], F32, kind="ExternalInput").ap()
    su = nc.dram_tensor("su", [128, c.NO], F32, kind="ExternalInput").ap()
    sd = nc.dram_tensor("sd", [128, c.ND], F32, kind="ExternalInput").ap()
    maskc = nc.dram_tensor("maskc", [128, 512], BF16, kind="ExternalInput").ap()
    identc = nc.dram_tensor("identc", [128, 128], BF16, kind="ExternalInput").ap()
    identfc = nc.dram_tensor("identfc", [128, 128], F32, kind="ExternalInput").ap()
    yT = nc.dram_tensor("yT", [c.D, c.TOKC], F32, kind="ExternalOutput").ap()

    kv_own = nc.dram_tensor("kv_own", [2 * c.D, c.TOKC], BF16)
    kv_gather = nc.dram_tensor("kv_gather", [4 * 2 * c.D, c.TOKC], BF16)

    dbg = {}
    if debug:
        for nm, shp in (("dbg_qt", [128, c.NH * c.TOKC]),
                        ("dbg_kvg", [4 * 2 * c.D, c.TOKC]),
                        ("dbg_attn", [128, c.NQT * c.D]),
                        ("dbg_attnT", [128, c.ND * c.TOKC]),
                        ("dbg_preln", [128, c.ND * c.TT]),
                        ("dbg_x1b", [128, c.ND * c.TOKC])):
            dbg[nm] = nc.dram_tensor(nm, shp, F32, kind="ExternalOutput").ap()

    P = Prog()

    with contextlib.ExitStack() as ctx:
        E = ctx.enter_context

        QTx1b = E(nc.sbuf_tensor("QTx1b", [128, c.NH * c.TOKC], BF16))
        bigA = E(nc.sbuf_tensor("bigA", [128, c.NH * c.TOKC], BF16))
        attnTf = E(nc.sbuf_tensor("attnTf", [128, c.ND * c.TOKC], BF16))

        VW = (c.S // 128) * 132
        KVC = max(3 * c.S + 2 * VW, 2 * c.ND * c.TT + 2 * c.ND * 128 + 8 * c.TT)
        kvar = E(nc.sbuf_tensor("kvar", [128, KVC], BF16))
        kh = [kvar[:, i * c.S:(i + 1) * c.S] for i in range(2)]
        vtT = kvar[:, 2 * c.S:3 * c.S]
        vtok = [kvar[:, 3 * c.S + i * VW: 3 * c.S + (i + 1) * VW] for i in range(2)]
        preln = kvar[:, 0:2 * c.ND * c.TT].bitcast(F32)
        wo0 = 2 * c.ND * c.TT
        woutb = [kvar[:, wo0 + i * c.ND * 128: wo0 + (i + 1) * c.ND * 128]
                 for i in range(2)]
        xr0 = wo0 + 2 * c.ND * 128
        xres = [kvar[:, xr0 + i * 2 * c.TT: xr0 + (i + 1) * 2 * c.TT].bitcast(F32)
                for i in range(2)]
        gact = [kvar[:, wo0 + i * 2 * c.TT: wo0 + (i + 1) * 2 * c.TT].bitcast(F32)
                for i in range(2)]
        ys0 = wo0 + 4 * c.TT
        ystage = [kvar[:, ys0 + i * 2 * c.TT: ys0 + (i + 1) * 2 * c.TT].bitcast(F32)
                  for i in range(2)]
        assert ys0 + 4 * c.TT <= KVC

        SPC = max(5 * c.S, 2 * c.NO * 128 + 4 * c.ND * 128,
                  2 * c.ND * 128 + 2 * c.TOKC + 4 * 512 + 3 * 1024)
        spar = E(nc.sbuf_tensor("spar", [128, SPC], BF16))
        Sbf = spar[:, 0:c.S]
        PTb = [spar[:, (2 + i) * c.S:(3 + i) * c.S] for i in range(2)]
        wqkvb = [spar[:, i * c.ND * 128:(i + 1) * c.ND * 128] for i in range(2)]
        kvs0 = 2 * c.ND * 128
        kvstage = [spar[:, kvs0 + i * c.TOKC: kvs0 + (i + 1) * c.TOKC] for i in range(2)]
        assert kvs0 + 2 * c.TOKC <= 2 * c.S
        wdb = [spar[:, i * c.NO * 128:(i + 1) * c.NO * 128] for i in range(2)]
        wg0 = 2 * c.NO * 128
        wgub = [spar[:, wg0 + i * 2 * c.ND * 128: wg0 + (i + 1) * 2 * c.ND * 128]
                for i in range(2)]
        assert wg0 + 4 * c.ND * 128 <= SPC

        bqkv_s = E(nc.sbuf_tensor("bqkv_s", [128, c.NE3], F32))
        sbq_s = E(nc.sbuf_tensor("sbq_s", [128, c.ND], F32))
        bout_s = E(nc.sbuf_tensor("bout_s", [128, c.ND], F32))
        sg_s = E(nc.sbuf_tensor("sg_s", [128, c.NO], F32))
        su_s = E(nc.sbuf_tensor("su_s", [128, c.NO], F32))
        sd_s = E(nc.sbuf_tensor("sd_s", [128, c.ND], F32))
        maskb = E(nc.sbuf_tensor("maskb", [128, 512], BF16))
        identb = E(nc.sbuf_tensor("identb", [128, 128], BF16))
        identf = E(nc.sbuf_tensor("identf", [128, 128], F32))
        onesb = E(nc.sbuf_tensor("onesb", [128, 128], BF16))
        onesf = E(nc.sbuf_tensor("onesf", [1, 128], F32))
        negmax = E(nc.sbuf_tensor("negmax", [128, 2], F32))
        mcols = E(nc.sbuf_tensor("mcols", [128, 8], F32))
        kmax = E(nc.sbuf_tensor("kmax", [128, 2], F32))
        tmn = E(nc.sbuf_tensor("tmn", [1, 2], F32))
        sumv = E(nc.sbuf_tensor("sumv", [128, 2], F32))
        rr = E(nc.sbuf_tensor("rr", [128, 2], F32))
        ln0 = kvs0 + 2 * c.TOKC
        zb = [spar[:, ln0 + i * 512: ln0 + (i + 1) * 512] for i in range(2)]
        sqb = [spar[:, ln0 + (2 + i) * 512: ln0 + (3 + i) * 512] for i in range(2)]
        mrt0 = ln0 + 4 * 512
        mb_s = spar[:, mrt0: mrt0 + 1024].bitcast(F32)
        rb_s = spar[:, mrt0 + 1024: mrt0 + 2048].bitcast(F32)
        tmp_n = spar[:, mrt0 + 2048: mrt0 + 3072].bitcast(F32)
        assert mrt0 + 3072 <= SPC
        stat_a = E(nc.sbuf_tensor("stat_a", [1, 512], F32))
        stat_b = E(nc.sbuf_tensor("stat_b", [1, 512], F32))
        stat_c = E(nc.sbuf_tensor("stat_c", [1, 512], F32))

        pb = [E(nc.psum_tensor(f"pb{i}", [128, 512], F32)) for i in range(7)]
        tb = [E(nc.psum_tensor("tbb0", [128, 1024], BF16))]

        sems = {n: E(nc.semaphore(n)) for n in SEMNAMES}
        blk = E(nc.Block())

        dwar = {"x1b": [], "fuse": [], "preln": []}
        lnscratch = {"w": []}
        freed = {}
        for nm in ("pb0", "pb1", "pb2", "pb3", "pb4", "pb5", "pb6", "tb0",
                   "wqkvb0", "wqkvb1", "kvstage0", "kvstage1",
                   "kh0", "kh1", "vtT", "vtok0", "vtok1",
                   "Ssb", "negmax0", "negmax1", "PTb0", "PTb1", "kmax0", "kmax1",
                   "woutb0", "woutb1", "xres0", "xres1",
                   "zb0", "zb1", "sqb0", "sqb1", "mbrb",
                   "wgub0", "wgub1", "wdb0", "wdb1", "gact0", "gact1",
                   "ystage0", "ystage1"):
            freed[nm] = []

        for dst, src in ((bqkv_s, bqkv), (bout_s, bout), (sg_s, sg), (su_s, su),
                         (sd_s, sd), (maskb, maskc), (identb, identc),
                         (identf, identfc)):
            P.emit("sync", lambda e, d=dst, s=src: e.dma_start(out=d[:, :], in_=s[:, :]),
                   inc=[("swc", 16)])
        sw_consts = P.now("swc")
        P.emit("vector", lambda e: e.memset(onesb[:, :], 1.0), inc=[("v", 1)])
        P.emit("vector", lambda e: e.memset(onesf[:, :], 1.0), inc=[("v", 1)])
        for i in range(2):
            P.emit("vector",
                   lambda e, i=i: e.memset(
                       vtok[i].rearrange("p (kc w) -> p kc w", w=132)[:, :, 128:132],
                       1.0),
                   inc=[("v", 1)])
        P.emit("vector", lambda e: e.tensor_scalar_mul(sbq_s[:, :], bqkv_s[:, 0:c.ND],
                                                       RS128),
               wait=[("swc", sw_consts)], inc=[("v", 1)])
        v_setup = P.now("v")

        xbf = bigA
        for dc in range(c.ND):
            P.emit("gpsimd",
                   lambda e, dc=dc: e.dma_start(
                       out=xbf[:, dc * c.TOKC:(dc + 1) * c.TOKC],
                       in_=xT[dc * 128:(dc + 1) * 128, :]),
                   inc=[("gwx", 16)])
        gw_xbf = P.now("gwx")

        nhalves = max(1, c.TOKC // 512)
        width = min(512, c.TOKC)
        ec_order = list(range(c.ND, c.NE3)) + list(range(c.ND))
        for eci, ec in enumerate(ec_order):
            if ec == 0:
                P.emit("gpsimd",
                       lambda e: e.collective_compute(
                           "AllGather", ALU.bypass,
                           replica_groups=[[0, 1, 2, 3], [4, 5, 6, 7]],
                           ins=[kv_own.ap().opt()], outs=[kv_gather.ap().opt()]),
                       wait=[("swkv0", P.now("swkv0")), ("swkv1", P.now("swkv1"))],
                       inc=[("cc", 1)])
            bi = eci % 2
            sn = f"gw{'ab'[bi]}"
            P.dma("gpsimd",
                  lambda e, bi=bi, ec=ec: e.dma_start(
                      out=wqkvb[bi][:, :],
                      in_=wqkvT[ec].rearrange("p d e -> p (d e)")),
                  sn, wait=freed[f"wqkvb{bi}"])
            gw_w = (sn, P.now(sn))

            is_q = ec < c.ND
            is_k = c.ND <= ec < 2 * c.ND
            mm_done = []
            for n in range(nhalves):
                bank = 2 * bi + (n % 2)
                for dc in range(c.ND):
                    P.emit("tensor",
                           lambda e, bi=bi, dc=dc, n=n, bank=bank: e.matmul(
                               pb[bank][:, 0:width],
                               wqkvb[bi][:, dc * 128:(dc + 1) * 128],
                               xbf[:, dc * c.TOKC + n * 512: dc * c.TOKC + n * 512 + width],
                               start=(dc == 0), stop=(dc == c.ND - 1)),
                           wait=([gw_w, ("gwx", gw_xbf)] + freed[f"pb{bank}"])
                           if dc == 0 else [],
                           inc=[("t", 1)] if dc == c.ND - 1 else [])
                mm_done.append((bank, n, P.now("t")))
            freed[f"wqkvb{bi}"] = [("t", P.now("t"))]

            if is_q:
                for bank, n, t_mm in mm_done:
                    P.emit("scalar",
                           lambda e, bank=bank, n=n, ec=ec: e.activation(
                               QTx1b[:, ec * c.TOKC + n * 512: ec * c.TOKC + n * 512 + width],
                               pb[bank][:, 0:width], AF.Identity,
                               bias=sbq_s[:, ec:ec + 1], scale=RS128),
                           wait=[("t", t_mm), ("v", v_setup)], inc=[("s", 1)])
                    freed[f"pb{bank}"] = [("s", P.now("s"))]
            else:
                kvrow = (ec - c.ND) * 128 if is_k else c.D + (ec - 2 * c.ND) * 128
                for bank, n, t_mm in mm_done:
                    P.emit("scalar",
                           lambda e, bank=bank, n=n, bi=bi, ec=ec: e.activation(
                               kvstage[bi][:, n * 512: n * 512 + width],
                               pb[bank][:, 0:width], AF.Identity,
                               bias=bqkv_s[:, ec:ec + 1]),
                           wait=[("t", t_mm), ("swc", sw_consts)] + freed[f"kvstage{bi}"],
                           inc=[("s", 1)])
                    freed[f"pb{bank}"] = [("s", P.now("s"))]
                s_evac = P.now("s")
                P.dma("sync",
                      lambda e, bi=bi, kvrow=kvrow: e.dma_start(
                          out=kv_own[kvrow:kvrow + 128, :], in_=kvstage[bi][:, :]),
                      f"swkv{bi}", wait=[("s", s_evac)])
                freed[f"kvstage{bi}"] = [(f"swkv{bi}", P.now(f"swkv{bi}"))]
        s_qkv_done = P.now("s")

        if debug:
            P.emit("gpsimd",
                   lambda e: e.dma_start(out=dbg["dbg_kvg"][:, :], in_=kv_gather.ap()),
                   wait=[("cc", 1), ("gwd", P.now("gwd"))], inc=[("gwd", 16)])
            P.emit("gpsimd",
                   lambda e: e.dma_start(out=dbg["dbg_qt"][:, :], in_=QTx1b[:, :]),
                   wait=[("s", s_qkv_done), ("gwd", P.now("gwd"))], inc=[("gwd", 16)])
            dwar["x1b"] = [("gwd", P.now("gwd"))]

        attn_out = bigA
        kvg4 = kv_gather.ap().rearrange("(r f) (sb tt) -> f sb r tt", r=4, tt=128)
        nkc_all = c.S // 128
        SCB = (0, 1, 2, 3, 5)
        vfm = [0]

        for h in range(c.NH):
            hb = h % 2
            P.dma("sync",
                  lambda e, h=h, hb=hb: e.dma_start(
                      out=kh[hb][:, 0:c.S].rearrange("p (sb r tt) -> p sb r tt",
                                                     r=4, tt=128),
                      in_=kvg4[h * 128:(h + 1) * 128]),
                  f"swk{hb}", wait=[("cc", 1)] + freed[f"kh{hb}"])
            sw_kh = (f"swk{hb}", P.now(f"swk{hb}"))
            P.dma("sync",
                  lambda e, h=h: e.dma_start(
                      out=vtT[:, 0:c.S].rearrange("p (sb r tt) -> p sb r tt",
                                                  r=4, tt=128),
                      in_=kvg4[c.D + h * 128: c.D + (h + 1) * 128]),
                  "swv", wait=[("cc", 1)] + freed["vtT"])
            sw_vt = P.now("swv")

            vb = f"vtok{hb}"
            for kc8 in range((nkc_all + 7) // 8):
                n_in = min(8, nkc_all - kc8 * 8)
                for i in range(n_in):
                    kc = kc8 * 8 + i
                    P.emit("tensor",
                           lambda e, kc=kc, i=i: e.transpose(
                               tb[0][:, i * 128:(i + 1) * 128],
                               vtT[:, kc * 128:(kc + 1) * 128], identb[:, :]),
                           wait=([("swv", sw_vt), ("swc", sw_consts)] + freed["tb0"]
                                 + (freed[vb] if kc8 == 0 else []))
                           if i == 0 else [],
                           inc=[("t", 1)] if i == n_in - 1 else [])
                t_tr = P.now("t")
                P.emit("vector",
                       lambda e, kc8=kc8, n_in=n_in, hb=hb: e.tensor_copy(
                           vtok[hb][:, kc8 * 8 * 132:]
                           .rearrange("p (kc w) -> p kc w", w=132)[:, 0:n_in, 0:128],
                           tb[0][:, 0:n_in * 128]
                           .rearrange("p (kc w) -> p kc w", w=128)),
                       wait=[("t", t_tr)], inc=[("v", 1)])
                freed["tb0"] = [("v", P.now("v"))]
            freed["vtT"] = [("t", P.now("t"))]
            v_vtok = P.now("v")

            pend = {}
            pend_g = {}
            pend_k = {}
            pend_c = {}

            def emit_scores(qt, h=h, hb=hb, sw_kh=sw_kh):
                qtp = qt % 2
                ngr = qt + 1
                spills = max(0, ngr - len(SCB))
                gbank = {}
                for g in range(ngr):
                    bank = SCB[g % len(SCB)]
                    gbank[g] = bank
                    for j4 in range(4):
                        kc = g * 4 + j4
                        diag = (g == qt)
                        P.emit("tensor",
                               lambda e, kc=kc, j4=j4, bank=bank, diag=diag: e.matmul(
                                   pb[bank][:, j4 * 128:(j4 + 1) * 128],
                                   kh[hb][:, kc * 128:(kc + 1) * 128],
                                   QTx1b[:, h * c.TOKC + qt * 128:
                                         h * c.TOKC + (qt + 1) * 128],
                                   start=True, stop=not diag),
                               wait=([sw_kh, ("s", s_qkv_done)] + freed[f"pb{bank}"])
                               if j4 == 0 else [],
                               inc=[("t", 1)] if (not diag and j4 == 3) else [])
                        if diag:
                            P.emit("tensor",
                                   lambda e, j4=j4, bank=bank: e.matmul(
                                       pb[bank][:, j4 * 128:(j4 + 1) * 128],
                                       maskb[:, j4 * 128:(j4 + 1) * 128],
                                       identb[:, :], start=False, stop=True),
                                   inc=[("t", 1)] if j4 == 3 else [])
                    t_g = P.now("t")
                    P.emit("vector",
                           lambda e, g=g, bank=bank: e.tensor_reduce(
                               mcols[:, g:g + 1], pb[bank][:, 0:512], AX.X, ALU.max),
                           wait=[("t", t_g), ("v", vfm[0])], inc=[("v", 1)])
                    v_bm = P.now("v")
                    if g < spills:
                        P.emit("vector",
                               lambda e, g=g, bank=bank: e.tensor_copy(
                                   Sbf[:, g * 512:(g + 1) * 512], pb[bank][:, 0:512]),
                               wait=(freed["Ssb"] if g == 0 else []), inc=[("v", 1)])
                        freed[f"pb{bank}"] = [("v", P.now("v"))]
                pend_g[qt] = (gbank, spills)
                pend_k[qt] = P.now("v")

            def emit_kchain_a(qt):
                qtp = qt % 2
                ngr = qt + 1
                v_bm = pend_k.pop(qt)
                P.emit("vector",
                       lambda e, qtp=qtp, ngr=ngr: e.tensor_reduce(
                           kmax[:, qtp:qtp + 1], mcols[:, 0:ngr], AX.X, ALU.max),
                       wait=[("v", v_bm)] + freed[f"kmax{qtp}"], inc=[("v", 1)])
                v_km = P.now("v")
                vfm[0] = v_km
                P.emit("tensor",
                       lambda e, qtp=qtp: e.transpose(
                           pb[6][0:1, 0:128], kmax[:, qtp:qtp + 1], identf[:, :]),
                       wait=[("v", v_km)] + freed["pb6"], inc=[("t", 1)])
                t_kt = P.now("t")
                P.emit("vector",
                       lambda e, qtp=qtp: e.tensor_reduce(
                           tmn[:, qtp:qtp + 1], pb[6][0:1, 0:128], AX.X, ALU.max,
                           negate=True),
                       wait=[("t", t_kt)], inc=[("v", 1)])
                pend_c[qt] = P.now("v")

            def emit_kchain_b(qt):
                qtp = qt % 2
                ngr = qt + 1
                v_tm = pend_c.pop(qt)
                P.emit("tensor",
                       lambda e, qtp=qtp: e.matmul(
                           pb[6][:, 128:129], onesf[:, :], tmn[:, qtp:qtp + 1],
                           start=True, stop=True),
                       wait=[("v", v_tm)], inc=[("t", 1)])
                t_bc = P.now("t")
                P.emit("scalar",
                       lambda e, qtp=qtp: e.copy(negmax[:, qtp:qtp + 1],
                                                 pb[6][:, 128:129]),
                       wait=[("t", t_bc)] + freed[f"negmax{qtp}"], inc=[("s", 1)])
                s_nm = P.now("s")
                freed["pb6"] = [("s", s_nm)]
                freed[f"kmax{qtp}"] = [("t", t_bc)]
                gbank, spills = pend_g.pop(qt)
                exp_marks = {}
                for g in range(ngr):
                    src_sp = g < spills
                    P.emit("scalar",
                           lambda e, g=g, qtp=qtp, src_sp=src_sp,
                           bank=gbank[g]: e.activation(
                               PTb[qtp][:, g * 512:(g + 1) * 512],
                               (Sbf[:, g * 512:(g + 1) * 512] if src_sp
                                else pb[bank][:, 0:512]), AF.Exp,
                               bias=negmax[:, qtp:qtp + 1]),
                           wait=[("s", s_nm)] + (freed[f"PTb{qtp}"] if g == 0 else []),
                           inc=[("s", 1)])
                    if not src_sp:
                        freed[f"pb{gbank[g]}"] = [("s", P.now("s"))]
                    exp_marks[g] = P.now("s")
                if spills:
                    freed["Ssb"] = [("s", exp_marks[spills - 1])]
                freed[f"negmax{qtp}"] = [("s", exp_marks[ngr - 1])]
                pend[qt] = (exp_marks, P.now("s"))

            def emit_ptpv(qt, h=h, hb=hb, v_vtok=v_vtok):
                qtp = qt % 2
                exp_marks, s_exp_all = pend.pop(qt)
                ngr = qt + 1
                nkc = 4 * ngr
                for kc in range(nkc):
                    w0 = [("s", exp_marks[kc // 4]), ("v", v_vtok)]
                    if kc == 0:
                        w0 = w0 + freed["pb4"]
                    P.emit("tensor",
                           lambda e, kc=kc, nkc=nkc, hb=hb, qtp=qtp: e.matmul(
                               pb[4][:, 0:129],
                               PTb[qtp][:, kc * 128:(kc + 1) * 128],
                               vtok[hb][:, kc * 132: kc * 132 + 129],
                               start=(kc == 0), stop=(kc == nkc - 1)),
                           wait=w0 if kc % 4 == 0 else [],
                           inc=[("t", 1)] if kc == nkc - 1 else [])
                t_pv = P.now("t")
                freed[f"PTb{qtp}"] = [("t", t_pv)]
                P.emit("vector",
                       lambda e, qtp=qtp: e.reciprocal(rr[:, qtp:qtp + 1],
                                                       pb[4][:, 128:129]),
                       wait=[("t", t_pv)], inc=[("v", 1)])
                v_rr = P.now("v")
                P.emit("scalar",
                       lambda e, h=h, qt=qt, qtp=qtp: e.mul(
                           attn_out[:, qt * c.D + h * 128: qt * c.D + (h + 1) * 128],
                           pb[4][:, 0:128], rr[:, qtp:qtp + 1]),
                       wait=[("v", v_rr)], inc=[("s", 1)])
                freed["pb4"] = [("s", P.now("s"))]

            for qt in range(c.NQT + 1):
                if qt < c.NQT:
                    emit_scores(qt)
                    emit_kchain_a(qt)
                if qt >= 1:
                    emit_ptpv(qt - 1)
                if qt < c.NQT:
                    emit_kchain_b(qt)
            freed[f"kh{hb}"] = [("t", P.now("t"))]
            freed[f"vtok{hb}"] = [("t", P.now("t"))]
        v_attn_done = max(P.now("v"), 1)
        s_attn_done = P.now("s")

        if debug:
            P.emit("gpsimd",
                   lambda e: e.dma_start(out=dbg["dbg_attn"][:, :],
                                         in_=attn_out[:, 0:c.NQT * c.D]),
                   wait=[("s", s_attn_done), ("gwd", P.now("gwd"))], inc=[("gwd", 16)])
            dwar["fuse"] = [("gwd", P.now("gwd"))]

        for h in range(c.NH):
            for qt in range(c.NQT):
                P.emit("tensor",
                       lambda e, h=h, qt=qt: e.transpose(
                           tb[0][:, qt * 128:(qt + 1) * 128],
                           attn_out[:, qt * c.D + h * 128: qt * c.D + (h + 1) * 128],
                           identb[:, :]),
                       wait=([("s", s_attn_done)] + freed["tb0"]) if qt == 0 else [],
                       inc=[("t", 1)] if qt == c.NQT - 1 else [])
            t_tr = P.now("t")
            P.emit("vector",
                   lambda e, h=h: e.tensor_copy(
                       attnTf[:, h * c.TOKC:(h + 1) * c.TOKC],
                       tb[0][:, 0:c.NQT * 128]),
                   wait=[("t", t_tr)], inc=[("v", 1)])
            freed["tb0"] = [("v", P.now("v"))]
        v_attnT = P.now("v")
        attn_end = [("t", P.now("t")), ("v", P.now("v")), ("s", P.now("s"))]
        for nm in ("woutb0", "woutb1", "xres0", "xres1",
                   "wgub0", "wgub1", "wdb0", "wdb1",
                   "gact0", "gact1", "ystage0", "ystage1"):
            freed[nm] = list(attn_end)
        lnscratch["w"] = list(attn_end)
        freed["pv"] = list(attn_end)
        freed["pb5"] = list(attn_end)

        if debug:
            P.emit("gpsimd",
                   lambda e: e.dma_start(out=dbg["dbg_attnT"][:, :], in_=attnTf[:, :]),
                   wait=[("v", v_attnT), ("gwd", P.now("gwd"))], inc=[("gwd", 16)])
            dwar["fuse"] = dwar["fuse"] + [("gwd", P.now("gwd"))]

        freed["pv"] = freed["pv"]

        def emit_ln(preln_marks, dst_write):
            war = lnscratch["w"] + [("t", P.now("t")),
                                    ("gwa", P.now("gwa")), ("gwb", P.now("gwb"))]
            for ec in range(c.ND):
                zi = ec % 2
                P.emit("gpsimd",
                       lambda e, ec=ec, zi=zi: e.tensor_copy(
                           zb[zi][:, 0:c.TT], preln[:, ec * c.TT:(ec + 1) * c.TT]),
                       wait=[("v", preln_marks[ec])] + freed[f"zb{zi}"] + war,
                       inc=[("g", 1)])
                g_z = P.now("g")
                P.emit("scalar",
                       lambda e, ec=ec, zi=zi: e.activation(
                           sqb[zi][:, 0:c.TT], preln[:, ec * c.TT:(ec + 1) * c.TT],
                           AF.Square),
                       wait=[("v", preln_marks[ec])] + freed[f"sqb{zi}"] + war,
                       inc=[("s", 1)])
                s_q = P.now("s")
                P.emit("tensor",
                       lambda e, ec=ec, zi=zi: e.matmul(
                           pb[4][0:1, 0:c.TT], onesb[:, 0:1], zb[zi][:, 0:c.TT],
                           start=(ec == 0), stop=(ec == c.ND - 1)),
                       wait=[("g", g_z), ("v", v_setup)] + (freed["pv"] if ec == 0 else []),
                       inc=[("t", 1)])
                freed[f"zb{zi}"] = [("t", P.now("t"))]
                P.emit("tensor",
                       lambda e, ec=ec, zi=zi: e.matmul(
                           pb[5][0:1, 0:c.TT], onesb[:, 0:1], sqb[zi][:, 0:c.TT],
                           start=(ec == 0), stop=(ec == c.ND - 1)),
                       wait=[("s", s_q)], inc=[("t", 1)])
                freed[f"sqb{zi}"] = [("t", P.now("t"))]
            t_stats = P.now("t")
            P.emit("vector",
                   lambda e: e.tensor_scalar_mul(stat_a[:, 0:c.TT], pb[4][0:1, 0:c.TT],
                                                 1.0 / c.D),
                   wait=[("t", t_stats)], inc=[("v", 1)])
            P.emit("vector",
                   lambda e: e.tensor_mul(stat_c[:, 0:c.TT], stat_a[:, 0:c.TT],
                                          stat_a[:, 0:c.TT]),
                   wait=[("v", P.now("v"))], inc=[("v", 1)])
            P.emit("vector",
                   lambda e: e.scalar_tensor_tensor(
                       stat_b[:, 0:c.TT], pb[5][0:1, 0:c.TT], 1.0 / c.D,
                       stat_c[:, 0:c.TT], ALU.mult, ALU.subtract),
                   wait=[("v", P.now("v"))], inc=[("v", 1)])
            P.emit("vector",
                   lambda e: e.tensor_scalar_add(stat_b[:, 0:c.TT], stat_b[:, 0:c.TT],
                                                 1e-5),
                   wait=[("v", P.now("v"))], inc=[("v", 1)])
            v_var = P.now("v")
            P.emit("scalar",
                   lambda e: e.activation(stat_c[:, 0:c.TT], stat_b[:, 0:c.TT],
                                          AF.Sqrt),
                   wait=[("v", v_var)], inc=[("s", 1)])
            s_sd = P.now("s")
            P.emit("vector",
                   lambda e: e.reciprocal(stat_b[:, 0:c.TT], stat_c[:, 0:c.TT]),
                   wait=[("s", s_sd)], inc=[("v", 1)])
            v_rstd = P.now("v")
            P.emit("tensor",
                   lambda e: e.matmul(pb[4][:, 0:c.TT], onesf[:, :], stat_a[:, 0:c.TT],
                                      start=True, stop=True),
                   wait=[("v", v_rstd)], inc=[("t", 1)])
            P.emit("tensor",
                   lambda e: e.matmul(pb[5][:, 0:c.TT], onesf[:, :], stat_b[:, 0:c.TT],
                                      start=True, stop=True),
                   inc=[("t", 1)])
            t_bc = P.now("t")
            P.emit("vector", lambda e: e.tensor_copy(mb_s[:, 0:c.TT], pb[4][:, 0:c.TT]),
                   wait=[("t", t_bc)] + freed["mbrb"] + war, inc=[("v", 1)])
            P.emit("vector", lambda e: e.tensor_copy(rb_s[:, 0:c.TT], pb[5][:, 0:c.TT]),
                   inc=[("v", 1)])
            v_mbrb = P.now("v")
            freed["pv"] = [("v", v_mbrb)]
            for ec in range(c.ND):
                P.emit("vector",
                       lambda e, ec=ec: e.tensor_sub(
                           tmp_n[:, 0:c.TT], preln[:, ec * c.TT:(ec + 1) * c.TT],
                           mb_s[:, 0:c.TT]),
                       wait=[("v", max(v_mbrb, P.now("v")))], inc=[("v", 1)])
                dst_write(ec)
            freed["mbrb"] = [("v", P.now("v"))]
            lnscratch["w"] = [("v", P.now("v")), ("t", P.now("t")),
                              ("s", P.now("s")), ("g", P.now("g"))]

        x1b = QTx1b
        for T in range(c.NTILES):
            preln_marks = {}
            for ec in range(c.ND):
                bi = ec % 2
                sn = f"gw{'ab'[bi]}"
                P.dma("gpsimd",
                      lambda e, ec=ec, bi=bi: e.dma_start(
                          out=woutb[bi][:, :],
                          in_=woutT[ec].rearrange("p d e -> p (d e)")),
                      sn, wait=freed[f"woutb{bi}"])
                gw_w = (sn, P.now(sn))
                P.dma("sync",
                      lambda e, ec=ec, bi=bi, T=T: e.dma_start(
                          out=xres[bi][:, 0:c.TT],
                          in_=xT[ec * 128:(ec + 1) * 128, T * c.TT:(T + 1) * c.TT]),
                      f"swx{bi}", wait=freed[f"xres{bi}"])
                sw_x = (f"swx{bi}", P.now(f"swx{bi}"))
                bank = ec % 4
                for dc in range(c.ND):
                    P.emit("tensor",
                           lambda e, dc=dc, bi=bi, bank=bank, T=T: e.matmul(
                               pb[bank][:, 0:c.TT],
                               woutb[bi][:, dc * 128:(dc + 1) * 128],
                               attnTf[:, dc * c.TOKC + T * c.TT:
                                      dc * c.TOKC + (T + 1) * c.TT],
                               start=(dc == 0), stop=(dc == c.ND - 1)),
                           wait=([gw_w, ("v", v_attnT)] + freed[f"pb{bank}"])
                           if dc == 0 else [],
                           inc=[("t", 1)] if dc == c.ND - 1 else [])
                t_mm = P.now("t")
                freed[f"woutb{bi}"] = [("t", t_mm)]
                P.emit("vector",
                       lambda e, ec=ec, bank=bank, bi=bi: e.scalar_tensor_tensor(
                           preln[:, ec * c.TT:(ec + 1) * c.TT], pb[bank][:, 0:c.TT],
                           bout_s[:, ec:ec + 1], xres[bi][:, 0:c.TT],
                           ALU.add, ALU.add),
                       wait=[("t", t_mm), sw_x, ("swc", sw_consts)] + dwar["preln"],
                       inc=[("v", 1)])
                freed[f"pb{bank}"] = [("v", P.now("v"))]
                freed[f"xres{bi}"] = [("v", P.now("v"))]
                preln_marks[ec] = P.now("v")

            if debug and T == 0:
                P.emit("gpsimd",
                       lambda e: e.dma_start(out=dbg["dbg_preln"][:, :],
                                             in_=preln[:, 0:c.ND * c.TT]),
                       wait=[("v", max(preln_marks.values())), ("gwd", P.now("gwd"))], inc=[("gwd", 16)])
                dwar["preln"] = [("gwd", P.now("gwd"))]

            def dst_x1(ec, T=T):
                P.emit("vector",
                       lambda e, ec=ec, T=T: e.tensor_mul(
                           x1b[:, ec * c.TOKC + T * c.TT: ec * c.TOKC + (T + 1) * c.TT],
                           tmp_n[:, 0:c.TT], rb_s[:, 0:c.TT]),
                       wait=[("v", P.now("v"))] + dwar["x1b"], inc=[("v", 1)])
            emit_ln(preln_marks, dst_x1)
        v_x1_done = P.now("v")

        if debug:
            P.emit("gpsimd",
                   lambda e: e.dma_start(out=dbg["dbg_x1b"][:, :], in_=x1b[:, :]),
                   wait=[("v", v_x1_done), ("gwd", P.now("gwd"))], inc=[("gwd", 16)])

        nslotA = (c.NH * c.TOKC) // c.TT

        def fuse_slice(oc):
            if oc < nslotA:
                return bigA[:, oc * c.TT:(oc + 1) * c.TT]
            oc2 = oc - nslotA
            return attnTf[:, oc2 * c.TT:(oc2 + 1) * c.TT]

        for T in range(c.NTILES):
            fuse_last = 0
            for oc in range(c.NO):
                bi = oc % 2
                sn = f"gw{'ab'[bi]}"
                P.dma("gpsimd",
                      lambda e, oc=oc, bi=bi: e.dma_start(
                          out=wgub[bi][:, :], in_=wgu[oc]),
                      sn, wait=freed[f"wgub{bi}"] + lnscratch["w"])
                gw_w = (sn, P.now(sn))
                bk_g = (oc % 2) * 2
                bk_u = bk_g + 1
                for which, bank in ((0, bk_g), (1, bk_u)):
                    for dc in range(c.ND):
                        P.emit("tensor",
                               lambda e, bi=bi, which=which, dc=dc, bank=bank, T=T: e.matmul(
                                   pb[bank][:, 0:c.TT],
                                   wgub[bi][:, which * c.ND * 128 + dc * 128:
                                             which * c.ND * 128 + (dc + 1) * 128],
                                   x1b[:, dc * c.TOKC + T * c.TT:
                                       dc * c.TOKC + (T + 1) * c.TT],
                                   start=(dc == 0), stop=(dc == c.ND - 1)),
                               wait=([gw_w, ("v", v_x1_done)] + freed[f"pb{bank}"])
                               if dc == 0 else [],
                               inc=[("t", 1)] if dc == c.ND - 1 else [])
                t_gu = P.now("t")
                freed[f"wgub{bi}"] = [("t", t_gu)]
                gi = oc % 2
                P.emit("scalar",
                       lambda e, oc=oc, bank=bk_g, gi=gi: e.activation(
                           gact[gi][:, 0:c.TT], pb[bank][:, 0:c.TT], AF.Silu,
                           scale=sg_s[:, oc:oc + 1]),
                       wait=[("t", t_gu), ("swc", sw_consts)] + freed[f"gact{gi}"],
                       inc=[("s", 1)])
                s_gact = P.now("s")
                freed[f"pb{bk_g}"] = [("s", s_gact)]
                P.emit("vector",
                       lambda e, oc=oc, bank=bk_u, gi=gi: e.scalar_tensor_tensor(
                           fuse_slice(oc), pb[bank][:, 0:c.TT], su_s[:, oc:oc + 1],
                           gact[gi][:, 0:c.TT], ALU.mult, ALU.mult),
                       wait=[("s", s_gact)] + dwar["fuse"], inc=[("v", 1)])
                v_f = P.now("v")
                freed[f"pb{bk_u}"] = [("v", v_f)]
                freed[f"gact{gi}"] = [("v", v_f)]
                fuse_last = v_f
            v_fuse_all = fuse_last

            preln_marks = {}
            for ec in range(c.ND):
                bi = ec % 2
                sn = f"gw{'ab'[bi]}"
                P.dma("gpsimd",
                      lambda e, ec=ec, bi=bi: e.dma_start(
                          out=wdb[bi][:, :], in_=wdt[ec]),
                      sn, wait=freed[f"wdb{bi}"] + lnscratch["w"])
                gw_w = (sn, P.now(sn))
                bank = ec % 4
                for oc in range(c.NO):
                    P.emit("tensor",
                           lambda e, oc=oc, bi=bi, bank=bank: e.matmul(
                               pb[bank][:, 0:c.TT],
                               wdb[bi][:, oc * 128:(oc + 1) * 128],
                               fuse_slice(oc),
                               start=(oc == 0), stop=(oc == c.NO - 1)),
                           wait=([gw_w, ("v", v_fuse_all)] + freed[f"pb{bank}"])
                           if oc == 0 else [],
                           inc=[("t", 1)] if oc == c.NO - 1 else [])
                t_mm = P.now("t")
                freed[f"wdb{bi}"] = [("t", t_mm)]
                P.emit("vector",
                       lambda e, ec=ec, bank=bank, T=T: e.scalar_tensor_tensor(
                           preln[:, ec * c.TT:(ec + 1) * c.TT], pb[bank][:, 0:c.TT],
                           sd_s[:, ec:ec + 1],
                           x1b[:, ec * c.TOKC + T * c.TT: ec * c.TOKC + (T + 1) * c.TT],
                           ALU.mult, ALU.add),
                       wait=[("t", t_mm)] + dwar["preln"], inc=[("v", 1)])
                freed[f"pb{bank}"] = [("v", P.now("v"))]
                preln_marks[ec] = P.now("v")

            def dst_y(ec, T=T):
                yi = ec % 2
                P.emit("vector",
                       lambda e, yi=yi: e.tensor_mul(
                           ystage[yi][:, 0:c.TT], tmp_n[:, 0:c.TT], rb_s[:, 0:c.TT]),
                       wait=[("v", P.now("v"))] + freed[f"ystage{yi}"], inc=[("v", 1)])
                v_y = P.now("v")
                P.dma("sync",
                      lambda e, ec=ec, yi=yi, T=T: e.dma_start(
                          out=yT[ec * 128:(ec + 1) * 128, T * c.TT:(T + 1) * c.TT],
                          in_=ystage[yi][:, 0:c.TT]),
                      f"swy{yi}", wait=[("v", v_y)])
                freed[f"ystage{yi}"] = [(f"swy{yi}", P.now(f"swy{yi}"))]
            emit_ln(preln_marks, dst_y)

        P.emit("sync", None, wait=[("swy0", P.now("swy0")), ("swy1", P.now("swy1"))])
        P.emit("gpsimd", None, wait=[("gwd", P.now("gwd")), ("gwa", P.now("gwa")),
                                     ("gwb", P.now("gwb"))])
        P.finalize(blk, sems)
    return nc



def prep_shared(cfg, wqkv, b_qkv, w_out, b_out, w_g, w_u, w_d, s_g, s_u, s_d):
    c = cfg
    f32 = np.float32
    sh = {}
    sh["wqkvT"] = np.ascontiguousarray(
        wqkv.reshape(c.NE3, 128, c.ND, 128).transpose(0, 3, 2, 1)).astype(f32)
    sh["woutT"] = np.ascontiguousarray(
        w_out.reshape(c.ND, 128, c.ND, 128).transpose(0, 3, 2, 1)).astype(f32)
    wgt = w_g.reshape(c.ND, 128, c.NO, 128).transpose(2, 1, 0, 3)
    wut = w_u.reshape(c.ND, 128, c.NO, 128).transpose(2, 1, 0, 3)
    sh["wgu"] = np.ascontiguousarray(
        np.concatenate([wgt.reshape(c.NO, 128, 1, c.ND * 128),
                        wut.reshape(c.NO, 128, 1, c.ND * 128)], axis=2)
        .reshape(c.NO, 128, 2 * c.ND * 128)).astype(np.int8)
    sh["wdt"] = np.ascontiguousarray(
        w_d.reshape(c.NO, 128, c.ND, 128).transpose(2, 1, 0, 3)
        .reshape(c.ND, 128, c.NO * 128)).astype(np.int8)
    sh["bqkv"] = np.ascontiguousarray(b_qkv.reshape(c.NE3, 128).T).astype(f32)
    sh["bout"] = np.ascontiguousarray(b_out.reshape(c.ND, 128).T).astype(f32)
    sh["sg"] = np.ascontiguousarray(s_g.reshape(c.NO, 128).T).astype(f32)
    sh["su"] = np.ascontiguousarray(s_u.reshape(c.NO, 128).T).astype(f32)
    sh["sd"] = np.ascontiguousarray(s_d.reshape(c.ND, 128).T).astype(f32)
    sh["identc"] = np.eye(128).astype(ml_dtypes.bfloat16)
    sh["identfc"] = np.eye(128, dtype=np.float32)
    return sh


def core_token_index(cfg, j):
    c = cfg
    idx = []
    for qt in range(c.NQT):
        base = 512 * qt + 128 * j
        idx.extend(range(base, base + 128))
    return np.asarray(idx)


def core_mask(j):
    q = np.arange(128)[:, None]
    k = np.arange(512)[None, :]
    m = np.where(k <= 128 * j + q, 0.0, NEG).astype(np.float32)
    return np.ascontiguousarray(m.astype(ml_dtypes.bfloat16))


def make_in_maps(cfg, x, sh):
    in_maps, idxs = [], []
    for core in range(NCORES):
        b, j = core // 4, core % 4
        idx = core_token_index(cfg, j)
        idxs.append(idx)
        m = dict(sh)
        m["xT"] = np.ascontiguousarray(x[b].T[:, idx])
        m["maskc"] = core_mask(j)
        in_maps.append(m)
    return in_maps, idxs


def kernel(L0_attn_x, L0_attn_w_qkv, L0_attn_b_qkv, L0_attn_w_out, L0_attn_b_out,
           L0_ffn_w_gate, L0_ffn_w_up, L0_ffn_w_down, L0_ffn_s_gate, L0_ffn_s_up,
           L0_ffn_s_down):
    cfg = Cfg()
    x = np.asarray(L0_attn_x, np.float32)
    sh = prep_shared(cfg, np.asarray(L0_attn_w_qkv, np.float32),
                     np.asarray(L0_attn_b_qkv, np.float32),
                     np.asarray(L0_attn_w_out, np.float32),
                     np.asarray(L0_attn_b_out, np.float32),
                     np.asarray(L0_ffn_w_gate), np.asarray(L0_ffn_w_up),
                     np.asarray(L0_ffn_w_down),
                     np.asarray(L0_ffn_s_gate, np.float32),
                     np.asarray(L0_ffn_s_up, np.float32),
                     np.asarray(L0_ffn_s_down, np.float32))
    in_maps, idxs = make_in_maps(cfg, x, sh)
    nc = build_nc(cfg, debug=False)
    res = run_bass_kernel_spmd(nc, in_maps, core_ids=list(range(NCORES)))
    out = np.empty((cfg.B, cfg.S, cfg.D), np.float32)
    for core in range(NCORES):
        b = core // 4
        out[b, idxs[core], :] = np.asarray(res.results[core]["yT"]).T
    return out


# version 07
# speedup vs baseline: 1.0954x; 1.0434x over previous
import math
import contextlib
import numpy as np
import ml_dtypes

import concourse.bass as bass
import concourse.mybir as mybir
from concourse.bass_utils import run_bass_kernel_spmd

F32 = mybir.dt.float32
BF16 = mybir.dt.bfloat16
I8 = mybir.dt.int8
AF = mybir.ActivationFunctionType
ALU = mybir.AluOpType
AX = mybir.AxisListType

NCORES = 8
NEG = -1e30
RS128 = 1.0 / math.sqrt(128.0)


class Cfg:
    def __init__(self, D=2048, S=4096, H=16, INTER=8192, Bsz=2):
        self.D, self.S, self.H, self.INTER, self.B = D, S, H, INTER, Bsz
        self.ND = D // 128
        self.NH = H
        self.NE3 = 3 * D // 128
        self.NO = INTER // 128
        self.TOKC = Bsz * S // NCORES
        self.NQT = self.TOKC // 128
        assert self.NQT == S // 512
        self.TT = min(512, self.TOKC)
        self.NTILES = self.TOKC // self.TT
        assert H == self.ND, "head_dim must be 128"


ENGINES = ("tensor", "vector", "scalar", "gpsimd", "sync")
SEMNAMES = ("t", "v", "s", "g", "cc", "gwx", "gwa", "gwb", "gwd",
            "swc", "swkv0", "swkv1", "swk0", "swk1", "swv",
            "swx0", "swx1", "swy0", "swy1")


class Prog:

    def __init__(self):
        self.cnt = {n: 0 for n in SEMNAMES}
        self.ops = {e: [] for e in ENGINES}

    def emit(self, eng, fn=None, wait=(), inc=()):
        w = [(sn, val) for (sn, val) in wait if val > 0]
        inc = list(inc)
        for sn, k in inc:
            self.cnt[sn] += k
        self.ops[eng].append((fn, w, inc))

    def now(self, sn):
        return self.cnt[sn]

    def dma(self, eng, fn, sem, wait=()):
        self.emit(eng, fn, wait=list(wait) + [(sem, self.now(sem))],
                  inc=[(sem, 16)])
        return self.now(sem)

    def finalize(self, block, semh):
        for ename in ENGINES:
            ops = self.ops[ename]

            def body(eng, ops=ops):
                lastw = {}
                for fn, waits, incs in ops:
                    for sn, val in waits:
                        if lastw.get(sn, 0) < val:
                            eng.wait_ge(semh[sn], val)
                            lastw[sn] = val
                    ins = fn(eng) if fn is not None else None
                    for sn, k in incs:
                        ins.then_inc(semh[sn], k)
            getattr(block, ename)(body)


def build_nc(cfg: Cfg, debug=False):
    c = cfg
    nc = bass.Bass("TRN2", target_bir_lowering=False)

    xT = nc.dram_tensor("xT", [c.D, c.TOKC], F32, kind="ExternalInput").ap()
    wqkvT = nc.dram_tensor("wqkvT", [c.NE3, 128, c.ND, 128], F32, kind="ExternalInput").ap()
    woutT = nc.dram_tensor("woutT", [c.ND, 128, c.ND, 128], F32, kind="ExternalInput").ap()
    wgu = nc.dram_tensor("wgu", [c.NO, 128, 2 * c.ND * 128], I8, kind="ExternalInput").ap()
    wdt = nc.dram_tensor("wdt", [c.ND, 128, c.NO * 128], I8, kind="ExternalInput").ap()
    bqkv = nc.dram_tensor("bqkv", [128, c.NE3], F32, kind="ExternalInput").ap()
    bout = nc.dram_tensor("bout", [128, c.ND], F32, kind="ExternalInput").ap()
    sg = nc.dram_tensor("sg", [128, c.NO], F32, kind="ExternalInput").ap()
    su = nc.dram_tensor("su", [128, c.NO], F32, kind="ExternalInput").ap()
    sd = nc.dram_tensor("sd", [128, c.ND], F32, kind="ExternalInput").ap()
    maskc = nc.dram_tensor("maskc", [128, 512], BF16, kind="ExternalInput").ap()
    identc = nc.dram_tensor("identc", [128, 128], BF16, kind="ExternalInput").ap()
    identfc = nc.dram_tensor("identfc", [128, 128], F32, kind="ExternalInput").ap()
    yT = nc.dram_tensor("yT", [c.D, c.TOKC], F32, kind="ExternalOutput").ap()

    kv_own = nc.dram_tensor("kv_own", [2 * c.D, c.TOKC], BF16)
    kv_gather = nc.dram_tensor("kv_gather", [4 * 2 * c.D, c.TOKC], BF16)

    dbg = {}
    if debug:
        for nm, shp in (("dbg_qt", [128, c.NH * c.TOKC]),
                        ("dbg_kvg", [4 * 2 * c.D, c.TOKC]),
                        ("dbg_attn", [128, c.NQT * c.D]),
                        ("dbg_attnT", [128, c.ND * c.TOKC]),
                        ("dbg_preln", [128, c.ND * c.TT]),
                        ("dbg_x1b", [128, c.ND * c.TOKC])):
            dbg[nm] = nc.dram_tensor(nm, shp, F32, kind="ExternalOutput").ap()

    P = Prog()

    with contextlib.ExitStack() as ctx:
        E = ctx.enter_context

        QTx1b = E(nc.sbuf_tensor("QTx1b", [128, c.NH * c.TOKC], BF16))
        bigA = E(nc.sbuf_tensor("bigA", [128, c.NH * c.TOKC], BF16))
        attnTf = E(nc.sbuf_tensor("attnTf", [128, c.ND * c.TOKC], BF16))

        VW = (c.S // 128) * 132
        KVC = max(3 * c.S + 2 * VW, 2 * c.ND * c.TT + 2 * c.ND * 128 + 8 * c.TT)
        kvar = E(nc.sbuf_tensor("kvar", [128, KVC], BF16))
        kh = [kvar[:, i * c.S:(i + 1) * c.S] for i in range(2)]
        vtT = kvar[:, 2 * c.S:3 * c.S]
        vtok = [kvar[:, 3 * c.S + i * VW: 3 * c.S + (i + 1) * VW] for i in range(2)]
        preln = kvar[:, 0:2 * c.ND * c.TT].bitcast(F32)
        wo0 = 2 * c.ND * c.TT
        woutb = [kvar[:, wo0 + i * c.ND * 128: wo0 + (i + 1) * c.ND * 128]
                 for i in range(2)]
        xr0 = wo0 + 2 * c.ND * 128
        xres = [kvar[:, xr0 + i * 2 * c.TT: xr0 + (i + 1) * 2 * c.TT].bitcast(F32)
                for i in range(2)]
        gact = [kvar[:, wo0 + i * 2 * c.TT: wo0 + (i + 1) * 2 * c.TT].bitcast(F32)
                for i in range(2)]
        ys0 = wo0 + 4 * c.TT
        ystage = [kvar[:, ys0 + i * 2 * c.TT: ys0 + (i + 1) * 2 * c.TT].bitcast(F32)
                  for i in range(2)]
        assert ys0 + 4 * c.TT <= KVC

        SPC = max(5 * c.S, 2 * c.NO * 128 + 4 * c.ND * 128,
                  2 * c.ND * 128 + 2 * c.TOKC + 4 * 512 + 3 * 1024)
        spar = E(nc.sbuf_tensor("spar", [128, SPC], BF16))
        Sbf = spar[:, 0:c.S]
        PTb = [spar[:, (2 + i) * c.S:(3 + i) * c.S] for i in range(2)]
        wqkvb = [spar[:, i * c.ND * 128:(i + 1) * c.ND * 128] for i in range(2)]
        kvs0 = 2 * c.ND * 128
        kvstage = [spar[:, kvs0 + i * c.TOKC: kvs0 + (i + 1) * c.TOKC] for i in range(2)]
        assert kvs0 + 2 * c.TOKC <= 2 * c.S
        wdb = [spar[:, i * c.NO * 128:(i + 1) * c.NO * 128] for i in range(2)]
        wg0 = 2 * c.NO * 128
        wgub = [spar[:, wg0 + i * 2 * c.ND * 128: wg0 + (i + 1) * 2 * c.ND * 128]
                for i in range(2)]
        assert wg0 + 4 * c.ND * 128 <= SPC

        bqkv_s = E(nc.sbuf_tensor("bqkv_s", [128, c.NE3], F32))
        sbq_s = E(nc.sbuf_tensor("sbq_s", [128, c.ND], F32))
        bout_s = E(nc.sbuf_tensor("bout_s", [128, c.ND], F32))
        sg_s = E(nc.sbuf_tensor("sg_s", [128, c.NO], F32))
        su_s = E(nc.sbuf_tensor("su_s", [128, c.NO], F32))
        sd_s = E(nc.sbuf_tensor("sd_s", [128, c.ND], F32))
        maskb = E(nc.sbuf_tensor("maskb", [128, 512], BF16))
        identb = E(nc.sbuf_tensor("identb", [128, 128], BF16))
        identf = E(nc.sbuf_tensor("identf", [128, 128], F32))
        onesb = E(nc.sbuf_tensor("onesb", [128, 128], BF16))
        onesf = E(nc.sbuf_tensor("onesf", [1, 128], F32))
        negmax = E(nc.sbuf_tensor("negmax", [128, 2], F32))
        mcols = E(nc.sbuf_tensor("mcols", [128, 8], F32))
        kmax = E(nc.sbuf_tensor("kmax", [128, 2], F32))
        tmn = E(nc.sbuf_tensor("tmn", [1, 2], F32))
        sumv = E(nc.sbuf_tensor("sumv", [128, 2], F32))
        rr = E(nc.sbuf_tensor("rr", [128, 2], F32))
        ln0 = kvs0 + 2 * c.TOKC
        zb = [spar[:, ln0 + i * 512: ln0 + (i + 1) * 512] for i in range(2)]
        sqb = [spar[:, ln0 + (2 + i) * 512: ln0 + (3 + i) * 512] for i in range(2)]
        mrt0 = ln0 + 4 * 512
        mb_s = spar[:, mrt0: mrt0 + 1024].bitcast(F32)
        rb_s = spar[:, mrt0 + 1024: mrt0 + 2048].bitcast(F32)
        tmp_n = spar[:, mrt0 + 2048: mrt0 + 3072].bitcast(F32)
        assert mrt0 + 3072 <= SPC
        stat_a = E(nc.sbuf_tensor("stat_a", [1, 512], F32))
        stat_b = E(nc.sbuf_tensor("stat_b", [1, 512], F32))
        stat_c = E(nc.sbuf_tensor("stat_c", [1, 512], F32))

        pb = [E(nc.psum_tensor(f"pb{i}", [128, 512], F32)) for i in range(7)]
        tb = [E(nc.psum_tensor("tbb0", [128, 1024], BF16))]

        sems = {n: E(nc.semaphore(n)) for n in SEMNAMES}
        blk = E(nc.Block())

        dwar = {"x1b": [], "fuse": [], "preln": []}
        lnscratch = {"w": []}
        freed = {}
        for nm in ("pb0", "pb1", "pb2", "pb3", "pb4", "pb5", "pb6", "tb0",
                   "wqkvb0", "wqkvb1", "kvstage0", "kvstage1",
                   "kh0", "kh1", "vtT", "vtok0", "vtok1",
                   "Ssb", "negmax0", "negmax1", "PTb0", "PTb1", "kmax0", "kmax1",
                   "woutb0", "woutb1", "xres0", "xres1",
                   "zb0", "zb1", "sqb0", "sqb1", "mbrb",
                   "wgub0", "wgub1", "wdb0", "wdb1", "gact0", "gact1",
                   "ystage0", "ystage1"):
            freed[nm] = []

        for dst, src in ((bqkv_s, bqkv), (bout_s, bout), (sg_s, sg), (su_s, su),
                         (sd_s, sd), (maskb, maskc), (identb, identc),
                         (identf, identfc)):
            P.emit("sync", lambda e, d=dst, s=src: e.dma_start(out=d[:, :], in_=s[:, :]),
                   inc=[("swc", 16)])
        sw_consts = P.now("swc")
        P.emit("vector", lambda e: e.memset(onesb[:, :], 1.0), inc=[("v", 1)])
        P.emit("vector", lambda e: e.memset(onesf[:, :], 1.0), inc=[("v", 1)])
        for i in range(2):
            P.emit("vector",
                   lambda e, i=i: e.memset(
                       vtok[i].rearrange("p (kc w) -> p kc w", w=132)[:, :, 128:132],
                       1.0),
                   inc=[("v", 1)])
        P.emit("vector", lambda e: e.tensor_scalar_mul(sbq_s[:, :], bqkv_s[:, 0:c.ND],
                                                       RS128),
               wait=[("swc", sw_consts)], inc=[("v", 1)])
        v_setup = P.now("v")

        xbf = bigA
        for dc in range(c.ND):
            P.emit("gpsimd",
                   lambda e, dc=dc: e.dma_start(
                       out=xbf[:, dc * c.TOKC:(dc + 1) * c.TOKC],
                       in_=xT[dc * 128:(dc + 1) * 128, :]),
                   inc=[("gwx", 16)])
        gw_xbf = P.now("gwx")

        nhalves = max(1, c.TOKC // 512)
        width = min(512, c.TOKC)
        ec_order = list(range(c.ND, c.NE3)) + list(range(c.ND))
        for eci, ec in enumerate(ec_order):
            if ec == 0:
                P.emit("gpsimd",
                       lambda e: e.collective_compute(
                           "AllGather", ALU.bypass,
                           replica_groups=[[0, 1, 2, 3], [4, 5, 6, 7]],
                           ins=[kv_own.ap().opt()], outs=[kv_gather.ap().opt()]),
                       wait=[("swkv0", P.now("swkv0")), ("swkv1", P.now("swkv1"))],
                       inc=[("cc", 1)])
            bi = eci % 2
            sn = f"gw{'ab'[bi]}"
            P.dma("gpsimd",
                  lambda e, bi=bi, ec=ec: e.dma_start(
                      out=wqkvb[bi][:, :],
                      in_=wqkvT[ec].rearrange("p d e -> p (d e)")),
                  sn, wait=freed[f"wqkvb{bi}"])
            gw_w = (sn, P.now(sn))

            is_q = ec < c.ND
            is_k = c.ND <= ec < 2 * c.ND
            mm_done = []
            for n in range(nhalves):
                bank = 2 * bi + (n % 2)
                for dc in range(c.ND):
                    P.emit("tensor",
                           lambda e, bi=bi, dc=dc, n=n, bank=bank: e.matmul(
                               pb[bank][:, 0:width],
                               wqkvb[bi][:, dc * 128:(dc + 1) * 128],
                               xbf[:, dc * c.TOKC + n * 512: dc * c.TOKC + n * 512 + width],
                               start=(dc == 0), stop=(dc == c.ND - 1)),
                           wait=([gw_w, ("gwx", gw_xbf)] + freed[f"pb{bank}"])
                           if dc == 0 else [],
                           inc=[("t", 1)] if dc == c.ND - 1 else [])
                mm_done.append((bank, n, P.now("t")))
            freed[f"wqkvb{bi}"] = [("t", P.now("t"))]

            if is_q:
                for bank, n, t_mm in mm_done:
                    P.emit("scalar",
                           lambda e, bank=bank, n=n, ec=ec: e.activation(
                               QTx1b[:, ec * c.TOKC + n * 512: ec * c.TOKC + n * 512 + width],
                               pb[bank][:, 0:width], AF.Identity,
                               bias=sbq_s[:, ec:ec + 1], scale=RS128),
                           wait=[("t", t_mm), ("v", v_setup)], inc=[("s", 1)])
                    freed[f"pb{bank}"] = [("s", P.now("s"))]
            else:
                kvrow = (ec - c.ND) * 128 if is_k else c.D + (ec - 2 * c.ND) * 128
                for bank, n, t_mm in mm_done:
                    P.emit("scalar",
                           lambda e, bank=bank, n=n, bi=bi, ec=ec: e.activation(
                               kvstage[bi][:, n * 512: n * 512 + width],
                               pb[bank][:, 0:width], AF.Identity,
                               bias=bqkv_s[:, ec:ec + 1]),
                           wait=[("t", t_mm), ("swc", sw_consts)] + freed[f"kvstage{bi}"],
                           inc=[("s", 1)])
                    freed[f"pb{bank}"] = [("s", P.now("s"))]
                s_evac = P.now("s")
                P.dma("sync",
                      lambda e, bi=bi, kvrow=kvrow: e.dma_start(
                          out=kv_own[kvrow:kvrow + 128, :], in_=kvstage[bi][:, :]),
                      f"swkv{bi}", wait=[("s", s_evac)])
                freed[f"kvstage{bi}"] = [(f"swkv{bi}", P.now(f"swkv{bi}"))]
        s_qkv_done = P.now("s")

        if debug:
            P.emit("gpsimd",
                   lambda e: e.dma_start(out=dbg["dbg_kvg"][:, :], in_=kv_gather.ap()),
                   wait=[("cc", 1), ("gwd", P.now("gwd"))], inc=[("gwd", 16)])
            P.emit("gpsimd",
                   lambda e: e.dma_start(out=dbg["dbg_qt"][:, :], in_=QTx1b[:, :]),
                   wait=[("s", s_qkv_done), ("gwd", P.now("gwd"))], inc=[("gwd", 16)])
            dwar["x1b"] = [("gwd", P.now("gwd"))]

        attn_out = bigA
        kvg4 = kv_gather.ap().rearrange("(r f) (sb tt) -> f sb r tt", r=4, tt=128)
        nkc_all = c.S // 128
        SCB = (0, 1, 2, 3, 5)
        vfm = [0]
        vstate = {"v_vtok0": 0, "v_vtok1": 0}

        for h in range(c.NH):
            hb = h % 2
            P.dma("sync",
                  lambda e, h=h, hb=hb: e.dma_start(
                      out=kh[hb][:, 0:c.S].rearrange("p (sb r tt) -> p sb r tt",
                                                     r=4, tt=128),
                      in_=kvg4[h * 128:(h + 1) * 128]),
                  f"swk{hb}", wait=[("cc", 1)] + freed[f"kh{hb}"])
            sw_kh = (f"swk{hb}", P.now(f"swk{hb}"))
            if h == 0:
                P.dma("sync",
                      lambda e: e.dma_start(
                          out=vtT[:, 0:c.S].rearrange("p (sb r tt) -> p sb r tt",
                                                      r=4, tt=128),
                          in_=kvg4[c.D: c.D + 128]),
                      "swv", wait=[("cc", 1)] + freed["vtT"])
            sw_vt = P.now("swv")

            if h == 0:
                for kc8 in range((nkc_all + 7) // 8):
                    n_in = min(8, nkc_all - kc8 * 8)
                    for i in range(n_in):
                        kc = kc8 * 8 + i
                        P.emit("tensor",
                               lambda e, kc=kc, i=i: e.transpose(
                                   tb[0][:, i * 128:(i + 1) * 128],
                                   vtT[:, kc * 128:(kc + 1) * 128], identb[:, :]),
                               wait=([("swv", sw_vt), ("swc", sw_consts)]
                                     + freed["tb0"] + freed["vtok0"])
                               if i == 0 else [],
                               inc=[("t", 1)] if i == n_in - 1 else [])
                    t_tr = P.now("t")
                    P.emit("vector",
                           lambda e, kc8=kc8, n_in=n_in: e.tensor_copy(
                               vtok[0][:, kc8 * 8 * 132:]
                               .rearrange("p (kc w) -> p kc w", w=132)[:, 0:n_in, 0:128],
                               tb[0][:, 0:n_in * 128]
                               .rearrange("p (kc w) -> p kc w", w=128)),
                           wait=[("t", t_tr)], inc=[("v", 1)])
                    freed["tb0"] = [("v", P.now("v"))]
                freed["vtT"] = [("t", P.now("t"))]
                vstate["v_vtok0"] = P.now("v")
            if h + 1 < c.NH:
                P.dma("sync",
                      lambda e, h=h: e.dma_start(
                          out=vtT[:, 0:c.S].rearrange("p (sb r tt) -> p sb r tt",
                                                      r=4, tt=128),
                          in_=kvg4[c.D + (h + 1) * 128: c.D + (h + 2) * 128]),
                      "swv", wait=freed["vtT"])
            v_vtok = vstate[f"v_vtok{h % 2}"]

            def emit_vspread(hn, qt, sw_vt=None):
                hnb = hn % 2
                for i in range(4):
                    kc = qt * 4 + i
                    P.emit("tensor",
                           lambda e, kc=kc, qt=qt, i=i: e.transpose(
                               tb[0][:, (qt % 2) * 512 + i * 128:
                                      (qt % 2) * 512 + (i + 1) * 128],
                               vtT[:, kc * 128:(kc + 1) * 128], identb[:, :]),
                           wait=([("swv", P.now("swv"))] + freed["tb0"]
                                 + (freed[f"vtok{hnb}"] if kc == 0 else []))
                           if i == 0 else [],
                           inc=[("t", 1)] if i == 3 else [])
                if qt % 2 == 1:
                    t_tr = P.now("t")
                    kc8 = (qt - 1) * 4
                    P.emit("vector",
                           lambda e, kc8=kc8, hnb=hnb: e.tensor_copy(
                               vtok[hnb][:, kc8 * 132:]
                               .rearrange("p (kc w) -> p kc w", w=132)[:, 0:8, 0:128],
                               tb[0][:, 0:1024]
                               .rearrange("p (kc w) -> p kc w", w=128)),
                           wait=[("t", t_tr)], inc=[("v", 1)])
                    freed["tb0"] = [("v", P.now("v"))]
                if qt == c.NQT - 1:
                    freed["vtT"] = [("t", P.now("t"))]
                    vstate[f"v_vtok{hnb}"] = P.now("v")

            pend = {}
            pend_g = {}
            pend_k = {}
            pend_c = {}

            def emit_scores(qt, h=h, hb=hb, sw_kh=sw_kh):
                qtp = qt % 2
                ngr = qt + 1
                spills = max(0, ngr - len(SCB))
                gbank = {}
                for g in range(ngr):
                    bank = SCB[g % len(SCB)]
                    gbank[g] = bank
                    for j4 in range(4):
                        kc = g * 4 + j4
                        diag = (g == qt)
                        P.emit("tensor",
                               lambda e, kc=kc, j4=j4, bank=bank, diag=diag: e.matmul(
                                   pb[bank][:, j4 * 128:(j4 + 1) * 128],
                                   kh[hb][:, kc * 128:(kc + 1) * 128],
                                   QTx1b[:, h * c.TOKC + qt * 128:
                                         h * c.TOKC + (qt + 1) * 128],
                                   start=True, stop=not diag),
                               wait=([sw_kh, ("s", s_qkv_done)] + freed[f"pb{bank}"])
                               if j4 == 0 else [],
                               inc=[("t", 1)] if (not diag and j4 == 3) else [])
                        if diag:
                            P.emit("tensor",
                                   lambda e, j4=j4, bank=bank: e.matmul(
                                       pb[bank][:, j4 * 128:(j4 + 1) * 128],
                                       maskb[:, j4 * 128:(j4 + 1) * 128],
                                       identb[:, :], start=False, stop=True),
                                   inc=[("t", 1)] if j4 == 3 else [])
                    t_g = P.now("t")
                    P.emit("vector",
                           lambda e, g=g, bank=bank: e.tensor_reduce(
                               mcols[:, g:g + 1], pb[bank][:, 0:512], AX.X, ALU.max),
                           wait=[("t", t_g), ("v", vfm[0])], inc=[("v", 1)])
                    v_bm = P.now("v")
                    if g < spills:
                        P.emit("vector",
                               lambda e, g=g, bank=bank: e.tensor_copy(
                                   Sbf[:, g * 512:(g + 1) * 512], pb[bank][:, 0:512]),
                               wait=(freed["Ssb"] if g == 0 else []), inc=[("v", 1)])
                        freed[f"pb{bank}"] = [("v", P.now("v"))]
                pend_g[qt] = (gbank, spills)
                pend_k[qt] = P.now("v")

            def emit_kchain_a(qt):
                qtp = qt % 2
                ngr = qt + 1
                v_bm = pend_k.pop(qt)
                P.emit("vector",
                       lambda e, qtp=qtp, ngr=ngr: e.tensor_reduce(
                           kmax[:, qtp:qtp + 1], mcols[:, 0:ngr], AX.X, ALU.max),
                       wait=[("v", v_bm)] + freed[f"kmax{qtp}"], inc=[("v", 1)])
                v_km = P.now("v")
                vfm[0] = v_km
                P.emit("tensor",
                       lambda e, qtp=qtp: e.transpose(
                           pb[6][0:1, 0:128], kmax[:, qtp:qtp + 1], identf[:, :]),
                       wait=[("v", v_km)] + freed["pb6"], inc=[("t", 1)])
                t_kt = P.now("t")
                P.emit("vector",
                       lambda e, qtp=qtp: e.tensor_reduce(
                           tmn[:, qtp:qtp + 1], pb[6][0:1, 0:128], AX.X, ALU.max,
                           negate=True),
                       wait=[("t", t_kt)], inc=[("v", 1)])
                pend_c[qt] = P.now("v")

            def emit_kchain_b(qt):
                qtp = qt % 2
                ngr = qt + 1
                v_tm = pend_c.pop(qt)
                P.emit("tensor",
                       lambda e, qtp=qtp: e.matmul(
                           pb[6][:, 128:129], onesf[:, :], tmn[:, qtp:qtp + 1],
                           start=True, stop=True),
                       wait=[("v", v_tm)], inc=[("t", 1)])
                t_bc = P.now("t")
                P.emit("scalar",
                       lambda e, qtp=qtp: e.copy(negmax[:, qtp:qtp + 1],
                                                 pb[6][:, 128:129]),
                       wait=[("t", t_bc)] + freed[f"negmax{qtp}"], inc=[("s", 1)])
                s_nm = P.now("s")
                freed["pb6"] = [("s", s_nm)]
                freed[f"kmax{qtp}"] = [("t", t_bc)]
                gbank, spills = pend_g.pop(qt)
                exp_marks = {}
                for g in range(ngr):
                    src_sp = g < spills
                    P.emit("scalar",
                           lambda e, g=g, qtp=qtp, src_sp=src_sp,
                           bank=gbank[g]: e.activation(
                               PTb[qtp][:, g * 512:(g + 1) * 512],
                               (Sbf[:, g * 512:(g + 1) * 512] if src_sp
                                else pb[bank][:, 0:512]), AF.Exp,
                               bias=negmax[:, qtp:qtp + 1]),
                           wait=[("s", s_nm)] + (freed[f"PTb{qtp}"] if g == 0 else []),
                           inc=[("s", 1)])
                    if not src_sp:
                        freed[f"pb{gbank[g]}"] = [("s", P.now("s"))]
                    exp_marks[g] = P.now("s")
                if spills:
                    freed["Ssb"] = [("s", exp_marks[spills - 1])]
                freed[f"negmax{qtp}"] = [("s", exp_marks[ngr - 1])]
                pend[qt] = (exp_marks, P.now("s"))

            def emit_ptpv(qt, h=h, hb=hb, v_vtok=v_vtok):
                qtp = qt % 2
                exp_marks, s_exp_all = pend.pop(qt)
                ngr = qt + 1
                nkc = 4 * ngr
                for kc in range(nkc):
                    w0 = [("s", exp_marks[kc // 4]), ("v", v_vtok)]
                    if kc == 0:
                        w0 = w0 + freed["pb4"]
                    P.emit("tensor",
                           lambda e, kc=kc, nkc=nkc, hb=hb, qtp=qtp: e.matmul(
                               pb[4][:, 0:129],
                               PTb[qtp][:, kc * 128:(kc + 1) * 128],
                               vtok[hb][:, kc * 132: kc * 132 + 129],
                               start=(kc == 0), stop=(kc == nkc - 1)),
                           wait=w0 if kc % 4 == 0 else [],
                           inc=[("t", 1)] if kc == nkc - 1 else [])
                t_pv = P.now("t")
                freed[f"PTb{qtp}"] = [("t", t_pv)]
                P.emit("vector",
                       lambda e, qtp=qtp: e.reciprocal(rr[:, qtp:qtp + 1],
                                                       pb[4][:, 128:129]),
                       wait=[("t", t_pv)], inc=[("v", 1)])
                v_rr = P.now("v")
                P.emit("scalar",
                       lambda e, h=h, qt=qt, qtp=qtp: e.mul(
                           attn_out[:, qt * c.D + h * 128: qt * c.D + (h + 1) * 128],
                           pb[4][:, 0:128], rr[:, qtp:qtp + 1]),
                       wait=[("v", v_rr)], inc=[("s", 1)])
                freed["pb4"] = [("s", P.now("s"))]

            for qt in range(c.NQT + 1):
                if qt < c.NQT:
                    emit_scores(qt)
                if qt >= 1:
                    emit_ptpv(qt - 1)
                if qt < c.NQT:
                    emit_kchain_a(qt)
                    emit_kchain_b(qt)
                if qt < c.NQT and h + 1 < c.NH:
                    emit_vspread(h + 1, qt)
            freed[f"kh{hb}"] = [("t", P.now("t"))]
            freed[f"vtok{hb}"] = [("t", P.now("t"))]
        v_attn_done = max(P.now("v"), 1)
        s_attn_done = P.now("s")

        if debug:
            P.emit("gpsimd",
                   lambda e: e.dma_start(out=dbg["dbg_attn"][:, :],
                                         in_=attn_out[:, 0:c.NQT * c.D]),
                   wait=[("s", s_attn_done), ("gwd", P.now("gwd"))], inc=[("gwd", 16)])
            dwar["fuse"] = [("gwd", P.now("gwd"))]

        for h in range(c.NH):
            for qt in range(c.NQT):
                P.emit("tensor",
                       lambda e, h=h, qt=qt: e.transpose(
                           tb[0][:, qt * 128:(qt + 1) * 128],
                           attn_out[:, qt * c.D + h * 128: qt * c.D + (h + 1) * 128],
                           identb[:, :]),
                       wait=([("s", s_attn_done)] + freed["tb0"]) if qt == 0 else [],
                       inc=[("t", 1)] if qt == c.NQT - 1 else [])
            t_tr = P.now("t")
            P.emit("vector",
                   lambda e, h=h: e.tensor_copy(
                       attnTf[:, h * c.TOKC:(h + 1) * c.TOKC],
                       tb[0][:, 0:c.NQT * 128]),
                   wait=[("t", t_tr)], inc=[("v", 1)])
            freed["tb0"] = [("v", P.now("v"))]
        v_attnT = P.now("v")
        attn_end = [("t", P.now("t")), ("v", P.now("v")), ("s", P.now("s"))]
        for nm in ("woutb0", "woutb1", "xres0", "xres1",
                   "wgub0", "wgub1", "wdb0", "wdb1",
                   "gact0", "gact1", "ystage0", "ystage1"):
            freed[nm] = list(attn_end)
        lnscratch["w"] = list(attn_end)
        freed["pv"] = list(attn_end)
        freed["pb5"] = list(attn_end)

        if debug:
            P.emit("gpsimd",
                   lambda e: e.dma_start(out=dbg["dbg_attnT"][:, :], in_=attnTf[:, :]),
                   wait=[("v", v_attnT), ("gwd", P.now("gwd"))], inc=[("gwd", 16)])
            dwar["fuse"] = dwar["fuse"] + [("gwd", P.now("gwd"))]

        freed["pv"] = freed["pv"]

        def emit_ln(preln_marks, dst_write):
            war = lnscratch["w"] + [("t", P.now("t")),
                                    ("gwa", P.now("gwa")), ("gwb", P.now("gwb"))]
            for ec in range(c.ND):
                zi = ec % 2
                P.emit("gpsimd",
                       lambda e, ec=ec, zi=zi: e.tensor_copy(
                           zb[zi][:, 0:c.TT], preln[:, ec * c.TT:(ec + 1) * c.TT]),
                       wait=[("v", preln_marks[ec])] + freed[f"zb{zi}"] + war,
                       inc=[("g", 1)])
                g_z = P.now("g")
                P.emit("scalar",
                       lambda e, ec=ec, zi=zi: e.activation(
                           sqb[zi][:, 0:c.TT], preln[:, ec * c.TT:(ec + 1) * c.TT],
                           AF.Square),
                       wait=[("v", preln_marks[ec])] + freed[f"sqb{zi}"] + war,
                       inc=[("s", 1)])
                s_q = P.now("s")
                P.emit("tensor",
                       lambda e, ec=ec, zi=zi: e.matmul(
                           pb[4][0:1, 0:c.TT], onesb[:, 0:1], zb[zi][:, 0:c.TT],
                           start=(ec == 0), stop=(ec == c.ND - 1)),
                       wait=[("g", g_z), ("v", v_setup)] + (freed["pv"] if ec == 0 else []),
                       inc=[("t", 1)])
                freed[f"zb{zi}"] = [("t", P.now("t"))]
                P.emit("tensor",
                       lambda e, ec=ec, zi=zi: e.matmul(
                           pb[5][0:1, 0:c.TT], onesb[:, 0:1], sqb[zi][:, 0:c.TT],
                           start=(ec == 0), stop=(ec == c.ND - 1)),
                       wait=[("s", s_q)], inc=[("t", 1)])
                freed[f"sqb{zi}"] = [("t", P.now("t"))]
            t_stats = P.now("t")
            P.emit("vector",
                   lambda e: e.tensor_scalar_mul(stat_a[:, 0:c.TT], pb[4][0:1, 0:c.TT],
                                                 1.0 / c.D),
                   wait=[("t", t_stats)], inc=[("v", 1)])
            P.emit("vector",
                   lambda e: e.tensor_mul(stat_c[:, 0:c.TT], stat_a[:, 0:c.TT],
                                          stat_a[:, 0:c.TT]),
                   wait=[("v", P.now("v"))], inc=[("v", 1)])
            P.emit("vector",
                   lambda e: e.scalar_tensor_tensor(
                       stat_b[:, 0:c.TT], pb[5][0:1, 0:c.TT], 1.0 / c.D,
                       stat_c[:, 0:c.TT], ALU.mult, ALU.subtract),
                   wait=[("v", P.now("v"))], inc=[("v", 1)])
            P.emit("vector",
                   lambda e: e.tensor_scalar_add(stat_b[:, 0:c.TT], stat_b[:, 0:c.TT],
                                                 1e-5),
                   wait=[("v", P.now("v"))], inc=[("v", 1)])
            v_var = P.now("v")
            P.emit("scalar",
                   lambda e: e.activation(stat_c[:, 0:c.TT], stat_b[:, 0:c.TT],
                                          AF.Sqrt),
                   wait=[("v", v_var)], inc=[("s", 1)])
            s_sd = P.now("s")
            P.emit("vector",
                   lambda e: e.reciprocal(stat_b[:, 0:c.TT], stat_c[:, 0:c.TT]),
                   wait=[("s", s_sd)], inc=[("v", 1)])
            v_rstd = P.now("v")
            P.emit("tensor",
                   lambda e: e.matmul(pb[4][:, 0:c.TT], onesf[:, :], stat_a[:, 0:c.TT],
                                      start=True, stop=True),
                   wait=[("v", v_rstd)], inc=[("t", 1)])
            P.emit("tensor",
                   lambda e: e.matmul(pb[5][:, 0:c.TT], onesf[:, :], stat_b[:, 0:c.TT],
                                      start=True, stop=True),
                   inc=[("t", 1)])
            t_bc = P.now("t")
            P.emit("vector", lambda e: e.tensor_copy(mb_s[:, 0:c.TT], pb[4][:, 0:c.TT]),
                   wait=[("t", t_bc)] + freed["mbrb"] + war, inc=[("v", 1)])
            P.emit("vector", lambda e: e.tensor_copy(rb_s[:, 0:c.TT], pb[5][:, 0:c.TT]),
                   inc=[("v", 1)])
            v_mbrb = P.now("v")
            freed["pv"] = [("v", v_mbrb)]
            for ec in range(c.ND):
                P.emit("vector",
                       lambda e, ec=ec: e.tensor_sub(
                           tmp_n[:, 0:c.TT], preln[:, ec * c.TT:(ec + 1) * c.TT],
                           mb_s[:, 0:c.TT]),
                       wait=[("v", max(v_mbrb, P.now("v")))], inc=[("v", 1)])
                dst_write(ec)
            freed["mbrb"] = [("v", P.now("v"))]
            lnscratch["w"] = [("v", P.now("v")), ("t", P.now("t")),
                              ("s", P.now("s")), ("g", P.now("g"))]

        x1b = QTx1b
        for T in range(c.NTILES):
            preln_marks = {}
            for ec in range(c.ND):
                bi = ec % 2
                sn = f"gw{'ab'[bi]}"
                P.dma("gpsimd",
                      lambda e, ec=ec, bi=bi: e.dma_start(
                          out=woutb[bi][:, :],
                          in_=woutT[ec].rearrange("p d e -> p (d e)")),
                      sn, wait=freed[f"woutb{bi}"])
                gw_w = (sn, P.now(sn))
                P.dma("sync",
                      lambda e, ec=ec, bi=bi, T=T: e.dma_start(
                          out=xres[bi][:, 0:c.TT],
                          in_=xT[ec * 128:(ec + 1) * 128, T * c.TT:(T + 1) * c.TT]),
                      f"swx{bi}", wait=freed[f"xres{bi}"])
                sw_x = (f"swx{bi}", P.now(f"swx{bi}"))
                bank = ec % 4
                for dc in range(c.ND):
                    P.emit("tensor",
                           lambda e, dc=dc, bi=bi, bank=bank, T=T: e.matmul(
                               pb[bank][:, 0:c.TT],
                               woutb[bi][:, dc * 128:(dc + 1) * 128],
                               attnTf[:, dc * c.TOKC + T * c.TT:
                                      dc * c.TOKC + (T + 1) * c.TT],
                               start=(dc == 0), stop=(dc == c.ND - 1)),
                           wait=([gw_w, ("v", v_attnT)] + freed[f"pb{bank}"])
                           if dc == 0 else [],
                           inc=[("t", 1)] if dc == c.ND - 1 else [])
                t_mm = P.now("t")
                freed[f"woutb{bi}"] = [("t", t_mm)]
                P.emit("vector",
                       lambda e, ec=ec, bank=bank, bi=bi: e.scalar_tensor_tensor(
                           preln[:, ec * c.TT:(ec + 1) * c.TT], pb[bank][:, 0:c.TT],
                           bout_s[:, ec:ec + 1], xres[bi][:, 0:c.TT],
                           ALU.add, ALU.add),
                       wait=[("t", t_mm), sw_x, ("swc", sw_consts)] + dwar["preln"],
                       inc=[("v", 1)])
                freed[f"pb{bank}"] = [("v", P.now("v"))]
                freed[f"xres{bi}"] = [("v", P.now("v"))]
                preln_marks[ec] = P.now("v")

            if debug and T == 0:
                P.emit("gpsimd",
                       lambda e: e.dma_start(out=dbg["dbg_preln"][:, :],
                                             in_=preln[:, 0:c.ND * c.TT]),
                       wait=[("v", max(preln_marks.values())), ("gwd", P.now("gwd"))], inc=[("gwd", 16)])
                dwar["preln"] = [("gwd", P.now("gwd"))]

            def dst_x1(ec, T=T):
                P.emit("vector",
                       lambda e, ec=ec, T=T: e.tensor_mul(
                           x1b[:, ec * c.TOKC + T * c.TT: ec * c.TOKC + (T + 1) * c.TT],
                           tmp_n[:, 0:c.TT], rb_s[:, 0:c.TT]),
                       wait=[("v", P.now("v"))] + dwar["x1b"], inc=[("v", 1)])
            emit_ln(preln_marks, dst_x1)
        v_x1_done = P.now("v")

        if debug:
            P.emit("gpsimd",
                   lambda e: e.dma_start(out=dbg["dbg_x1b"][:, :], in_=x1b[:, :]),
                   wait=[("v", v_x1_done), ("gwd", P.now("gwd"))], inc=[("gwd", 16)])

        nslotA = (c.NH * c.TOKC) // c.TT

        def fuse_slice(oc):
            if oc < nslotA:
                return bigA[:, oc * c.TT:(oc + 1) * c.TT]
            oc2 = oc - nslotA
            return attnTf[:, oc2 * c.TT:(oc2 + 1) * c.TT]

        for T in range(c.NTILES):
            fuse_last = 0
            for oc in range(c.NO):
                bi = oc % 2
                sn = f"gw{'ab'[bi]}"
                P.dma("gpsimd",
                      lambda e, oc=oc, bi=bi: e.dma_start(
                          out=wgub[bi][:, :], in_=wgu[oc]),
                      sn, wait=freed[f"wgub{bi}"] + lnscratch["w"])
                gw_w = (sn, P.now(sn))
                bk_g = (oc % 2) * 2
                bk_u = bk_g + 1
                for which, bank in ((0, bk_g), (1, bk_u)):
                    for dc in range(c.ND):
                        P.emit("tensor",
                               lambda e, bi=bi, which=which, dc=dc, bank=bank, T=T: e.matmul(
                                   pb[bank][:, 0:c.TT],
                                   wgub[bi][:, which * c.ND * 128 + dc * 128:
                                             which * c.ND * 128 + (dc + 1) * 128],
                                   x1b[:, dc * c.TOKC + T * c.TT:
                                       dc * c.TOKC + (T + 1) * c.TT],
                                   start=(dc == 0), stop=(dc == c.ND - 1)),
                               wait=([gw_w, ("v", v_x1_done)] + freed[f"pb{bank}"])
                               if dc == 0 else [],
                               inc=[("t", 1)] if dc == c.ND - 1 else [])
                t_gu = P.now("t")
                freed[f"wgub{bi}"] = [("t", t_gu)]
                gi = oc % 2
                P.emit("scalar",
                       lambda e, oc=oc, bank=bk_g, gi=gi: e.activation(
                           gact[gi][:, 0:c.TT], pb[bank][:, 0:c.TT], AF.Silu,
                           scale=sg_s[:, oc:oc + 1]),
                       wait=[("t", t_gu), ("swc", sw_consts)] + freed[f"gact{gi}"],
                       inc=[("s", 1)])
                s_gact = P.now("s")
                freed[f"pb{bk_g}"] = [("s", s_gact)]
                P.emit("vector",
                       lambda e, oc=oc, bank=bk_u, gi=gi: e.scalar_tensor_tensor(
                           fuse_slice(oc), pb[bank][:, 0:c.TT], su_s[:, oc:oc + 1],
                           gact[gi][:, 0:c.TT], ALU.mult, ALU.mult),
                       wait=[("s", s_gact)] + dwar["fuse"], inc=[("v", 1)])
                v_f = P.now("v")
                freed[f"pb{bk_u}"] = [("v", v_f)]
                freed[f"gact{gi}"] = [("v", v_f)]
                fuse_last = v_f
            v_fuse_all = fuse_last

            preln_marks = {}
            for ec in range(c.ND):
                bi = ec % 2
                sn = f"gw{'ab'[bi]}"
                P.dma("gpsimd",
                      lambda e, ec=ec, bi=bi: e.dma_start(
                          out=wdb[bi][:, :], in_=wdt[ec]),
                      sn, wait=freed[f"wdb{bi}"] + lnscratch["w"])
                gw_w = (sn, P.now(sn))
                bank = ec % 4
                for oc in range(c.NO):
                    P.emit("tensor",
                           lambda e, oc=oc, bi=bi, bank=bank: e.matmul(
                               pb[bank][:, 0:c.TT],
                               wdb[bi][:, oc * 128:(oc + 1) * 128],
                               fuse_slice(oc),
                               start=(oc == 0), stop=(oc == c.NO - 1)),
                           wait=([gw_w, ("v", v_fuse_all)] + freed[f"pb{bank}"])
                           if oc == 0 else [],
                           inc=[("t", 1)] if oc == c.NO - 1 else [])
                t_mm = P.now("t")
                freed[f"wdb{bi}"] = [("t", t_mm)]
                P.emit("vector",
                       lambda e, ec=ec, bank=bank, T=T: e.scalar_tensor_tensor(
                           preln[:, ec * c.TT:(ec + 1) * c.TT], pb[bank][:, 0:c.TT],
                           sd_s[:, ec:ec + 1],
                           x1b[:, ec * c.TOKC + T * c.TT: ec * c.TOKC + (T + 1) * c.TT],
                           ALU.mult, ALU.add),
                       wait=[("t", t_mm)] + dwar["preln"], inc=[("v", 1)])
                freed[f"pb{bank}"] = [("v", P.now("v"))]
                preln_marks[ec] = P.now("v")

            def dst_y(ec, T=T):
                yi = ec % 2
                P.emit("vector",
                       lambda e, yi=yi: e.tensor_mul(
                           ystage[yi][:, 0:c.TT], tmp_n[:, 0:c.TT], rb_s[:, 0:c.TT]),
                       wait=[("v", P.now("v"))] + freed[f"ystage{yi}"], inc=[("v", 1)])
                v_y = P.now("v")
                P.dma("sync",
                      lambda e, ec=ec, yi=yi, T=T: e.dma_start(
                          out=yT[ec * 128:(ec + 1) * 128, T * c.TT:(T + 1) * c.TT],
                          in_=ystage[yi][:, 0:c.TT]),
                      f"swy{yi}", wait=[("v", v_y)])
                freed[f"ystage{yi}"] = [(f"swy{yi}", P.now(f"swy{yi}"))]
            emit_ln(preln_marks, dst_y)

        P.emit("sync", None, wait=[("swy0", P.now("swy0")), ("swy1", P.now("swy1"))])
        P.emit("gpsimd", None, wait=[("gwd", P.now("gwd")), ("gwa", P.now("gwa")),
                                     ("gwb", P.now("gwb"))])
        P.finalize(blk, sems)
    return nc



def prep_shared(cfg, wqkv, b_qkv, w_out, b_out, w_g, w_u, w_d, s_g, s_u, s_d):
    c = cfg
    f32 = np.float32
    sh = {}
    sh["wqkvT"] = np.ascontiguousarray(
        wqkv.reshape(c.NE3, 128, c.ND, 128).transpose(0, 3, 2, 1)).astype(f32)
    sh["woutT"] = np.ascontiguousarray(
        w_out.reshape(c.ND, 128, c.ND, 128).transpose(0, 3, 2, 1)).astype(f32)
    wgt = w_g.reshape(c.ND, 128, c.NO, 128).transpose(2, 1, 0, 3)
    wut = w_u.reshape(c.ND, 128, c.NO, 128).transpose(2, 1, 0, 3)
    sh["wgu"] = np.ascontiguousarray(
        np.concatenate([wgt.reshape(c.NO, 128, 1, c.ND * 128),
                        wut.reshape(c.NO, 128, 1, c.ND * 128)], axis=2)
        .reshape(c.NO, 128, 2 * c.ND * 128)).astype(np.int8)
    sh["wdt"] = np.ascontiguousarray(
        w_d.reshape(c.NO, 128, c.ND, 128).transpose(2, 1, 0, 3)
        .reshape(c.ND, 128, c.NO * 128)).astype(np.int8)
    sh["bqkv"] = np.ascontiguousarray(b_qkv.reshape(c.NE3, 128).T).astype(f32)
    sh["bout"] = np.ascontiguousarray(b_out.reshape(c.ND, 128).T).astype(f32)
    sh["sg"] = np.ascontiguousarray(s_g.reshape(c.NO, 128).T).astype(f32)
    sh["su"] = np.ascontiguousarray(s_u.reshape(c.NO, 128).T).astype(f32)
    sh["sd"] = np.ascontiguousarray(s_d.reshape(c.ND, 128).T).astype(f32)
    sh["identc"] = np.eye(128).astype(ml_dtypes.bfloat16)
    sh["identfc"] = np.eye(128, dtype=np.float32)
    return sh


def core_token_index(cfg, j):
    c = cfg
    idx = []
    for qt in range(c.NQT):
        base = 512 * qt + 128 * j
        idx.extend(range(base, base + 128))
    return np.asarray(idx)


def core_mask(j):
    q = np.arange(128)[:, None]
    k = np.arange(512)[None, :]
    m = np.where(k <= 128 * j + q, 0.0, NEG).astype(np.float32)
    return np.ascontiguousarray(m.astype(ml_dtypes.bfloat16))


def make_in_maps(cfg, x, sh):
    in_maps, idxs = [], []
    for core in range(NCORES):
        b, j = core // 4, core % 4
        idx = core_token_index(cfg, j)
        idxs.append(idx)
        m = dict(sh)
        m["xT"] = np.ascontiguousarray(x[b].T[:, idx])
        m["maskc"] = core_mask(j)
        in_maps.append(m)
    return in_maps, idxs


def kernel(L0_attn_x, L0_attn_w_qkv, L0_attn_b_qkv, L0_attn_w_out, L0_attn_b_out,
           L0_ffn_w_gate, L0_ffn_w_up, L0_ffn_w_down, L0_ffn_s_gate, L0_ffn_s_up,
           L0_ffn_s_down):
    cfg = Cfg()
    x = np.asarray(L0_attn_x, np.float32)
    sh = prep_shared(cfg, np.asarray(L0_attn_w_qkv, np.float32),
                     np.asarray(L0_attn_b_qkv, np.float32),
                     np.asarray(L0_attn_w_out, np.float32),
                     np.asarray(L0_attn_b_out, np.float32),
                     np.asarray(L0_ffn_w_gate), np.asarray(L0_ffn_w_up),
                     np.asarray(L0_ffn_w_down),
                     np.asarray(L0_ffn_s_gate, np.float32),
                     np.asarray(L0_ffn_s_up, np.float32),
                     np.asarray(L0_ffn_s_down, np.float32))
    in_maps, idxs = make_in_maps(cfg, x, sh)
    nc = build_nc(cfg, debug=False)
    res = run_bass_kernel_spmd(nc, in_maps, core_ids=list(range(NCORES)))
    out = np.empty((cfg.B, cfg.S, cfg.D), np.float32)
    for core in range(NCORES):
        b = core // 4
        out[b, idxs[core], :] = np.asarray(res.results[core]["yT"]).T
    return out


# version 08
# speedup vs baseline: 1.1192x; 1.0218x over previous
import math
import contextlib
import numpy as np
import ml_dtypes

import concourse.bass as bass
import concourse.mybir as mybir
from concourse.bass_utils import run_bass_kernel_spmd

F32 = mybir.dt.float32
BF16 = mybir.dt.bfloat16
I8 = mybir.dt.int8
AF = mybir.ActivationFunctionType
ALU = mybir.AluOpType
AX = mybir.AxisListType

NCORES = 8
NEG = -1e30
RS128 = 1.0 / math.sqrt(128.0)


class Cfg:
    def __init__(self, D=2048, S=4096, H=16, INTER=8192, Bsz=2):
        self.D, self.S, self.H, self.INTER, self.B = D, S, H, INTER, Bsz
        self.ND = D // 128
        self.NH = H
        self.NE3 = 3 * D // 128
        self.NO = INTER // 128
        self.TOKC = Bsz * S // NCORES
        self.NQT = self.TOKC // 128
        assert self.NQT == S // 512
        self.TT = min(512, self.TOKC)
        self.NTILES = self.TOKC // self.TT
        assert H == self.ND, "head_dim must be 128"


ENGINES = ("tensor", "vector", "scalar", "gpsimd", "sync")
SEMNAMES = ("t", "v", "s", "g", "cc", "gwx", "gwa", "gwb", "gwd",
            "swc", "swkv0", "swkv1", "swk0", "swk1", "swv",
            "swx0", "swx1", "swy0", "swy1")


class Prog:

    def __init__(self):
        self.cnt = {n: 0 for n in SEMNAMES}
        self.ops = {e: [] for e in ENGINES}

    def emit(self, eng, fn=None, wait=(), inc=()):
        w = [(sn, val) for (sn, val) in wait if val > 0]
        inc = list(inc)
        for sn, k in inc:
            self.cnt[sn] += k
        self.ops[eng].append((fn, w, inc))

    def now(self, sn):
        return self.cnt[sn]

    def dma(self, eng, fn, sem, wait=()):
        self.emit(eng, fn, wait=list(wait) + [(sem, self.now(sem))],
                  inc=[(sem, 16)])
        return self.now(sem)

    def finalize(self, block, semh):
        for ename in ENGINES:
            ops = self.ops[ename]

            def body(eng, ops=ops):
                lastw = {}
                for fn, waits, incs in ops:
                    for sn, val in waits:
                        if lastw.get(sn, 0) < val:
                            eng.wait_ge(semh[sn], val)
                            lastw[sn] = val
                    ins = fn(eng) if fn is not None else None
                    for sn, k in incs:
                        ins.then_inc(semh[sn], k)
            getattr(block, ename)(body)


def build_nc(cfg: Cfg, debug=False):
    c = cfg
    nc = bass.Bass("TRN2", target_bir_lowering=False)

    xT = nc.dram_tensor("xT", [c.D, c.TOKC], F32, kind="ExternalInput").ap()
    wqkvT = nc.dram_tensor("wqkvT", [c.NE3, 128, c.ND, 128], F32, kind="ExternalInput").ap()
    woutT = nc.dram_tensor("woutT", [c.ND, 128, c.ND, 128], F32, kind="ExternalInput").ap()
    wgu = nc.dram_tensor("wgu", [c.NO, 128, 2 * c.ND * 128], I8, kind="ExternalInput").ap()
    wdt = nc.dram_tensor("wdt", [c.ND, 128, c.NO * 128], I8, kind="ExternalInput").ap()
    bqkv = nc.dram_tensor("bqkv", [128, c.NE3], F32, kind="ExternalInput").ap()
    bout = nc.dram_tensor("bout", [128, c.ND], F32, kind="ExternalInput").ap()
    sg = nc.dram_tensor("sg", [128, c.NO], F32, kind="ExternalInput").ap()
    su = nc.dram_tensor("su", [128, c.NO], F32, kind="ExternalInput").ap()
    sd = nc.dram_tensor("sd", [128, c.ND], F32, kind="ExternalInput").ap()
    maskc = nc.dram_tensor("maskc", [128, 512], BF16, kind="ExternalInput").ap()
    identc = nc.dram_tensor("identc", [128, 128], BF16, kind="ExternalInput").ap()
    identfc = nc.dram_tensor("identfc", [128, 128], F32, kind="ExternalInput").ap()
    yT = nc.dram_tensor("yT", [c.D, c.TOKC], F32, kind="ExternalOutput").ap()

    kv_own = nc.dram_tensor("kv_own", [2 * c.D, c.TOKC], BF16)
    kgat = nc.dram_tensor("kgat", [4 * c.D, c.TOKC], BF16)
    vgat = nc.dram_tensor("vgat", [4 * c.D, c.TOKC], BF16)

    dbg = {}
    if debug:
        for nm, shp in (("dbg_qt", [128, c.NH * c.TOKC]),
                        ("dbg_kvg", [4 * 2 * c.D, c.TOKC]),
                        ("dbg_attn", [128, c.NQT * c.D]),
                        ("dbg_attnT", [128, c.ND * c.TOKC]),
                        ("dbg_preln", [128, c.ND * c.TT]),
                        ("dbg_x1b", [128, c.ND * c.TOKC])):
            dbg[nm] = nc.dram_tensor(nm, shp, F32, kind="ExternalOutput").ap()

    P = Prog()

    with contextlib.ExitStack() as ctx:
        E = ctx.enter_context

        QTx1b = E(nc.sbuf_tensor("QTx1b", [128, c.NH * c.TOKC], BF16))
        bigA = E(nc.sbuf_tensor("bigA", [128, c.NH * c.TOKC], BF16))
        attnTf = E(nc.sbuf_tensor("attnTf", [128, c.ND * c.TOKC], BF16))

        VW = (c.S // 128) * 132
        KVC = max(3 * c.S + 2 * VW, 2 * c.ND * c.TT + 2 * c.ND * 128 + 8 * c.TT)
        kvar = E(nc.sbuf_tensor("kvar", [128, KVC], BF16))
        kh = [kvar[:, i * c.S:(i + 1) * c.S] for i in range(2)]
        vtT = kvar[:, 2 * c.S:3 * c.S]
        vtok = [kvar[:, 3 * c.S + i * VW: 3 * c.S + (i + 1) * VW] for i in range(2)]
        preln = kvar[:, 0:2 * c.ND * c.TT].bitcast(F32)
        wo0 = 2 * c.ND * c.TT
        woutb = [kvar[:, wo0 + i * c.ND * 128: wo0 + (i + 1) * c.ND * 128]
                 for i in range(2)]
        xr0 = wo0 + 2 * c.ND * 128
        xres = [kvar[:, xr0 + i * 2 * c.TT: xr0 + (i + 1) * 2 * c.TT].bitcast(F32)
                for i in range(2)]
        gact = [kvar[:, wo0 + i * 2 * c.TT: wo0 + (i + 1) * 2 * c.TT].bitcast(F32)
                for i in range(2)]
        ys0 = wo0 + 4 * c.TT
        ystage = [kvar[:, ys0 + i * 2 * c.TT: ys0 + (i + 1) * 2 * c.TT].bitcast(F32)
                  for i in range(2)]
        assert ys0 + 4 * c.TT <= KVC

        SPC = max(5 * c.S, 2 * c.NO * 128 + 4 * c.ND * 128,
                  2 * c.ND * 128 + 2 * c.TOKC + 4 * 512 + 3 * 1024)
        spar = E(nc.sbuf_tensor("spar", [128, SPC], BF16))
        Sbf = spar[:, 0:c.S]
        PTb = [spar[:, (2 + i) * c.S:(3 + i) * c.S] for i in range(2)]
        wqkvb = [spar[:, i * c.ND * 128:(i + 1) * c.ND * 128] for i in range(2)]
        kvs0 = 2 * c.ND * 128
        kvstage = [spar[:, kvs0 + i * c.TOKC: kvs0 + (i + 1) * c.TOKC] for i in range(2)]
        assert kvs0 + 2 * c.TOKC <= 2 * c.S
        wdb = [spar[:, i * c.NO * 128:(i + 1) * c.NO * 128] for i in range(2)]
        wg0 = 2 * c.NO * 128
        wgub = [spar[:, wg0 + i * 2 * c.ND * 128: wg0 + (i + 1) * 2 * c.ND * 128]
                for i in range(2)]
        assert wg0 + 4 * c.ND * 128 <= SPC

        bqkv_s = E(nc.sbuf_tensor("bqkv_s", [128, c.NE3], F32))
        sbq_s = E(nc.sbuf_tensor("sbq_s", [128, c.ND], F32))
        bout_s = E(nc.sbuf_tensor("bout_s", [128, c.ND], F32))
        sg_s = E(nc.sbuf_tensor("sg_s", [128, c.NO], F32))
        su_s = E(nc.sbuf_tensor("su_s", [128, c.NO], F32))
        sd_s = E(nc.sbuf_tensor("sd_s", [128, c.ND], F32))
        maskb = E(nc.sbuf_tensor("maskb", [128, 512], BF16))
        identb = E(nc.sbuf_tensor("identb", [128, 128], BF16))
        identf = E(nc.sbuf_tensor("identf", [128, 128], F32))
        onesb = E(nc.sbuf_tensor("onesb", [128, 128], BF16))
        onesf = E(nc.sbuf_tensor("onesf", [1, 128], F32))
        negmax = E(nc.sbuf_tensor("negmax", [128, 2], F32))
        mcols = E(nc.sbuf_tensor("mcols", [128, 8], F32))
        kmax = E(nc.sbuf_tensor("kmax", [128, 2], F32))
        tmn = E(nc.sbuf_tensor("tmn", [1, 2], F32))
        sumv = E(nc.sbuf_tensor("sumv", [128, 2], F32))
        rr = E(nc.sbuf_tensor("rr", [128, 2], F32))
        ln0 = kvs0 + 2 * c.TOKC
        zb = [spar[:, ln0 + i * 512: ln0 + (i + 1) * 512] for i in range(2)]
        sqb = [spar[:, ln0 + (2 + i) * 512: ln0 + (3 + i) * 512] for i in range(2)]
        mrt0 = ln0 + 4 * 512
        mb_s = spar[:, mrt0: mrt0 + 1024].bitcast(F32)
        rb_s = spar[:, mrt0 + 1024: mrt0 + 2048].bitcast(F32)
        tmp_n = spar[:, mrt0 + 2048: mrt0 + 3072].bitcast(F32)
        assert mrt0 + 3072 <= SPC
        stat_a = E(nc.sbuf_tensor("stat_a", [1, 512], F32))
        stat_b = E(nc.sbuf_tensor("stat_b", [1, 512], F32))
        stat_c = E(nc.sbuf_tensor("stat_c", [1, 512], F32))

        pb = [E(nc.psum_tensor(f"pb{i}", [128, 512], F32)) for i in range(7)]
        tb = [E(nc.psum_tensor("tbb0", [128, 1024], BF16))]

        sems = {n: E(nc.semaphore(n)) for n in SEMNAMES}
        blk = E(nc.Block())

        dwar = {"x1b": [], "fuse": [], "preln": []}
        lnscratch = {"w": []}
        freed = {}
        for nm in ("pb0", "pb1", "pb2", "pb3", "pb4", "pb5", "pb6", "tb0",
                   "wqkvb0", "wqkvb1", "kvstage0", "kvstage1",
                   "kh0", "kh1", "vtT", "vtok0", "vtok1",
                   "Ssb", "negmax0", "negmax1", "PTb0", "PTb1", "kmax0", "kmax1",
                   "woutb0", "woutb1", "xres0", "xres1",
                   "zb0", "zb1", "sqb0", "sqb1", "mbrb",
                   "wgub0", "wgub1", "wdb0", "wdb1", "gact0", "gact1",
                   "ystage0", "ystage1"):
            freed[nm] = []

        for dst, src in ((bqkv_s, bqkv), (bout_s, bout), (sg_s, sg), (su_s, su),
                         (sd_s, sd), (maskb, maskc), (identb, identc),
                         (identf, identfc)):
            P.emit("sync", lambda e, d=dst, s=src: e.dma_start(out=d[:, :], in_=s[:, :]),
                   inc=[("swc", 16)])
        sw_consts = P.now("swc")
        P.emit("vector", lambda e: e.memset(onesb[:, :], 1.0), inc=[("v", 1)])
        P.emit("vector", lambda e: e.memset(onesf[:, :], 1.0), inc=[("v", 1)])
        for i in range(2):
            P.emit("vector",
                   lambda e, i=i: e.memset(
                       vtok[i].rearrange("p (kc w) -> p kc w", w=132)[:, :, 128:132],
                       1.0),
                   inc=[("v", 1)])
        P.emit("vector", lambda e: e.tensor_scalar_mul(sbq_s[:, :], bqkv_s[:, 0:c.ND],
                                                       RS128),
               wait=[("swc", sw_consts)], inc=[("v", 1)])
        v_setup = P.now("v")

        xbf = bigA
        for dc in range(c.ND):
            P.emit("gpsimd",
                   lambda e, dc=dc: e.dma_start(
                       out=xbf[:, dc * c.TOKC:(dc + 1) * c.TOKC],
                       in_=xT[dc * 128:(dc + 1) * 128, :]),
                   inc=[("gwx", 16)])
        gw_xbf = P.now("gwx")

        nhalves = max(1, c.TOKC // 512)
        width = min(512, c.TOKC)
        ec_order = list(range(c.ND, c.NE3)) + list(range(c.ND))
        for eci, ec in enumerate(ec_order):
            if ec == 2 * c.ND:
                P.emit("gpsimd",
                       lambda e: e.collective_compute(
                           "AllGather", ALU.bypass,
                           replica_groups=[[0, 1, 2, 3], [4, 5, 6, 7]],
                           ins=[kv_own[0:c.D, :].opt()], outs=[kgat.ap().opt()]),
                       wait=[("swkv0", P.now("swkv0")), ("swkv1", P.now("swkv1"))],
                       inc=[("cc", 1)])
            if ec == 0:
                P.emit("gpsimd",
                       lambda e: e.collective_compute(
                           "AllGather", ALU.bypass,
                           replica_groups=[[0, 1, 2, 3], [4, 5, 6, 7]],
                           ins=[kv_own[c.D:2 * c.D, :].opt()], outs=[vgat.ap().opt()]),
                       wait=[("swkv0", P.now("swkv0")), ("swkv1", P.now("swkv1"))],
                       inc=[("cc", 1)])
            bi = eci % 2
            sn = f"gw{'ab'[bi]}"
            P.dma("gpsimd",
                  lambda e, bi=bi, ec=ec: e.dma_start(
                      out=wqkvb[bi][:, :],
                      in_=wqkvT[ec].rearrange("p d e -> p (d e)")),
                  sn, wait=freed[f"wqkvb{bi}"])
            gw_w = (sn, P.now(sn))

            is_q = ec < c.ND
            is_k = c.ND <= ec < 2 * c.ND
            mm_done = []
            for n in range(nhalves):
                bank = 2 * bi + (n % 2)
                for dc in range(c.ND):
                    P.emit("tensor",
                           lambda e, bi=bi, dc=dc, n=n, bank=bank: e.matmul(
                               pb[bank][:, 0:width],
                               wqkvb[bi][:, dc * 128:(dc + 1) * 128],
                               xbf[:, dc * c.TOKC + n * 512: dc * c.TOKC + n * 512 + width],
                               start=(dc == 0), stop=(dc == c.ND - 1)),
                           wait=([gw_w, ("gwx", gw_xbf)] + freed[f"pb{bank}"])
                           if dc == 0 else [],
                           inc=[("t", 1)] if dc == c.ND - 1 else [])
                mm_done.append((bank, n, P.now("t")))
            freed[f"wqkvb{bi}"] = [("t", P.now("t"))]

            if is_q:
                for bank, n, t_mm in mm_done:
                    P.emit("scalar",
                           lambda e, bank=bank, n=n, ec=ec: e.activation(
                               QTx1b[:, ec * c.TOKC + n * 512: ec * c.TOKC + n * 512 + width],
                               pb[bank][:, 0:width], AF.Identity,
                               bias=sbq_s[:, ec:ec + 1], scale=RS128),
                           wait=[("t", t_mm), ("v", v_setup)], inc=[("s", 1)])
                    freed[f"pb{bank}"] = [("s", P.now("s"))]
            else:
                kvrow = (ec - c.ND) * 128 if is_k else c.D + (ec - 2 * c.ND) * 128
                for bank, n, t_mm in mm_done:
                    P.emit("scalar",
                           lambda e, bank=bank, n=n, bi=bi, ec=ec: e.activation(
                               kvstage[bi][:, n * 512: n * 512 + width],
                               pb[bank][:, 0:width], AF.Identity,
                               bias=bqkv_s[:, ec:ec + 1]),
                           wait=[("t", t_mm), ("swc", sw_consts)] + freed[f"kvstage{bi}"],
                           inc=[("s", 1)])
                    freed[f"pb{bank}"] = [("s", P.now("s"))]
                s_evac = P.now("s")
                P.dma("sync",
                      lambda e, bi=bi, kvrow=kvrow: e.dma_start(
                          out=kv_own[kvrow:kvrow + 128, :], in_=kvstage[bi][:, :]),
                      f"swkv{bi}", wait=[("s", s_evac)])
                freed[f"kvstage{bi}"] = [(f"swkv{bi}", P.now(f"swkv{bi}"))]
        s_qkv_done = P.now("s")

        if debug:
            for r_ in range(4):
                P.emit("gpsimd",
                       lambda e, r_=r_: e.dma_start(
                           out=dbg["dbg_kvg"][r_ * 2 * c.D: r_ * 2 * c.D + c.D, :],
                           in_=kgat[r_ * c.D:(r_ + 1) * c.D, :]),
                       wait=[("cc", 2), ("gwd", P.now("gwd"))], inc=[("gwd", 16)])
                P.emit("gpsimd",
                       lambda e, r_=r_: e.dma_start(
                           out=dbg["dbg_kvg"][r_ * 2 * c.D + c.D:(r_ + 1) * 2 * c.D, :],
                           in_=vgat[r_ * c.D:(r_ + 1) * c.D, :]),
                       wait=[("cc", 2), ("gwd", P.now("gwd"))], inc=[("gwd", 16)])
            P.emit("gpsimd",
                   lambda e: e.dma_start(out=dbg["dbg_qt"][:, :], in_=QTx1b[:, :]),
                   wait=[("s", s_qkv_done), ("gwd", P.now("gwd"))], inc=[("gwd", 16)])
            dwar["x1b"] = [("gwd", P.now("gwd"))]

        attn_out = bigA
        kg4 = kgat.ap().rearrange("(r f) (sb tt) -> f sb r tt", r=4, tt=128)
        vg4 = vgat.ap().rearrange("(r f) (sb tt) -> f sb r tt", r=4, tt=128)
        nkc_all = c.S // 128
        SCB = (0, 1, 2, 3, 5)
        vfm = [0]
        vstate = {"v_vtok0": 0, "v_vtok1": 0}

        for h in range(c.NH):
            hb = h % 2
            P.dma("sync",
                  lambda e, h=h, hb=hb: e.dma_start(
                      out=kh[hb][:, 0:c.S].rearrange("p (sb r tt) -> p sb r tt",
                                                     r=4, tt=128),
                      in_=kg4[h * 128:(h + 1) * 128]),
                  f"swk{hb}", wait=[("cc", 1)] + freed[f"kh{hb}"])
            sw_kh = (f"swk{hb}", P.now(f"swk{hb}"))
            if h == 0:
                P.dma("sync",
                      lambda e: e.dma_start(
                          out=vtT[:, 0:c.S].rearrange("p (sb r tt) -> p sb r tt",
                                                      r=4, tt=128),
                          in_=vg4[0:128]),
                      "swv", wait=[("cc", 2)] + freed["vtT"])
            sw_vt = P.now("swv")

            if h == 0:
                for kc8 in range((nkc_all + 7) // 8):
                    n_in = min(8, nkc_all - kc8 * 8)
                    for i in range(n_in):
                        kc = kc8 * 8 + i
                        P.emit("tensor",
                               lambda e, kc=kc, i=i: e.transpose(
                                   tb[0][:, i * 128:(i + 1) * 128],
                                   vtT[:, kc * 128:(kc + 1) * 128], identb[:, :]),
                               wait=([("swv", sw_vt), ("swc", sw_consts)]
                                     + freed["tb0"] + freed["vtok0"])
                               if i == 0 else [],
                               inc=[("t", 1)] if i == n_in - 1 else [])
                    t_tr = P.now("t")
                    P.emit("vector",
                           lambda e, kc8=kc8, n_in=n_in: e.tensor_copy(
                               vtok[0][:, kc8 * 8 * 132:]
                               .rearrange("p (kc w) -> p kc w", w=132)[:, 0:n_in, 0:128],
                               tb[0][:, 0:n_in * 128]
                               .rearrange("p (kc w) -> p kc w", w=128)),
                           wait=[("t", t_tr)], inc=[("v", 1)])
                    freed["tb0"] = [("v", P.now("v"))]
                freed["vtT"] = [("t", P.now("t"))]
                vstate["v_vtok0"] = P.now("v")
            if h + 1 < c.NH:
                P.dma("sync",
                      lambda e, h=h: e.dma_start(
                          out=vtT[:, 0:c.S].rearrange("p (sb r tt) -> p sb r tt",
                                                      r=4, tt=128),
                          in_=vg4[(h + 1) * 128: (h + 2) * 128]),
                      "swv", wait=freed["vtT"])
            v_vtok = vstate[f"v_vtok{h % 2}"]

            def emit_vspread(hn, qt, sw_vt=None):
                hnb = hn % 2
                for i in range(4):
                    kc = qt * 4 + i
                    P.emit("tensor",
                           lambda e, kc=kc, qt=qt, i=i: e.transpose(
                               tb[0][:, (qt % 2) * 512 + i * 128:
                                      (qt % 2) * 512 + (i + 1) * 128],
                               vtT[:, kc * 128:(kc + 1) * 128], identb[:, :]),
                           wait=([("swv", P.now("swv"))] + freed["tb0"]
                                 + (freed[f"vtok{hnb}"] if kc == 0 else []))
                           if i == 0 else [],
                           inc=[("t", 1)] if i == 3 else [])
                if qt % 2 == 1:
                    t_tr = P.now("t")
                    kc8 = (qt - 1) * 4
                    P.emit("vector",
                           lambda e, kc8=kc8, hnb=hnb: e.tensor_copy(
                               vtok[hnb][:, kc8 * 132:]
                               .rearrange("p (kc w) -> p kc w", w=132)[:, 0:8, 0:128],
                               tb[0][:, 0:1024]
                               .rearrange("p (kc w) -> p kc w", w=128)),
                           wait=[("t", t_tr)], inc=[("v", 1)])
                    freed["tb0"] = [("v", P.now("v"))]
                if qt == c.NQT - 1:
                    freed["vtT"] = [("t", P.now("t"))]
                    vstate[f"v_vtok{hnb}"] = P.now("v")

            pend = {}
            pend_g = {}
            pend_k = {}
            pend_c = {}

            def emit_scores(qt, h=h, hb=hb, sw_kh=sw_kh):
                qtp = qt % 2
                ngr = qt + 1
                spills = max(0, ngr - len(SCB))
                gbank = {}
                for g in range(ngr):
                    bank = SCB[g % len(SCB)]
                    gbank[g] = bank
                    for j4 in range(4):
                        kc = g * 4 + j4
                        diag = (g == qt)
                        P.emit("tensor",
                               lambda e, kc=kc, j4=j4, bank=bank, diag=diag: e.matmul(
                                   pb[bank][:, j4 * 128:(j4 + 1) * 128],
                                   kh[hb][:, kc * 128:(kc + 1) * 128],
                                   QTx1b[:, h * c.TOKC + qt * 128:
                                         h * c.TOKC + (qt + 1) * 128],
                                   start=True, stop=not diag),
                               wait=([sw_kh, ("s", s_qkv_done)] + freed[f"pb{bank}"])
                               if j4 == 0 else [],
                               inc=[("t", 1)] if (not diag and j4 == 3) else [])
                        if diag:
                            P.emit("tensor",
                                   lambda e, j4=j4, bank=bank: e.matmul(
                                       pb[bank][:, j4 * 128:(j4 + 1) * 128],
                                       maskb[:, j4 * 128:(j4 + 1) * 128],
                                       identb[:, :], start=False, stop=True),
                                   inc=[("t", 1)] if j4 == 3 else [])
                    t_g = P.now("t")
                    P.emit("vector",
                           lambda e, g=g, bank=bank: e.tensor_reduce(
                               mcols[:, g:g + 1], pb[bank][:, 0:512], AX.X, ALU.max),
                           wait=[("t", t_g), ("v", vfm[0])], inc=[("v", 1)])
                    v_bm = P.now("v")
                    if g < spills:
                        P.emit("vector",
                               lambda e, g=g, bank=bank: e.tensor_copy(
                                   Sbf[:, g * 512:(g + 1) * 512], pb[bank][:, 0:512]),
                               wait=(freed["Ssb"] if g == 0 else []), inc=[("v", 1)])
                        freed[f"pb{bank}"] = [("v", P.now("v"))]
                pend_g[qt] = (gbank, spills)
                pend_k[qt] = P.now("v")

            def emit_kchain_a(qt):
                qtp = qt % 2
                ngr = qt + 1
                v_bm = pend_k.pop(qt)
                P.emit("vector",
                       lambda e, qtp=qtp, ngr=ngr: e.tensor_reduce(
                           kmax[:, qtp:qtp + 1], mcols[:, 0:ngr], AX.X, ALU.max),
                       wait=[("v", v_bm)] + freed[f"kmax{qtp}"], inc=[("v", 1)])
                v_km = P.now("v")
                vfm[0] = v_km
                P.emit("tensor",
                       lambda e, qtp=qtp: e.transpose(
                           pb[6][0:1, 0:128], kmax[:, qtp:qtp + 1], identf[:, :]),
                       wait=[("v", v_km)] + freed["pb6"], inc=[("t", 1)])
                t_kt = P.now("t")
                P.emit("vector",
                       lambda e, qtp=qtp: e.tensor_reduce(
                           tmn[:, qtp:qtp + 1], pb[6][0:1, 0:128], AX.X, ALU.max,
                           negate=True),
                       wait=[("t", t_kt)], inc=[("v", 1)])
                pend_c[qt] = P.now("v")

            def emit_kchain_b(qt):
                qtp = qt % 2
                ngr = qt + 1
                v_tm = pend_c.pop(qt)
                P.emit("tensor",
                       lambda e, qtp=qtp: e.matmul(
                           pb[6][:, 128:129], onesf[:, :], tmn[:, qtp:qtp + 1],
                           start=True, stop=True),
                       wait=[("v", v_tm)], inc=[("t", 1)])
                t_bc = P.now("t")
                P.emit("scalar",
                       lambda e, qtp=qtp: e.copy(negmax[:, qtp:qtp + 1],
                                                 pb[6][:, 128:129]),
                       wait=[("t", t_bc)] + freed[f"negmax{qtp}"], inc=[("s", 1)])
                s_nm = P.now("s")
                freed["pb6"] = [("s", s_nm)]
                freed[f"kmax{qtp}"] = [("t", t_bc)]
                gbank, spills = pend_g.pop(qt)
                exp_marks = {}
                for g in range(ngr):
                    src_sp = g < spills
                    P.emit("scalar",
                           lambda e, g=g, qtp=qtp, src_sp=src_sp,
                           bank=gbank[g]: e.activation(
                               PTb[qtp][:, g * 512:(g + 1) * 512],
                               (Sbf[:, g * 512:(g + 1) * 512] if src_sp
                                else pb[bank][:, 0:512]), AF.Exp,
                               bias=negmax[:, qtp:qtp + 1]),
                           wait=[("s", s_nm)] + (freed[f"PTb{qtp}"] if g == 0 else []),
                           inc=[("s", 1)])
                    if not src_sp:
                        freed[f"pb{gbank[g]}"] = [("s", P.now("s"))]
                    exp_marks[g] = P.now("s")
                if spills:
                    freed["Ssb"] = [("s", exp_marks[spills - 1])]
                freed[f"negmax{qtp}"] = [("s", exp_marks[ngr - 1])]
                pend[qt] = (exp_marks, P.now("s"))

            def emit_ptpv(qt, h=h, hb=hb, v_vtok=v_vtok):
                qtp = qt % 2
                exp_marks, s_exp_all = pend.pop(qt)
                ngr = qt + 1
                nkc = 4 * ngr
                for kc in range(nkc):
                    w0 = [("s", exp_marks[kc // 4]), ("v", v_vtok)]
                    if kc == 0:
                        w0 = w0 + freed["pb4"]
                    P.emit("tensor",
                           lambda e, kc=kc, nkc=nkc, hb=hb, qtp=qtp: e.matmul(
                               pb[4][:, 0:129],
                               PTb[qtp][:, kc * 128:(kc + 1) * 128],
                               vtok[hb][:, kc * 132: kc * 132 + 129],
                               start=(kc == 0), stop=(kc == nkc - 1)),
                           wait=w0 if kc % 4 == 0 else [],
                           inc=[("t", 1)] if kc == nkc - 1 else [])
                t_pv = P.now("t")
                freed[f"PTb{qtp}"] = [("t", t_pv)]
                P.emit("vector",
                       lambda e, qtp=qtp: e.reciprocal(rr[:, qtp:qtp + 1],
                                                       pb[4][:, 128:129]),
                       wait=[("t", t_pv)], inc=[("v", 1)])
                v_rr = P.now("v")
                P.emit("scalar",
                       lambda e, h=h, qt=qt, qtp=qtp: e.mul(
                           attn_out[:, qt * c.D + h * 128: qt * c.D + (h + 1) * 128],
                           pb[4][:, 0:128], rr[:, qtp:qtp + 1]),
                       wait=[("v", v_rr)], inc=[("s", 1)])
                freed["pb4"] = [("s", P.now("s"))]

            for qt in range(c.NQT + 1):
                if qt < c.NQT:
                    emit_scores(qt)
                if qt >= 1:
                    emit_ptpv(qt - 1)
                if qt < c.NQT:
                    emit_kchain_a(qt)
                    emit_kchain_b(qt)
                if qt < c.NQT and h + 1 < c.NH:
                    emit_vspread(h + 1, qt)
            freed[f"kh{hb}"] = [("t", P.now("t"))]
            freed[f"vtok{hb}"] = [("t", P.now("t"))]
        v_attn_done = max(P.now("v"), 1)
        s_attn_done = P.now("s")

        if debug:
            P.emit("gpsimd",
                   lambda e: e.dma_start(out=dbg["dbg_attn"][:, :],
                                         in_=attn_out[:, 0:c.NQT * c.D]),
                   wait=[("s", s_attn_done), ("gwd", P.now("gwd"))], inc=[("gwd", 16)])
            dwar["fuse"] = [("gwd", P.now("gwd"))]

        for h in range(c.NH):
            for qt in range(c.NQT):
                P.emit("tensor",
                       lambda e, h=h, qt=qt: e.transpose(
                           tb[0][:, qt * 128:(qt + 1) * 128],
                           attn_out[:, qt * c.D + h * 128: qt * c.D + (h + 1) * 128],
                           identb[:, :]),
                       wait=([("s", s_attn_done)] + freed["tb0"]) if qt == 0 else [],
                       inc=[("t", 1)] if qt == c.NQT - 1 else [])
            t_tr = P.now("t")
            P.emit("vector",
                   lambda e, h=h: e.tensor_copy(
                       attnTf[:, h * c.TOKC:(h + 1) * c.TOKC],
                       tb[0][:, 0:c.NQT * 128]),
                   wait=[("t", t_tr)], inc=[("v", 1)])
            freed["tb0"] = [("v", P.now("v"))]
        v_attnT = P.now("v")
        attn_end = [("t", P.now("t")), ("v", P.now("v")), ("s", P.now("s"))]
        for nm in ("woutb0", "woutb1", "xres0", "xres1",
                   "wgub0", "wgub1", "wdb0", "wdb1",
                   "gact0", "gact1", "ystage0", "ystage1"):
            freed[nm] = list(attn_end)
        lnscratch["w"] = list(attn_end)
        freed["pv"] = list(attn_end)
        freed["pb5"] = list(attn_end)

        if debug:
            P.emit("gpsimd",
                   lambda e: e.dma_start(out=dbg["dbg_attnT"][:, :], in_=attnTf[:, :]),
                   wait=[("v", v_attnT), ("gwd", P.now("gwd"))], inc=[("gwd", 16)])
            dwar["fuse"] = dwar["fuse"] + [("gwd", P.now("gwd"))]

        freed["pv"] = freed["pv"]

        def emit_ln(preln_marks, dst_write):
            war = lnscratch["w"] + [("t", P.now("t")),
                                    ("gwa", P.now("gwa")), ("gwb", P.now("gwb"))]
            for ec in range(c.ND):
                zi = ec % 2
                P.emit("gpsimd",
                       lambda e, ec=ec, zi=zi: e.tensor_copy(
                           zb[zi][:, 0:c.TT], preln[:, ec * c.TT:(ec + 1) * c.TT]),
                       wait=[("v", preln_marks[ec])] + freed[f"zb{zi}"] + war,
                       inc=[("g", 1)])
                g_z = P.now("g")
                P.emit("scalar",
                       lambda e, ec=ec, zi=zi: e.activation(
                           sqb[zi][:, 0:c.TT], preln[:, ec * c.TT:(ec + 1) * c.TT],
                           AF.Square),
                       wait=[("v", preln_marks[ec])] + freed[f"sqb{zi}"] + war,
                       inc=[("s", 1)])
                s_q = P.now("s")
                P.emit("tensor",
                       lambda e, ec=ec, zi=zi: e.matmul(
                           pb[4][0:1, 0:c.TT], onesb[:, 0:1], zb[zi][:, 0:c.TT],
                           start=(ec == 0), stop=(ec == c.ND - 1)),
                       wait=[("g", g_z), ("v", v_setup)] + (freed["pv"] if ec == 0 else []),
                       inc=[("t", 1)])
                freed[f"zb{zi}"] = [("t", P.now("t"))]
                P.emit("tensor",
                       lambda e, ec=ec, zi=zi: e.matmul(
                           pb[5][0:1, 0:c.TT], onesb[:, 0:1], sqb[zi][:, 0:c.TT],
                           start=(ec == 0), stop=(ec == c.ND - 1)),
                       wait=[("s", s_q)], inc=[("t", 1)])
                freed[f"sqb{zi}"] = [("t", P.now("t"))]
            t_stats = P.now("t")
            P.emit("vector",
                   lambda e: e.tensor_scalar_mul(stat_a[:, 0:c.TT], pb[4][0:1, 0:c.TT],
                                                 1.0 / c.D),
                   wait=[("t", t_stats)], inc=[("v", 1)])
            P.emit("vector",
                   lambda e: e.tensor_mul(stat_c[:, 0:c.TT], stat_a[:, 0:c.TT],
                                          stat_a[:, 0:c.TT]),
                   wait=[("v", P.now("v"))], inc=[("v", 1)])
            P.emit("vector",
                   lambda e: e.scalar_tensor_tensor(
                       stat_b[:, 0:c.TT], pb[5][0:1, 0:c.TT], 1.0 / c.D,
                       stat_c[:, 0:c.TT], ALU.mult, ALU.subtract),
                   wait=[("v", P.now("v"))], inc=[("v", 1)])
            P.emit("vector",
                   lambda e: e.tensor_scalar_add(stat_b[:, 0:c.TT], stat_b[:, 0:c.TT],
                                                 1e-5),
                   wait=[("v", P.now("v"))], inc=[("v", 1)])
            v_var = P.now("v")
            P.emit("scalar",
                   lambda e: e.activation(stat_c[:, 0:c.TT], stat_b[:, 0:c.TT],
                                          AF.Sqrt),
                   wait=[("v", v_var)], inc=[("s", 1)])
            s_sd = P.now("s")
            P.emit("vector",
                   lambda e: e.reciprocal(stat_b[:, 0:c.TT], stat_c[:, 0:c.TT]),
                   wait=[("s", s_sd)], inc=[("v", 1)])
            v_rstd = P.now("v")
            P.emit("tensor",
                   lambda e: e.matmul(pb[4][:, 0:c.TT], onesf[:, :], stat_a[:, 0:c.TT],
                                      start=True, stop=True),
                   wait=[("v", v_rstd)], inc=[("t", 1)])
            P.emit("tensor",
                   lambda e: e.matmul(pb[5][:, 0:c.TT], onesf[:, :], stat_b[:, 0:c.TT],
                                      start=True, stop=True),
                   inc=[("t", 1)])
            t_bc = P.now("t")
            P.emit("vector", lambda e: e.tensor_copy(mb_s[:, 0:c.TT], pb[4][:, 0:c.TT]),
                   wait=[("t", t_bc)] + freed["mbrb"] + war, inc=[("v", 1)])
            P.emit("vector", lambda e: e.tensor_copy(rb_s[:, 0:c.TT], pb[5][:, 0:c.TT]),
                   inc=[("v", 1)])
            v_mbrb = P.now("v")
            freed["pv"] = [("v", v_mbrb)]
            for ec in range(c.ND):
                P.emit("vector",
                       lambda e, ec=ec: e.tensor_sub(
                           tmp_n[:, 0:c.TT], preln[:, ec * c.TT:(ec + 1) * c.TT],
                           mb_s[:, 0:c.TT]),
                       wait=[("v", max(v_mbrb, P.now("v")))], inc=[("v", 1)])
                dst_write(ec)
            freed["mbrb"] = [("v", P.now("v"))]
            lnscratch["w"] = [("v", P.now("v")), ("t", P.now("t")),
                              ("s", P.now("s")), ("g", P.now("g"))]

        x1b = QTx1b
        for T in range(c.NTILES):
            preln_marks = {}
            for ec in range(c.ND):
                bi = ec % 2
                sn = f"gw{'ab'[bi]}"
                P.dma("gpsimd",
                      lambda e, ec=ec, bi=bi: e.dma_start(
                          out=woutb[bi][:, :],
                          in_=woutT[ec].rearrange("p d e -> p (d e)")),
                      sn, wait=freed[f"woutb{bi}"])
                gw_w = (sn, P.now(sn))
                P.dma("sync",
                      lambda e, ec=ec, bi=bi, T=T: e.dma_start(
                          out=xres[bi][:, 0:c.TT],
                          in_=xT[ec * 128:(ec + 1) * 128, T * c.TT:(T + 1) * c.TT]),
                      f"swx{bi}", wait=freed[f"xres{bi}"])
                sw_x = (f"swx{bi}", P.now(f"swx{bi}"))
                bank = ec % 4
                for dc in range(c.ND):
                    P.emit("tensor",
                           lambda e, dc=dc, bi=bi, bank=bank, T=T: e.matmul(
                               pb[bank][:, 0:c.TT],
                               woutb[bi][:, dc * 128:(dc + 1) * 128],
                               attnTf[:, dc * c.TOKC + T * c.TT:
                                      dc * c.TOKC + (T + 1) * c.TT],
                               start=(dc == 0), stop=(dc == c.ND - 1)),
                           wait=([gw_w, ("v", v_attnT)] + freed[f"pb{bank}"])
                           if dc == 0 else [],
                           inc=[("t", 1)] if dc == c.ND - 1 else [])
                t_mm = P.now("t")
                freed[f"woutb{bi}"] = [("t", t_mm)]
                P.emit("vector",
                       lambda e, ec=ec, bank=bank, bi=bi: e.scalar_tensor_tensor(
                           preln[:, ec * c.TT:(ec + 1) * c.TT], pb[bank][:, 0:c.TT],
                           bout_s[:, ec:ec + 1], xres[bi][:, 0:c.TT],
                           ALU.add, ALU.add),
                       wait=[("t", t_mm), sw_x, ("swc", sw_consts)] + dwar["preln"],
                       inc=[("v", 1)])
                freed[f"pb{bank}"] = [("v", P.now("v"))]
                freed[f"xres{bi}"] = [("v", P.now("v"))]
                preln_marks[ec] = P.now("v")

            if debug and T == 0:
                P.emit("gpsimd",
                       lambda e: e.dma_start(out=dbg["dbg_preln"][:, :],
                                             in_=preln[:, 0:c.ND * c.TT]),
                       wait=[("v", max(preln_marks.values())), ("gwd", P.now("gwd"))], inc=[("gwd", 16)])
                dwar["preln"] = [("gwd", P.now("gwd"))]

            def dst_x1(ec, T=T):
                P.emit("vector",
                       lambda e, ec=ec, T=T: e.tensor_mul(
                           x1b[:, ec * c.TOKC + T * c.TT: ec * c.TOKC + (T + 1) * c.TT],
                           tmp_n[:, 0:c.TT], rb_s[:, 0:c.TT]),
                       wait=[("v", P.now("v"))] + dwar["x1b"], inc=[("v", 1)])
            emit_ln(preln_marks, dst_x1)
        v_x1_done = P.now("v")

        if debug:
            P.emit("gpsimd",
                   lambda e: e.dma_start(out=dbg["dbg_x1b"][:, :], in_=x1b[:, :]),
                   wait=[("v", v_x1_done), ("gwd", P.now("gwd"))], inc=[("gwd", 16)])

        nslotA = (c.NH * c.TOKC) // c.TT

        def fuse_slice(oc):
            if oc < nslotA:
                return bigA[:, oc * c.TT:(oc + 1) * c.TT]
            oc2 = oc - nslotA
            return attnTf[:, oc2 * c.TT:(oc2 + 1) * c.TT]

        fuse_war = {"w": []}

        def emit_gateup(T):
            fuse_last = 0
            for oc in range(c.NO):
                bi = oc % 2
                sn = f"gw{'ab'[bi]}"
                P.dma("gpsimd",
                      lambda e, oc=oc, bi=bi: e.dma_start(
                          out=wgub[bi][:, :], in_=wgu[oc]),
                      sn, wait=freed[f"wgub{bi}"] + lnscratch["w"])
                gw_w = (sn, P.now(sn))
                bk_g = (oc % 2) * 2
                bk_u = bk_g + 1
                for which, bank in ((0, bk_g), (1, bk_u)):
                    for dc in range(c.ND):
                        P.emit("tensor",
                               lambda e, bi=bi, which=which, dc=dc, bank=bank, T=T: e.matmul(
                                   pb[bank][:, 0:c.TT],
                                   wgub[bi][:, which * c.ND * 128 + dc * 128:
                                             which * c.ND * 128 + (dc + 1) * 128],
                                   x1b[:, dc * c.TOKC + T * c.TT:
                                       dc * c.TOKC + (T + 1) * c.TT],
                                   start=(dc == 0), stop=(dc == c.ND - 1)),
                               wait=([gw_w, ("v", v_x1_done)] + freed[f"pb{bank}"])
                               if dc == 0 else [],
                               inc=[("t", 1)] if dc == c.ND - 1 else [])
                t_gu = P.now("t")
                freed[f"wgub{bi}"] = [("t", t_gu)]
                gi = oc % 2
                P.emit("scalar",
                       lambda e, oc=oc, bank=bk_g, gi=gi: e.activation(
                           gact[gi][:, 0:c.TT], pb[bank][:, 0:c.TT], AF.Silu,
                           scale=sg_s[:, oc:oc + 1]),
                       wait=[("t", t_gu), ("swc", sw_consts)] + freed[f"gact{gi}"],
                       inc=[("s", 1)])
                s_gact = P.now("s")
                freed[f"pb{bk_g}"] = [("s", s_gact)]
                P.emit("vector",
                       lambda e, oc=oc, bank=bk_u, gi=gi: e.scalar_tensor_tensor(
                           fuse_slice(oc), pb[bank][:, 0:c.TT], su_s[:, oc:oc + 1],
                           gact[gi][:, 0:c.TT], ALU.mult, ALU.mult),
                       wait=[("s", s_gact)] + dwar["fuse"] + fuse_war["w"],
                       inc=[("v", 1)])
                v_f = P.now("v")
                freed[f"pb{bk_u}"] = [("v", v_f)]
                freed[f"gact{gi}"] = [("v", v_f)]
                fuse_last = v_f
            return fuse_last

        def emit_down(T, v_fuse_all):
            preln_marks = {}
            for ec in range(c.ND):
                bi = ec % 2
                sn = f"gw{'ab'[bi]}"
                P.dma("gpsimd",
                      lambda e, ec=ec, bi=bi: e.dma_start(
                          out=wdb[bi][:, :], in_=wdt[ec]),
                      sn, wait=freed[f"wdb{bi}"] + lnscratch["w"])
                gw_w = (sn, P.now(sn))
                bank = ec % 4
                for oc in range(c.NO):
                    P.emit("tensor",
                           lambda e, oc=oc, bi=bi, bank=bank: e.matmul(
                               pb[bank][:, 0:c.TT],
                               wdb[bi][:, oc * 128:(oc + 1) * 128],
                               fuse_slice(oc),
                               start=(oc == 0), stop=(oc == c.NO - 1)),
                           wait=([gw_w, ("v", v_fuse_all)] + freed[f"pb{bank}"])
                           if oc == 0 else [],
                           inc=[("t", 1)] if oc == c.NO - 1 else [])
                t_mm = P.now("t")
                freed[f"wdb{bi}"] = [("t", t_mm)]
                P.emit("vector",
                       lambda e, ec=ec, bank=bank, T=T: e.scalar_tensor_tensor(
                           preln[:, ec * c.TT:(ec + 1) * c.TT], pb[bank][:, 0:c.TT],
                           sd_s[:, ec:ec + 1],
                           x1b[:, ec * c.TOKC + T * c.TT: ec * c.TOKC + (T + 1) * c.TT],
                           ALU.mult, ALU.add),
                       wait=[("t", t_mm)] + dwar["preln"] + lnscratch["w"],
                       inc=[("v", 1)])
                freed[f"pb{bank}"] = [("v", P.now("v"))]
                preln_marks[ec] = P.now("v")
            fuse_war["w"] = [("t", P.now("t"))]
            return preln_marks

        def emit_ln2(T, preln_marks):
            def dst_y(ec, T=T):
                yi = ec % 2
                P.emit("vector",
                       lambda e, yi=yi: e.tensor_mul(
                           ystage[yi][:, 0:c.TT], tmp_n[:, 0:c.TT], rb_s[:, 0:c.TT]),
                       wait=[("v", P.now("v"))] + freed[f"ystage{yi}"], inc=[("v", 1)])
                v_y = P.now("v")
                P.dma("sync",
                      lambda e, ec=ec, yi=yi, T=T: e.dma_start(
                          out=yT[ec * 128:(ec + 1) * 128, T * c.TT:(T + 1) * c.TT],
                          in_=ystage[yi][:, 0:c.TT]),
                      f"swy{yi}", wait=[("v", v_y)])
                freed[f"ystage{yi}"] = [(f"swy{yi}", P.now(f"swy{yi}"))]
            emit_ln(preln_marks, dst_y)

        if c.NTILES == 2:
            vf0 = emit_gateup(0)
            pm0 = emit_down(0, vf0)
            vf1 = emit_gateup(1)
            emit_ln2(0, pm0)
            pm1 = emit_down(1, vf1)
            emit_ln2(1, pm1)
        else:
            vf0 = emit_gateup(0)
            pm0 = emit_down(0, vf0)
            emit_ln2(0, pm0)

        P.emit("sync", None, wait=[("swy0", P.now("swy0")), ("swy1", P.now("swy1"))])
        P.emit("gpsimd", None, wait=[("gwd", P.now("gwd")), ("gwa", P.now("gwa")),
                                     ("gwb", P.now("gwb"))])
        P.finalize(blk, sems)
    return nc



def prep_shared(cfg, wqkv, b_qkv, w_out, b_out, w_g, w_u, w_d, s_g, s_u, s_d):
    c = cfg
    f32 = np.float32
    sh = {}
    sh["wqkvT"] = np.ascontiguousarray(
        wqkv.reshape(c.NE3, 128, c.ND, 128).transpose(0, 3, 2, 1)).astype(f32)
    sh["woutT"] = np.ascontiguousarray(
        w_out.reshape(c.ND, 128, c.ND, 128).transpose(0, 3, 2, 1)).astype(f32)
    wgt = w_g.reshape(c.ND, 128, c.NO, 128).transpose(2, 1, 0, 3)
    wut = w_u.reshape(c.ND, 128, c.NO, 128).transpose(2, 1, 0, 3)
    sh["wgu"] = np.ascontiguousarray(
        np.concatenate([wgt.reshape(c.NO, 128, 1, c.ND * 128),
                        wut.reshape(c.NO, 128, 1, c.ND * 128)], axis=2)
        .reshape(c.NO, 128, 2 * c.ND * 128)).astype(np.int8)
    sh["wdt"] = np.ascontiguousarray(
        w_d.reshape(c.NO, 128, c.ND, 128).transpose(2, 1, 0, 3)
        .reshape(c.ND, 128, c.NO * 128)).astype(np.int8)
    sh["bqkv"] = np.ascontiguousarray(b_qkv.reshape(c.NE3, 128).T).astype(f32)
    sh["bout"] = np.ascontiguousarray(b_out.reshape(c.ND, 128).T).astype(f32)
    sh["sg"] = np.ascontiguousarray(s_g.reshape(c.NO, 128).T).astype(f32)
    sh["su"] = np.ascontiguousarray(s_u.reshape(c.NO, 128).T).astype(f32)
    sh["sd"] = np.ascontiguousarray(s_d.reshape(c.ND, 128).T).astype(f32)
    sh["identc"] = np.eye(128).astype(ml_dtypes.bfloat16)
    sh["identfc"] = np.eye(128, dtype=np.float32)
    return sh


def core_token_index(cfg, j):
    c = cfg
    idx = []
    for qt in range(c.NQT):
        base = 512 * qt + 128 * j
        idx.extend(range(base, base + 128))
    return np.asarray(idx)


def core_mask(j):
    q = np.arange(128)[:, None]
    k = np.arange(512)[None, :]
    m = np.where(k <= 128 * j + q, 0.0, NEG).astype(np.float32)
    return np.ascontiguousarray(m.astype(ml_dtypes.bfloat16))


def make_in_maps(cfg, x, sh):
    in_maps, idxs = [], []
    for core in range(NCORES):
        b, j = core // 4, core % 4
        idx = core_token_index(cfg, j)
        idxs.append(idx)
        m = dict(sh)
        m["xT"] = np.ascontiguousarray(x[b].T[:, idx])
        m["maskc"] = core_mask(j)
        in_maps.append(m)
    return in_maps, idxs


def kernel(L0_attn_x, L0_attn_w_qkv, L0_attn_b_qkv, L0_attn_w_out, L0_attn_b_out,
           L0_ffn_w_gate, L0_ffn_w_up, L0_ffn_w_down, L0_ffn_s_gate, L0_ffn_s_up,
           L0_ffn_s_down):
    cfg = Cfg()
    x = np.asarray(L0_attn_x, np.float32)
    sh = prep_shared(cfg, np.asarray(L0_attn_w_qkv, np.float32),
                     np.asarray(L0_attn_b_qkv, np.float32),
                     np.asarray(L0_attn_w_out, np.float32),
                     np.asarray(L0_attn_b_out, np.float32),
                     np.asarray(L0_ffn_w_gate), np.asarray(L0_ffn_w_up),
                     np.asarray(L0_ffn_w_down),
                     np.asarray(L0_ffn_s_gate, np.float32),
                     np.asarray(L0_ffn_s_up, np.float32),
                     np.asarray(L0_ffn_s_down, np.float32))
    in_maps, idxs = make_in_maps(cfg, x, sh)
    nc = build_nc(cfg, debug=False)
    res = run_bass_kernel_spmd(nc, in_maps, core_ids=list(range(NCORES)))
    out = np.empty((cfg.B, cfg.S, cfg.D), np.float32)
    for core in range(NCORES):
        b = core // 4
        out[b, idxs[core], :] = np.asarray(res.results[core]["yT"]).T
    return out


# version 09
# speedup vs baseline: 1.1271x; 1.0070x over previous
import math
import contextlib
import numpy as np
import ml_dtypes

import concourse.bass as bass
import concourse.mybir as mybir
from concourse.bass_utils import run_bass_kernel_spmd

F32 = mybir.dt.float32
BF16 = mybir.dt.bfloat16
I8 = mybir.dt.int8
AF = mybir.ActivationFunctionType
ALU = mybir.AluOpType
AX = mybir.AxisListType

NCORES = 8
NEG = -1e30
RS128 = 1.0 / math.sqrt(128.0)


class Cfg:
    def __init__(self, D=2048, S=4096, H=16, INTER=8192, Bsz=2):
        self.D, self.S, self.H, self.INTER, self.B = D, S, H, INTER, Bsz
        self.ND = D // 128
        self.NH = H
        self.NE3 = 3 * D // 128
        self.NO = INTER // 128
        self.TOKC = Bsz * S // NCORES
        self.NQT = self.TOKC // 128
        assert self.NQT == S // 512
        self.TT = min(512, self.TOKC)
        self.NTILES = self.TOKC // self.TT
        assert H == self.ND, "head_dim must be 128"


ENGINES = ("tensor", "vector", "scalar", "gpsimd", "sync")
SEMNAMES = ("t", "v", "s", "g", "cc", "gwx", "gwa", "gwb", "gwc", "gwe", "gwd",
            "swc", "swkv0", "swkv1", "swkv2", "swkv3", "swk0", "swk1", "swv",
            "swx0", "swx1", "swy0", "swy1")


class Prog:

    def __init__(self):
        self.cnt = {n: 0 for n in SEMNAMES}
        self.ops = {e: [] for e in ENGINES}

    def emit(self, eng, fn=None, wait=(), inc=()):
        w = [(sn, val) for (sn, val) in wait if val > 0]
        inc = list(inc)
        for sn, k in inc:
            self.cnt[sn] += k
        self.ops[eng].append((fn, w, inc))

    def now(self, sn):
        return self.cnt[sn]

    def dma(self, eng, fn, sem, wait=()):
        self.emit(eng, fn, wait=list(wait) + [(sem, self.now(sem))],
                  inc=[(sem, 16)])
        return self.now(sem)

    def finalize(self, block, semh):
        for ename in ENGINES:
            ops = self.ops[ename]

            def body(eng, ops=ops):
                lastw = {}
                for fn, waits, incs in ops:
                    for sn, val in waits:
                        if lastw.get(sn, 0) < val:
                            eng.wait_ge(semh[sn], val)
                            lastw[sn] = val
                    ins = fn(eng) if fn is not None else None
                    for sn, k in incs:
                        ins.then_inc(semh[sn], k)
            getattr(block, ename)(body)


def build_nc(cfg: Cfg, debug=False):
    c = cfg
    nc = bass.Bass("TRN2", target_bir_lowering=False)

    xT = nc.dram_tensor("xT", [c.D, c.TOKC], F32, kind="ExternalInput").ap()
    wqkvT = nc.dram_tensor("wqkvT", [c.NE3, 128, c.ND, 128], F32, kind="ExternalInput").ap()
    woutT = nc.dram_tensor("woutT", [c.ND, 128, c.ND, 128], F32, kind="ExternalInput").ap()
    wgu = nc.dram_tensor("wgu", [c.NO, 128, 2 * c.ND * 128], I8, kind="ExternalInput").ap()
    wdt = nc.dram_tensor("wdt", [c.ND, 128, c.NO * 128], I8, kind="ExternalInput").ap()
    bqkv = nc.dram_tensor("bqkv", [128, c.NE3], F32, kind="ExternalInput").ap()
    bout = nc.dram_tensor("bout", [128, c.ND], F32, kind="ExternalInput").ap()
    sg = nc.dram_tensor("sg", [128, c.NO], F32, kind="ExternalInput").ap()
    su = nc.dram_tensor("su", [128, c.NO], F32, kind="ExternalInput").ap()
    sd = nc.dram_tensor("sd", [128, c.ND], F32, kind="ExternalInput").ap()
    maskc = nc.dram_tensor("maskc", [128, 512], BF16, kind="ExternalInput").ap()
    identc = nc.dram_tensor("identc", [128, 128], BF16, kind="ExternalInput").ap()
    identfc = nc.dram_tensor("identfc", [128, 128], F32, kind="ExternalInput").ap()
    yT = nc.dram_tensor("yT", [c.D, c.TOKC], F32, kind="ExternalOutput").ap()

    kv_own = nc.dram_tensor("kv_own", [2 * c.D, c.TOKC], BF16)
    kgat = nc.dram_tensor("kgat", [4 * c.D, c.TOKC], BF16)
    vgat = nc.dram_tensor("vgat", [4 * c.D, c.TOKC], BF16)

    dbg = {}
    if debug:
        for nm, shp in (("dbg_qt", [128, c.NH * c.TOKC]),
                        ("dbg_kvg", [4 * 2 * c.D, c.TOKC]),
                        ("dbg_attn", [128, c.NQT * c.D]),
                        ("dbg_attnT", [128, c.ND * c.TOKC]),
                        ("dbg_preln", [128, c.ND * c.TT]),
                        ("dbg_x1b", [128, c.ND * c.TOKC])):
            dbg[nm] = nc.dram_tensor(nm, shp, F32, kind="ExternalOutput").ap()

    P = Prog()

    with contextlib.ExitStack() as ctx:
        E = ctx.enter_context

        QTx1b = E(nc.sbuf_tensor("QTx1b", [128, c.NH * c.TOKC], BF16))
        bigA = E(nc.sbuf_tensor("bigA", [128, c.NH * c.TOKC], BF16))
        attnTf = E(nc.sbuf_tensor("attnTf", [128, c.ND * c.TOKC], BF16))

        VW = (c.S // 128) * 132
        KVC = max(3 * c.S + 2 * VW, 2 * c.ND * c.TT + 2 * c.ND * 128 + 8 * c.TT)
        kvar = E(nc.sbuf_tensor("kvar", [128, KVC], BF16))
        kh = [kvar[:, i * c.S:(i + 1) * c.S] for i in range(2)]
        vtT = kvar[:, 2 * c.S:3 * c.S]
        vtok = [kvar[:, 3 * c.S + i * VW: 3 * c.S + (i + 1) * VW] for i in range(2)]
        preln = kvar[:, 0:2 * c.ND * c.TT].bitcast(F32)
        wo0 = 2 * c.ND * c.TT
        woutb = [kvar[:, wo0 + i * c.ND * 128: wo0 + (i + 1) * c.ND * 128]
                 for i in range(2)]
        xr0 = wo0 + 2 * c.ND * 128
        xres = [kvar[:, xr0 + i * 2 * c.TT: xr0 + (i + 1) * 2 * c.TT].bitcast(F32)
                for i in range(2)]
        gact = [kvar[:, wo0 + i * 2 * c.TT: wo0 + (i + 1) * 2 * c.TT].bitcast(F32)
                for i in range(2)]
        ys0 = wo0 + 4 * c.TT
        ystage = [kvar[:, ys0 + i * 2 * c.TT: ys0 + (i + 1) * 2 * c.TT].bitcast(F32)
                  for i in range(2)]
        assert ys0 + 4 * c.TT <= KVC

        SPC = max(5 * c.S, 2 * c.NO * 128 + 4 * c.ND * 128,
                  4 * c.ND * 128 + 4 * c.TOKC + 4 * 512 + 3 * 1024)
        spar = E(nc.sbuf_tensor("spar", [128, SPC], BF16))
        Sbf = spar[:, 0:c.S]
        PTb = [spar[:, (2 + i) * c.S:(3 + i) * c.S] for i in range(2)]
        wqkvb = [spar[:, i * c.ND * 128:(i + 1) * c.ND * 128] for i in range(4)]
        kvs0 = 4 * c.ND * 128
        kvstage = [spar[:, kvs0 + i * c.TOKC: kvs0 + (i + 1) * c.TOKC] for i in range(4)]
        assert kvs0 + 4 * c.TOKC <= 4 * c.S
        wdb = [spar[:, i * c.NO * 128:(i + 1) * c.NO * 128] for i in range(2)]
        wg0 = 2 * c.NO * 128
        wgub = [spar[:, wg0 + i * 2 * c.ND * 128: wg0 + (i + 1) * 2 * c.ND * 128]
                for i in range(2)]
        assert wg0 + 4 * c.ND * 128 <= SPC

        bqkv_s = E(nc.sbuf_tensor("bqkv_s", [128, c.NE3], F32))
        sbq_s = E(nc.sbuf_tensor("sbq_s", [128, c.ND], F32))
        bout_s = E(nc.sbuf_tensor("bout_s", [128, c.ND], F32))
        sg_s = E(nc.sbuf_tensor("sg_s", [128, c.NO], F32))
        su_s = E(nc.sbuf_tensor("su_s", [128, c.NO], F32))
        sd_s = E(nc.sbuf_tensor("sd_s", [128, c.ND], F32))
        maskb = E(nc.sbuf_tensor("maskb", [128, 512], BF16))
        identb = E(nc.sbuf_tensor("identb", [128, 128], BF16))
        identf = E(nc.sbuf_tensor("identf", [128, 128], F32))
        onesb = E(nc.sbuf_tensor("onesb", [128, 128], BF16))
        onesf = E(nc.sbuf_tensor("onesf", [1, 128], F32))
        negmax = E(nc.sbuf_tensor("negmax", [128, 2], F32))
        mcols = E(nc.sbuf_tensor("mcols", [128, 8], F32))
        kmax = E(nc.sbuf_tensor("kmax", [128, 2], F32))
        tmn = E(nc.sbuf_tensor("tmn", [1, 2], F32))
        sumv = E(nc.sbuf_tensor("sumv", [128, 2], F32))
        rr = E(nc.sbuf_tensor("rr", [128, 2], F32))
        ln0 = kvs0 + 4 * c.TOKC
        zb = [spar[:, ln0 + i * 512: ln0 + (i + 1) * 512] for i in range(2)]
        sqb = [spar[:, ln0 + (2 + i) * 512: ln0 + (3 + i) * 512] for i in range(2)]
        mrt0 = ln0 + 4 * 512
        mb_s = spar[:, mrt0: mrt0 + 1024].bitcast(F32)
        rb_s = spar[:, mrt0 + 1024: mrt0 + 2048].bitcast(F32)
        tmp_n = spar[:, mrt0 + 2048: mrt0 + 3072].bitcast(F32)
        assert mrt0 + 3072 <= SPC
        stat_a = E(nc.sbuf_tensor("stat_a", [1, 512], F32))
        stat_b = E(nc.sbuf_tensor("stat_b", [1, 512], F32))
        stat_c = E(nc.sbuf_tensor("stat_c", [1, 512], F32))

        pb = [E(nc.psum_tensor(f"pb{i}", [128, 512], F32)) for i in range(7)]
        tb = [E(nc.psum_tensor("tbb0", [128, 1024], BF16))]

        sems = {n: E(nc.semaphore(n)) for n in SEMNAMES}
        blk = E(nc.Block())

        dwar = {"x1b": [], "fuse": [], "preln": []}
        lnscratch = {"w": []}
        freed = {}
        for nm in ("pb0", "pb1", "pb2", "pb3", "pb4", "pb5", "pb6", "tb0",
                   "wqkvb0", "wqkvb1", "wqkvb2", "wqkvb3",
                   "kvstage0", "kvstage1", "kvstage2", "kvstage3",
                   "kh0", "kh1", "vtT", "vtok0", "vtok1",
                   "Ssb", "negmax0", "negmax1", "PTb0", "PTb1", "kmax0", "kmax1",
                   "woutb0", "woutb1", "xres0", "xres1",
                   "zb0", "zb1", "sqb0", "sqb1", "mbrb",
                   "wgub0", "wgub1", "wdb0", "wdb1", "gact0", "gact1",
                   "ystage0", "ystage1"):
            freed[nm] = []

        for dst, src in ((bqkv_s, bqkv), (bout_s, bout), (sg_s, sg), (su_s, su),
                         (sd_s, sd), (maskb, maskc), (identb, identc),
                         (identf, identfc)):
            P.emit("sync", lambda e, d=dst, s=src: e.dma_start(out=d[:, :], in_=s[:, :]),
                   inc=[("swc", 16)])
        sw_consts = P.now("swc")
        P.emit("vector", lambda e: e.memset(onesb[:, :], 1.0), inc=[("v", 1)])
        P.emit("vector", lambda e: e.memset(onesf[:, :], 1.0), inc=[("v", 1)])
        for i in range(2):
            P.emit("vector",
                   lambda e, i=i: e.memset(
                       vtok[i].rearrange("p (kc w) -> p kc w", w=132)[:, :, 128:132],
                       1.0),
                   inc=[("v", 1)])
        P.emit("vector", lambda e: e.tensor_scalar_mul(sbq_s[:, :], bqkv_s[:, 0:c.ND],
                                                       RS128),
               wait=[("swc", sw_consts)], inc=[("v", 1)])
        v_setup = P.now("v")

        xbf = bigA
        for dc in range(c.ND):
            P.emit("gpsimd",
                   lambda e, dc=dc: e.dma_start(
                       out=xbf[:, dc * c.TOKC:(dc + 1) * c.TOKC],
                       in_=xT[dc * 128:(dc + 1) * 128, :]),
                   inc=[("gwx", 16)])
        gw_xbf = P.now("gwx")

        nhalves = max(1, c.TOKC // 512)
        width = min(512, c.TOKC)
        ec_order = list(range(c.ND, c.NE3)) + list(range(c.ND))
        for eci, ec in enumerate(ec_order):
            if ec == 2 * c.ND:
                P.emit("gpsimd",
                       lambda e: e.collective_compute(
                           "AllGather", ALU.bypass,
                           replica_groups=[[0, 1, 2, 3], [4, 5, 6, 7]],
                           ins=[kv_own[0:c.D, :].opt()], outs=[kgat.ap().opt()]),
                       wait=[(f"swkv{i}", P.now(f"swkv{i}")) for i in range(4)],
                       inc=[("cc", 1)])
            if ec == 0:
                P.emit("gpsimd",
                       lambda e: e.collective_compute(
                           "AllGather", ALU.bypass,
                           replica_groups=[[0, 1, 2, 3], [4, 5, 6, 7]],
                           ins=[kv_own[c.D:2 * c.D, :].opt()], outs=[vgat.ap().opt()]),
                       wait=[(f"swkv{i}", P.now(f"swkv{i}")) for i in range(4)],
                       inc=[("cc", 1)])
            bi = eci % 4
            sn = f"gw{'abce'[bi]}"
            P.dma("gpsimd",
                  lambda e, bi=bi, ec=ec: e.dma_start(
                      out=wqkvb[bi][:, :],
                      in_=wqkvT[ec].rearrange("p d e -> p (d e)")),
                  sn, wait=freed[f"wqkvb{bi}"])
            gw_w = (sn, P.now(sn))

            is_q = ec < c.ND
            is_k = c.ND <= ec < 2 * c.ND
            mm_done = []
            for n in range(nhalves):
                bank = 2 * (bi % 2) + (n % 2)
                for dc in range(c.ND):
                    P.emit("tensor",
                           lambda e, bi=bi, dc=dc, n=n, bank=bank: e.matmul(
                               pb[bank][:, 0:width],
                               wqkvb[bi][:, dc * 128:(dc + 1) * 128],
                               xbf[:, dc * c.TOKC + n * 512: dc * c.TOKC + n * 512 + width],
                               start=(dc == 0), stop=(dc == c.ND - 1)),
                           wait=([gw_w, ("gwx", gw_xbf)] + freed[f"pb{bank}"])
                           if dc == 0 else [],
                           inc=[("t", 1)] if dc == c.ND - 1 else [])
                mm_done.append((bank, n, P.now("t")))
            freed[f"wqkvb{bi}"] = [("t", P.now("t"))]

            if is_q:
                for bank, n, t_mm in mm_done:
                    P.emit("scalar",
                           lambda e, bank=bank, n=n, ec=ec: e.activation(
                               QTx1b[:, ec * c.TOKC + n * 512: ec * c.TOKC + n * 512 + width],
                               pb[bank][:, 0:width], AF.Identity,
                               bias=sbq_s[:, ec:ec + 1], scale=RS128),
                           wait=[("t", t_mm), ("v", v_setup)], inc=[("s", 1)])
                    freed[f"pb{bank}"] = [("s", P.now("s"))]
            else:
                kvrow = (ec - c.ND) * 128 if is_k else c.D + (ec - 2 * c.ND) * 128
                for bank, n, t_mm in mm_done:
                    P.emit("scalar",
                           lambda e, bank=bank, n=n, bi=bi, ec=ec: e.activation(
                               kvstage[bi][:, n * 512: n * 512 + width],
                               pb[bank][:, 0:width], AF.Identity,
                               bias=bqkv_s[:, ec:ec + 1]),
                           wait=[("t", t_mm), ("swc", sw_consts)] + freed[f"kvstage{bi}"],
                           inc=[("s", 1)])
                    freed[f"pb{bank}"] = [("s", P.now("s"))]
                s_evac = P.now("s")
                P.dma("sync",
                      lambda e, bi=bi, kvrow=kvrow: e.dma_start(
                          out=kv_own[kvrow:kvrow + 128, :], in_=kvstage[bi][:, :]),
                      f"swkv{bi}", wait=[("s", s_evac)])
                freed[f"kvstage{bi}"] = [(f"swkv{bi}", P.now(f"swkv{bi}"))]
        s_qkv_done = P.now("s")

        if debug:
            for r_ in range(4):
                P.emit("gpsimd",
                       lambda e, r_=r_: e.dma_start(
                           out=dbg["dbg_kvg"][r_ * 2 * c.D: r_ * 2 * c.D + c.D, :],
                           in_=kgat[r_ * c.D:(r_ + 1) * c.D, :]),
                       wait=[("cc", 2), ("gwd", P.now("gwd"))], inc=[("gwd", 16)])
                P.emit("gpsimd",
                       lambda e, r_=r_: e.dma_start(
                           out=dbg["dbg_kvg"][r_ * 2 * c.D + c.D:(r_ + 1) * 2 * c.D, :],
                           in_=vgat[r_ * c.D:(r_ + 1) * c.D, :]),
                       wait=[("cc", 2), ("gwd", P.now("gwd"))], inc=[("gwd", 16)])
            P.emit("gpsimd",
                   lambda e: e.dma_start(out=dbg["dbg_qt"][:, :], in_=QTx1b[:, :]),
                   wait=[("s", s_qkv_done), ("gwd", P.now("gwd"))], inc=[("gwd", 16)])
            dwar["x1b"] = [("gwd", P.now("gwd"))]

        attn_out = bigA
        kg4 = kgat.ap().rearrange("(r f) (sb tt) -> f sb r tt", r=4, tt=128)
        vg4 = vgat.ap().rearrange("(r f) (sb tt) -> f sb r tt", r=4, tt=128)
        nkc_all = c.S // 128
        SCB = (0, 1, 2, 3, 5)
        vfm = [0]
        vstate = {"v_vtok0": 0, "v_vtok1": 0}

        for h in range(c.NH):
            hb = h % 2
            P.dma("sync",
                  lambda e, h=h, hb=hb: e.dma_start(
                      out=kh[hb][:, 0:c.S].rearrange("p (sb r tt) -> p sb r tt",
                                                     r=4, tt=128),
                      in_=kg4[h * 128:(h + 1) * 128]),
                  f"swk{hb}", wait=[("cc", 1)] + freed[f"kh{hb}"])
            sw_kh = (f"swk{hb}", P.now(f"swk{hb}"))
            if h == 0:
                P.dma("sync",
                      lambda e: e.dma_start(
                          out=vtT[:, 0:c.S].rearrange("p (sb r tt) -> p sb r tt",
                                                      r=4, tt=128),
                          in_=vg4[0:128]),
                      "swv", wait=[("cc", 2)] + freed["vtT"])
            sw_vt = P.now("swv")

            if h == 0:
                for kc8 in range((nkc_all + 7) // 8):
                    n_in = min(8, nkc_all - kc8 * 8)
                    for i in range(n_in):
                        kc = kc8 * 8 + i
                        P.emit("tensor",
                               lambda e, kc=kc, i=i: e.transpose(
                                   tb[0][:, i * 128:(i + 1) * 128],
                                   vtT[:, kc * 128:(kc + 1) * 128], identb[:, :]),
                               wait=([("swv", sw_vt), ("swc", sw_consts)]
                                     + freed["tb0"] + freed["vtok0"])
                               if i == 0 else [],
                               inc=[("t", 1)] if i == n_in - 1 else [])
                    t_tr = P.now("t")
                    P.emit("vector",
                           lambda e, kc8=kc8, n_in=n_in: e.tensor_copy(
                               vtok[0][:, kc8 * 8 * 132:]
                               .rearrange("p (kc w) -> p kc w", w=132)[:, 0:n_in, 0:128],
                               tb[0][:, 0:n_in * 128]
                               .rearrange("p (kc w) -> p kc w", w=128)),
                           wait=[("t", t_tr)], inc=[("v", 1)])
                    freed["tb0"] = [("v", P.now("v"))]
                freed["vtT"] = [("t", P.now("t"))]
                vstate["v_vtok0"] = P.now("v")
            if h + 1 < c.NH:
                P.dma("sync",
                      lambda e, h=h: e.dma_start(
                          out=vtT[:, 0:c.S].rearrange("p (sb r tt) -> p sb r tt",
                                                      r=4, tt=128),
                          in_=vg4[(h + 1) * 128: (h + 2) * 128]),
                      "swv", wait=freed["vtT"])
            v_vtok = vstate[f"v_vtok{h % 2}"]

            def emit_vspread(hn, qt, sw_vt=None):
                hnb = hn % 2
                for i in range(4):
                    kc = qt * 4 + i
                    P.emit("tensor",
                           lambda e, kc=kc, qt=qt, i=i: e.transpose(
                               tb[0][:, (qt % 2) * 512 + i * 128:
                                      (qt % 2) * 512 + (i + 1) * 128],
                               vtT[:, kc * 128:(kc + 1) * 128], identb[:, :]),
                           wait=([("swv", P.now("swv"))] + freed["tb0"]
                                 + (freed[f"vtok{hnb}"] if kc == 0 else []))
                           if i == 0 else [],
                           inc=[("t", 1)] if i == 3 else [])
                if qt % 2 == 1:
                    t_tr = P.now("t")
                    kc8 = (qt - 1) * 4
                    P.emit("vector",
                           lambda e, kc8=kc8, hnb=hnb: e.tensor_copy(
                               vtok[hnb][:, kc8 * 132:]
                               .rearrange("p (kc w) -> p kc w", w=132)[:, 0:8, 0:128],
                               tb[0][:, 0:1024]
                               .rearrange("p (kc w) -> p kc w", w=128)),
                           wait=[("t", t_tr)], inc=[("v", 1)])
                    freed["tb0"] = [("v", P.now("v"))]
                if qt == c.NQT - 1:
                    freed["vtT"] = [("t", P.now("t"))]
                    vstate[f"v_vtok{hnb}"] = P.now("v")

            pend = {}
            pend_g = {}
            pend_k = {}
            pend_c = {}

            def emit_scores(qt, h=h, hb=hb, sw_kh=sw_kh):
                qtp = qt % 2
                ngr = qt + 1
                spills = max(0, ngr - len(SCB))
                gbank = {}
                for g in range(ngr):
                    bank = SCB[g % len(SCB)]
                    gbank[g] = bank
                    for j4 in range(4):
                        kc = g * 4 + j4
                        diag = (g == qt)
                        P.emit("tensor",
                               lambda e, kc=kc, j4=j4, bank=bank, diag=diag: e.matmul(
                                   pb[bank][:, j4 * 128:(j4 + 1) * 128],
                                   kh[hb][:, kc * 128:(kc + 1) * 128],
                                   QTx1b[:, h * c.TOKC + qt * 128:
                                         h * c.TOKC + (qt + 1) * 128],
                                   start=True, stop=not diag),
                               wait=([sw_kh, ("s", s_qkv_done)] + freed[f"pb{bank}"])
                               if j4 == 0 else [],
                               inc=[("t", 1)] if (not diag and j4 == 3) else [])
                        if diag:
                            P.emit("tensor",
                                   lambda e, j4=j4, bank=bank: e.matmul(
                                       pb[bank][:, j4 * 128:(j4 + 1) * 128],
                                       maskb[:, j4 * 128:(j4 + 1) * 128],
                                       identb[:, :], start=False, stop=True),
                                   inc=[("t", 1)] if j4 == 3 else [])
                    t_g = P.now("t")
                    P.emit("vector",
                           lambda e, g=g, bank=bank: e.tensor_reduce(
                               mcols[:, g:g + 1], pb[bank][:, 0:512], AX.X, ALU.max),
                           wait=[("t", t_g), ("v", vfm[0])], inc=[("v", 1)])
                    v_bm = P.now("v")
                    if g < spills:
                        P.emit("vector",
                               lambda e, g=g, bank=bank: e.tensor_copy(
                                   Sbf[:, g * 512:(g + 1) * 512], pb[bank][:, 0:512]),
                               wait=(freed["Ssb"] if g == 0 else []), inc=[("v", 1)])
                        freed[f"pb{bank}"] = [("v", P.now("v"))]
                pend_g[qt] = (gbank, spills)
                pend_k[qt] = P.now("v")

            def emit_kchain_a(qt):
                qtp = qt % 2
                ngr = qt + 1
                v_bm = pend_k.pop(qt)
                P.emit("vector",
                       lambda e, qtp=qtp, ngr=ngr: e.tensor_reduce(
                           kmax[:, qtp:qtp + 1], mcols[:, 0:ngr], AX.X, ALU.max),
                       wait=[("v", v_bm)] + freed[f"kmax{qtp}"], inc=[("v", 1)])
                v_km = P.now("v")
                vfm[0] = v_km
                P.emit("tensor",
                       lambda e, qtp=qtp: e.transpose(
                           pb[6][0:1, 0:128], kmax[:, qtp:qtp + 1], identf[:, :]),
                       wait=[("v", v_km)] + freed["pb6"], inc=[("t", 1)])
                t_kt = P.now("t")
                P.emit("vector",
                       lambda e, qtp=qtp: e.tensor_reduce(
                           tmn[:, qtp:qtp + 1], pb[6][0:1, 0:128], AX.X, ALU.max,
                           negate=True),
                       wait=[("t", t_kt)], inc=[("v", 1)])
                pend_c[qt] = P.now("v")

            def emit_kchain_b(qt):
                qtp = qt % 2
                ngr = qt + 1
                v_tm = pend_c.pop(qt)
                P.emit("tensor",
                       lambda e, qtp=qtp: e.matmul(
                           pb[6][:, 128:129], onesf[:, :], tmn[:, qtp:qtp + 1],
                           start=True, stop=True),
                       wait=[("v", v_tm)], inc=[("t", 1)])
                t_bc = P.now("t")
                P.emit("scalar",
                       lambda e, qtp=qtp: e.copy(negmax[:, qtp:qtp + 1],
                                                 pb[6][:, 128:129]),
                       wait=[("t", t_bc)] + freed[f"negmax{qtp}"], inc=[("s", 1)])
                s_nm = P.now("s")
                freed["pb6"] = [("s", s_nm)]
                freed[f"kmax{qtp}"] = [("t", t_bc)]
                gbank, spills = pend_g.pop(qt)
                exp_marks = {}
                for g in range(ngr):
                    src_sp = g < spills
                    P.emit("scalar",
                           lambda e, g=g, qtp=qtp, src_sp=src_sp,
                           bank=gbank[g]: e.activation(
                               PTb[qtp][:, g * 512:(g + 1) * 512],
                               (Sbf[:, g * 512:(g + 1) * 512] if src_sp
                                else pb[bank][:, 0:512]), AF.Exp,
                               bias=negmax[:, qtp:qtp + 1]),
                           wait=[("s", s_nm)] + (freed[f"PTb{qtp}"] if g == 0 else []),
                           inc=[("s", 1)])
                    if not src_sp:
                        freed[f"pb{gbank[g]}"] = [("s", P.now("s"))]
                    exp_marks[g] = P.now("s")
                if spills:
                    freed["Ssb"] = [("s", exp_marks[spills - 1])]
                freed[f"negmax{qtp}"] = [("s", exp_marks[ngr - 1])]
                pend[qt] = (exp_marks, P.now("s"))

            def emit_ptpv(qt, h=h, hb=hb, v_vtok=v_vtok):
                qtp = qt % 2
                exp_marks, s_exp_all = pend.pop(qt)
                ngr = qt + 1
                nkc = 4 * ngr
                for kc in range(nkc):
                    w0 = [("s", exp_marks[kc // 4]), ("v", v_vtok)]
                    if kc == 0:
                        w0 = w0 + freed["pb4"]
                    P.emit("tensor",
                           lambda e, kc=kc, nkc=nkc, hb=hb, qtp=qtp: e.matmul(
                               pb[4][:, 0:129],
                               PTb[qtp][:, kc * 128:(kc + 1) * 128],
                               vtok[hb][:, kc * 132: kc * 132 + 129],
                               start=(kc == 0), stop=(kc == nkc - 1)),
                           wait=w0 if kc % 4 == 0 else [],
                           inc=[("t", 1)] if kc == nkc - 1 else [])
                t_pv = P.now("t")
                freed[f"PTb{qtp}"] = [("t", t_pv)]
                P.emit("vector",
                       lambda e, qtp=qtp: e.reciprocal(rr[:, qtp:qtp + 1],
                                                       pb[4][:, 128:129]),
                       wait=[("t", t_pv)], inc=[("v", 1)])
                v_rr = P.now("v")
                P.emit("scalar",
                       lambda e, h=h, qt=qt, qtp=qtp: e.mul(
                           attn_out[:, qt * c.D + h * 128: qt * c.D + (h + 1) * 128],
                           pb[4][:, 0:128], rr[:, qtp:qtp + 1]),
                       wait=[("v", v_rr)], inc=[("s", 1)])
                freed["pb4"] = [("s", P.now("s"))]

            for qt in range(c.NQT + 1):
                if qt < c.NQT:
                    emit_scores(qt)
                if qt >= 1:
                    emit_ptpv(qt - 1)
                if qt < c.NQT:
                    emit_kchain_a(qt)
                    emit_kchain_b(qt)
                if qt < c.NQT and h + 1 < c.NH:
                    emit_vspread(h + 1, qt)
            freed[f"kh{hb}"] = [("t", P.now("t"))]
            freed[f"vtok{hb}"] = [("t", P.now("t"))]
        v_attn_done = max(P.now("v"), 1)
        s_attn_done = P.now("s")

        if debug:
            P.emit("gpsimd",
                   lambda e: e.dma_start(out=dbg["dbg_attn"][:, :],
                                         in_=attn_out[:, 0:c.NQT * c.D]),
                   wait=[("s", s_attn_done), ("gwd", P.now("gwd"))], inc=[("gwd", 16)])
            dwar["fuse"] = [("gwd", P.now("gwd"))]

        for h in range(c.NH):
            for qt in range(c.NQT):
                P.emit("tensor",
                       lambda e, h=h, qt=qt: e.transpose(
                           tb[0][:, qt * 128:(qt + 1) * 128],
                           attn_out[:, qt * c.D + h * 128: qt * c.D + (h + 1) * 128],
                           identb[:, :]),
                       wait=([("s", s_attn_done)] + freed["tb0"]) if qt == 0 else [],
                       inc=[("t", 1)] if qt == c.NQT - 1 else [])
            t_tr = P.now("t")
            P.emit("vector",
                   lambda e, h=h: e.tensor_copy(
                       attnTf[:, h * c.TOKC:(h + 1) * c.TOKC],
                       tb[0][:, 0:c.NQT * 128]),
                   wait=[("t", t_tr)], inc=[("v", 1)])
            freed["tb0"] = [("v", P.now("v"))]
        v_attnT = P.now("v")
        attn_end = [("t", P.now("t")), ("v", P.now("v")), ("s", P.now("s"))]
        for nm in ("woutb0", "woutb1", "xres0", "xres1",
                   "wgub0", "wgub1", "wdb0", "wdb1",
                   "gact0", "gact1", "ystage0", "ystage1"):
            freed[nm] = list(attn_end)
        lnscratch["w"] = list(attn_end)
        freed["pv"] = list(attn_end)
        freed["pb5"] = list(attn_end)

        if debug:
            P.emit("gpsimd",
                   lambda e: e.dma_start(out=dbg["dbg_attnT"][:, :], in_=attnTf[:, :]),
                   wait=[("v", v_attnT), ("gwd", P.now("gwd"))], inc=[("gwd", 16)])
            dwar["fuse"] = dwar["fuse"] + [("gwd", P.now("gwd"))]

        freed["pv"] = freed["pv"]

        def emit_ln(preln_marks, dst_write):
            war = lnscratch["w"] + [("t", P.now("t")),
                                    ("gwa", P.now("gwa")), ("gwb", P.now("gwb"))]
            for ec in range(c.ND):
                zi = ec % 2
                P.emit("gpsimd",
                       lambda e, ec=ec, zi=zi: e.tensor_copy(
                           zb[zi][:, 0:c.TT], preln[:, ec * c.TT:(ec + 1) * c.TT]),
                       wait=[("v", preln_marks[ec])] + freed[f"zb{zi}"] + war,
                       inc=[("g", 1)])
                g_z = P.now("g")
                P.emit("scalar",
                       lambda e, ec=ec, zi=zi: e.activation(
                           sqb[zi][:, 0:c.TT], preln[:, ec * c.TT:(ec + 1) * c.TT],
                           AF.Square),
                       wait=[("v", preln_marks[ec])] + freed[f"sqb{zi}"] + war,
                       inc=[("s", 1)])
                s_q = P.now("s")
                P.emit("tensor",
                       lambda e, ec=ec, zi=zi: e.matmul(
                           pb[4][0:1, 0:c.TT], onesb[:, 0:1], zb[zi][:, 0:c.TT],
                           start=(ec == 0), stop=(ec == c.ND - 1)),
                       wait=[("g", g_z), ("v", v_setup)] + (freed["pv"] if ec == 0 else []),
                       inc=[("t", 1)])
                freed[f"zb{zi}"] = [("t", P.now("t"))]
                P.emit("tensor",
                       lambda e, ec=ec, zi=zi: e.matmul(
                           pb[5][0:1, 0:c.TT], onesb[:, 0:1], sqb[zi][:, 0:c.TT],
                           start=(ec == 0), stop=(ec == c.ND - 1)),
                       wait=[("s", s_q)], inc=[("t", 1)])
                freed[f"sqb{zi}"] = [("t", P.now("t"))]
            t_stats = P.now("t")
            P.emit("vector",
                   lambda e: e.tensor_scalar_mul(stat_a[:, 0:c.TT], pb[4][0:1, 0:c.TT],
                                                 1.0 / c.D),
                   wait=[("t", t_stats)], inc=[("v", 1)])
            P.emit("vector",
                   lambda e: e.tensor_mul(stat_c[:, 0:c.TT], stat_a[:, 0:c.TT],
                                          stat_a[:, 0:c.TT]),
                   wait=[("v", P.now("v"))], inc=[("v", 1)])
            P.emit("vector",
                   lambda e: e.scalar_tensor_tensor(
                       stat_b[:, 0:c.TT], pb[5][0:1, 0:c.TT], 1.0 / c.D,
                       stat_c[:, 0:c.TT], ALU.mult, ALU.subtract),
                   wait=[("v", P.now("v"))], inc=[("v", 1)])
            P.emit("vector",
                   lambda e: e.tensor_scalar_add(stat_b[:, 0:c.TT], stat_b[:, 0:c.TT],
                                                 1e-5),
                   wait=[("v", P.now("v"))], inc=[("v", 1)])
            v_var = P.now("v")
            P.emit("scalar",
                   lambda e: e.activation(stat_c[:, 0:c.TT], stat_b[:, 0:c.TT],
                                          AF.Sqrt),
                   wait=[("v", v_var)], inc=[("s", 1)])
            s_sd = P.now("s")
            P.emit("vector",
                   lambda e: e.reciprocal(stat_b[:, 0:c.TT], stat_c[:, 0:c.TT]),
                   wait=[("s", s_sd)], inc=[("v", 1)])
            v_rstd = P.now("v")
            P.emit("tensor",
                   lambda e: e.matmul(pb[4][:, 0:c.TT], onesf[:, :], stat_a[:, 0:c.TT],
                                      start=True, stop=True),
                   wait=[("v", v_rstd)], inc=[("t", 1)])
            P.emit("tensor",
                   lambda e: e.matmul(pb[5][:, 0:c.TT], onesf[:, :], stat_b[:, 0:c.TT],
                                      start=True, stop=True),
                   inc=[("t", 1)])
            t_bc = P.now("t")
            P.emit("vector", lambda e: e.tensor_copy(mb_s[:, 0:c.TT], pb[4][:, 0:c.TT]),
                   wait=[("t", t_bc)] + freed["mbrb"] + war, inc=[("v", 1)])
            P.emit("vector", lambda e: e.tensor_copy(rb_s[:, 0:c.TT], pb[5][:, 0:c.TT]),
                   inc=[("v", 1)])
            v_mbrb = P.now("v")
            freed["pv"] = [("v", v_mbrb)]
            for ec in range(c.ND):
                P.emit("vector",
                       lambda e, ec=ec: e.tensor_sub(
                           tmp_n[:, 0:c.TT], preln[:, ec * c.TT:(ec + 1) * c.TT],
                           mb_s[:, 0:c.TT]),
                       wait=[("v", max(v_mbrb, P.now("v")))], inc=[("v", 1)])
                dst_write(ec)
            freed["mbrb"] = [("v", P.now("v"))]
            lnscratch["w"] = [("v", P.now("v")), ("t", P.now("t")),
                              ("s", P.now("s")), ("g", P.now("g"))]

        x1b = QTx1b
        for T in range(c.NTILES):
            preln_marks = {}
            for ec in range(c.ND):
                bi = ec % 2
                sn = f"gw{'ab'[bi]}"
                P.dma("gpsimd",
                      lambda e, ec=ec, bi=bi: e.dma_start(
                          out=woutb[bi][:, :],
                          in_=woutT[ec].rearrange("p d e -> p (d e)")),
                      sn, wait=freed[f"woutb{bi}"])
                gw_w = (sn, P.now(sn))
                P.dma("sync",
                      lambda e, ec=ec, bi=bi, T=T: e.dma_start(
                          out=xres[bi][:, 0:c.TT],
                          in_=xT[ec * 128:(ec + 1) * 128, T * c.TT:(T + 1) * c.TT]),
                      f"swx{bi}", wait=freed[f"xres{bi}"])
                sw_x = (f"swx{bi}", P.now(f"swx{bi}"))
                bank = ec % 4
                for dc in range(c.ND):
                    P.emit("tensor",
                           lambda e, dc=dc, bi=bi, bank=bank, T=T: e.matmul(
                               pb[bank][:, 0:c.TT],
                               woutb[bi][:, dc * 128:(dc + 1) * 128],
                               attnTf[:, dc * c.TOKC + T * c.TT:
                                      dc * c.TOKC + (T + 1) * c.TT],
                               start=(dc == 0), stop=(dc == c.ND - 1)),
                           wait=([gw_w, ("v", v_attnT)] + freed[f"pb{bank}"])
                           if dc == 0 else [],
                           inc=[("t", 1)] if dc == c.ND - 1 else [])
                t_mm = P.now("t")
                freed[f"woutb{bi}"] = [("t", t_mm)]
                P.emit("vector",
                       lambda e, ec=ec, bank=bank, bi=bi: e.scalar_tensor_tensor(
                           preln[:, ec * c.TT:(ec + 1) * c.TT], pb[bank][:, 0:c.TT],
                           bout_s[:, ec:ec + 1], xres[bi][:, 0:c.TT],
                           ALU.add, ALU.add),
                       wait=[("t", t_mm), sw_x, ("swc", sw_consts)] + dwar["preln"],
                       inc=[("v", 1)])
                freed[f"pb{bank}"] = [("v", P.now("v"))]
                freed[f"xres{bi}"] = [("v", P.now("v"))]
                preln_marks[ec] = P.now("v")

            if debug and T == 0:
                P.emit("gpsimd",
                       lambda e: e.dma_start(out=dbg["dbg_preln"][:, :],
                                             in_=preln[:, 0:c.ND * c.TT]),
                       wait=[("v", max(preln_marks.values())), ("gwd", P.now("gwd"))], inc=[("gwd", 16)])
                dwar["preln"] = [("gwd", P.now("gwd"))]

            def dst_x1(ec, T=T):
                P.emit("vector",
                       lambda e, ec=ec, T=T: e.tensor_mul(
                           x1b[:, ec * c.TOKC + T * c.TT: ec * c.TOKC + (T + 1) * c.TT],
                           tmp_n[:, 0:c.TT], rb_s[:, 0:c.TT]),
                       wait=[("v", P.now("v"))] + dwar["x1b"], inc=[("v", 1)])
            emit_ln(preln_marks, dst_x1)
        v_x1_done = P.now("v")

        if debug:
            P.emit("gpsimd",
                   lambda e: e.dma_start(out=dbg["dbg_x1b"][:, :], in_=x1b[:, :]),
                   wait=[("v", v_x1_done), ("gwd", P.now("gwd"))], inc=[("gwd", 16)])

        nslotA = (c.NH * c.TOKC) // c.TT

        def fuse_slice(oc):
            if oc < nslotA:
                return bigA[:, oc * c.TT:(oc + 1) * c.TT]
            oc2 = oc - nslotA
            return attnTf[:, oc2 * c.TT:(oc2 + 1) * c.TT]

        fuse_war = {"w": []}

        def emit_gateup(T):
            fuse_last = 0
            for oc in range(c.NO):
                bi = oc % 2
                sn = f"gw{'ab'[bi]}"
                P.dma("gpsimd",
                      lambda e, oc=oc, bi=bi: e.dma_start(
                          out=wgub[bi][:, :], in_=wgu[oc]),
                      sn, wait=freed[f"wgub{bi}"] + lnscratch["w"])
                gw_w = (sn, P.now(sn))
                bk_g = (oc % 2) * 2
                bk_u = bk_g + 1
                for which, bank in ((0, bk_g), (1, bk_u)):
                    for dc in range(c.ND):
                        P.emit("tensor",
                               lambda e, bi=bi, which=which, dc=dc, bank=bank, T=T: e.matmul(
                                   pb[bank][:, 0:c.TT],
                                   wgub[bi][:, which * c.ND * 128 + dc * 128:
                                             which * c.ND * 128 + (dc + 1) * 128],
                                   x1b[:, dc * c.TOKC + T * c.TT:
                                       dc * c.TOKC + (T + 1) * c.TT],
                                   start=(dc == 0), stop=(dc == c.ND - 1)),
                               wait=([gw_w, ("v", v_x1_done)] + freed[f"pb{bank}"])
                               if dc == 0 else [],
                               inc=[("t", 1)] if dc == c.ND - 1 else [])
                t_gu = P.now("t")
                freed[f"wgub{bi}"] = [("t", t_gu)]
                gi = oc % 2
                P.emit("scalar",
                       lambda e, oc=oc, bank=bk_g, gi=gi: e.activation(
                           gact[gi][:, 0:c.TT], pb[bank][:, 0:c.TT], AF.Silu,
                           scale=sg_s[:, oc:oc + 1]),
                       wait=[("t", t_gu), ("swc", sw_consts)] + freed[f"gact{gi}"],
                       inc=[("s", 1)])
                s_gact = P.now("s")
                freed[f"pb{bk_g}"] = [("s", s_gact)]
                P.emit("vector",
                       lambda e, oc=oc, bank=bk_u, gi=gi: e.scalar_tensor_tensor(
                           fuse_slice(oc), pb[bank][:, 0:c.TT], su_s[:, oc:oc + 1],
                           gact[gi][:, 0:c.TT], ALU.mult, ALU.mult),
                       wait=[("s", s_gact)] + dwar["fuse"] + fuse_war["w"],
                       inc=[("v", 1)])
                v_f = P.now("v")
                freed[f"pb{bk_u}"] = [("v", v_f)]
                freed[f"gact{gi}"] = [("v", v_f)]
                fuse_last = v_f
            return fuse_last

        def emit_down(T, v_fuse_all):
            preln_marks = {}
            for ec in range(c.ND):
                bi = ec % 2
                sn = f"gw{'ab'[bi]}"
                P.dma("gpsimd",
                      lambda e, ec=ec, bi=bi: e.dma_start(
                          out=wdb[bi][:, :], in_=wdt[ec]),
                      sn, wait=freed[f"wdb{bi}"] + lnscratch["w"])
                gw_w = (sn, P.now(sn))
                bank = ec % 4
                for oc in range(c.NO):
                    P.emit("tensor",
                           lambda e, oc=oc, bi=bi, bank=bank: e.matmul(
                               pb[bank][:, 0:c.TT],
                               wdb[bi][:, oc * 128:(oc + 1) * 128],
                               fuse_slice(oc),
                               start=(oc == 0), stop=(oc == c.NO - 1)),
                           wait=([gw_w, ("v", v_fuse_all)] + freed[f"pb{bank}"])
                           if oc == 0 else [],
                           inc=[("t", 1)] if oc == c.NO - 1 else [])
                t_mm = P.now("t")
                freed[f"wdb{bi}"] = [("t", t_mm)]
                P.emit("vector",
                       lambda e, ec=ec, bank=bank, T=T: e.scalar_tensor_tensor(
                           preln[:, ec * c.TT:(ec + 1) * c.TT], pb[bank][:, 0:c.TT],
                           sd_s[:, ec:ec + 1],
                           x1b[:, ec * c.TOKC + T * c.TT: ec * c.TOKC + (T + 1) * c.TT],
                           ALU.mult, ALU.add),
                       wait=[("t", t_mm)] + dwar["preln"] + lnscratch["w"],
                       inc=[("v", 1)])
                freed[f"pb{bank}"] = [("v", P.now("v"))]
                preln_marks[ec] = P.now("v")
            fuse_war["w"] = [("t", P.now("t"))]
            return preln_marks

        def emit_ln2(T, preln_marks):
            def dst_y(ec, T=T):
                yi = ec % 2
                P.emit("vector",
                       lambda e, yi=yi: e.tensor_mul(
                           ystage[yi][:, 0:c.TT], tmp_n[:, 0:c.TT], rb_s[:, 0:c.TT]),
                       wait=[("v", P.now("v"))] + freed[f"ystage{yi}"], inc=[("v", 1)])
                v_y = P.now("v")
                P.dma("sync",
                      lambda e, ec=ec, yi=yi, T=T: e.dma_start(
                          out=yT[ec * 128:(ec + 1) * 128, T * c.TT:(T + 1) * c.TT],
                          in_=ystage[yi][:, 0:c.TT]),
                      f"swy{yi}", wait=[("v", v_y)])
                freed[f"ystage{yi}"] = [(f"swy{yi}", P.now(f"swy{yi}"))]
            emit_ln(preln_marks, dst_y)

        if c.NTILES == 2:
            vf0 = emit_gateup(0)
            pm0 = emit_down(0, vf0)
            vf1 = emit_gateup(1)
            emit_ln2(0, pm0)
            pm1 = emit_down(1, vf1)
            emit_ln2(1, pm1)
        else:
            vf0 = emit_gateup(0)
            pm0 = emit_down(0, vf0)
            emit_ln2(0, pm0)

        P.emit("sync", None, wait=[("swy0", P.now("swy0")), ("swy1", P.now("swy1"))])
        P.emit("gpsimd", None, wait=[("gwd", P.now("gwd")), ("gwa", P.now("gwa")),
                                     ("gwb", P.now("gwb"))])
        P.finalize(blk, sems)
    return nc



def prep_shared(cfg, wqkv, b_qkv, w_out, b_out, w_g, w_u, w_d, s_g, s_u, s_d):
    c = cfg
    f32 = np.float32
    sh = {}
    sh["wqkvT"] = np.ascontiguousarray(
        wqkv.reshape(c.NE3, 128, c.ND, 128).transpose(0, 3, 2, 1)).astype(f32)
    sh["woutT"] = np.ascontiguousarray(
        w_out.reshape(c.ND, 128, c.ND, 128).transpose(0, 3, 2, 1)).astype(f32)
    wgt = w_g.reshape(c.ND, 128, c.NO, 128).transpose(2, 1, 0, 3)
    wut = w_u.reshape(c.ND, 128, c.NO, 128).transpose(2, 1, 0, 3)
    sh["wgu"] = np.ascontiguousarray(
        np.concatenate([wgt.reshape(c.NO, 128, 1, c.ND * 128),
                        wut.reshape(c.NO, 128, 1, c.ND * 128)], axis=2)
        .reshape(c.NO, 128, 2 * c.ND * 128)).astype(np.int8)
    sh["wdt"] = np.ascontiguousarray(
        w_d.reshape(c.NO, 128, c.ND, 128).transpose(2, 1, 0, 3)
        .reshape(c.ND, 128, c.NO * 128)).astype(np.int8)
    sh["bqkv"] = np.ascontiguousarray(b_qkv.reshape(c.NE3, 128).T).astype(f32)
    sh["bout"] = np.ascontiguousarray(b_out.reshape(c.ND, 128).T).astype(f32)
    sh["sg"] = np.ascontiguousarray(s_g.reshape(c.NO, 128).T).astype(f32)
    sh["su"] = np.ascontiguousarray(s_u.reshape(c.NO, 128).T).astype(f32)
    sh["sd"] = np.ascontiguousarray(s_d.reshape(c.ND, 128).T).astype(f32)
    sh["identc"] = np.eye(128).astype(ml_dtypes.bfloat16)
    sh["identfc"] = np.eye(128, dtype=np.float32)
    return sh


def core_token_index(cfg, j):
    c = cfg
    idx = []
    for qt in range(c.NQT):
        base = 512 * qt + 128 * j
        idx.extend(range(base, base + 128))
    return np.asarray(idx)


def core_mask(j):
    q = np.arange(128)[:, None]
    k = np.arange(512)[None, :]
    m = np.where(k <= 128 * j + q, 0.0, NEG).astype(np.float32)
    return np.ascontiguousarray(m.astype(ml_dtypes.bfloat16))


def make_in_maps(cfg, x, sh):
    in_maps, idxs = [], []
    for core in range(NCORES):
        b, j = core // 4, core % 4
        idx = core_token_index(cfg, j)
        idxs.append(idx)
        m = dict(sh)
        m["xT"] = np.ascontiguousarray(x[b].T[:, idx])
        m["maskc"] = core_mask(j)
        in_maps.append(m)
    return in_maps, idxs


def kernel(L0_attn_x, L0_attn_w_qkv, L0_attn_b_qkv, L0_attn_w_out, L0_attn_b_out,
           L0_ffn_w_gate, L0_ffn_w_up, L0_ffn_w_down, L0_ffn_s_gate, L0_ffn_s_up,
           L0_ffn_s_down):
    cfg = Cfg()
    x = np.asarray(L0_attn_x, np.float32)
    sh = prep_shared(cfg, np.asarray(L0_attn_w_qkv, np.float32),
                     np.asarray(L0_attn_b_qkv, np.float32),
                     np.asarray(L0_attn_w_out, np.float32),
                     np.asarray(L0_attn_b_out, np.float32),
                     np.asarray(L0_ffn_w_gate), np.asarray(L0_ffn_w_up),
                     np.asarray(L0_ffn_w_down),
                     np.asarray(L0_ffn_s_gate, np.float32),
                     np.asarray(L0_ffn_s_up, np.float32),
                     np.asarray(L0_ffn_s_down, np.float32))
    in_maps, idxs = make_in_maps(cfg, x, sh)
    nc = build_nc(cfg, debug=False)
    res = run_bass_kernel_spmd(nc, in_maps, core_ids=list(range(NCORES)))
    out = np.empty((cfg.B, cfg.S, cfg.D), np.float32)
    for core in range(NCORES):
        b = core // 4
        out[b, idxs[core], :] = np.asarray(res.results[core]["yT"]).T
    return out
